# version 22
# baseline (speedup 1.0000x reference)
import os
import contextlib
import numpy as np
import concourse.bass as bass
import concourse.mybir as mybir
from concourse.bass_utils import run_bass_kernel_spmd

F32 = mybir.dt.float32
F16 = mybir.dt.float16
BF16 = mybir.dt.bfloat16
I32 = mybir.dt.int32
AF = mybir.ActivationFunctionType
ALU = mybir.AluOpType
AX = mybir.AxisListType

OWN = 2064
HALO = 4
WIN = OWN + 2 * HALO
FW = 2176
NT = 17
NC_CH = 8
DM = 1024
GRPS = [(0, 512), (512, 512), (1024, 512), (1536, 512), (2048, 128)]
EPS = 1e-5

ENGS = ("sync", "tensor", "vector", "scalar", "gpsimd")


class Sem:
    __slots__ = ("name", "handle", "count", "last_tok")

    def __init__(self, name, handle):
        self.name = name
        self.handle = handle
        self.count = 0
        self.last_tok = None


class Tok:
    __slots__ = ("sem", "val")

    def __init__(self, sem, val):
        self.sem = sem
        self.val = val


class Arena:

    def __init__(self, handle, n):
        self.h = handle
        self.n = n
        self.blocks = {}
        self.free_list = [(0, n)]
        self.peak = 0

    def alloc(self, name, nf32, top=False):
        nf32 = (nf32 + 1) & ~1
        assert name not in self.blocks
        it = enumerate(self.free_list)
        if top:
            it = reversed(list(it))
        for i, (off, sz) in it:
            if sz >= nf32:
                if top:
                    self.blocks[name] = (off + sz - nf32, nf32)
                    if sz == nf32:
                        self.free_list.pop(i)
                    else:
                        self.free_list[i] = (off, sz - nf32)
                else:
                    self.blocks[name] = (off, nf32)
                    if sz == nf32:
                        self.free_list.pop(i)
                    else:
                        self.free_list[i] = (off + nf32, sz - nf32)
                used = sum(s for (_, s) in self.blocks.values())
                self.peak = max(self.peak, used)
                return self.blocks[name][0]
        raise MemoryError(f"SBUF arena OOM allocating {name} ({nf32} f32); "
                          f"live={ {k: v[1] for k, v in self.blocks.items()} }")

    def free(self, name):
        off, sz = self.blocks.pop(name)
        if not hasattr(self, "pending"):
            self.pending = []
        self.pending.append((off, sz))

    def flush_frees(self):
        for b in getattr(self, "pending", []):
            self.free_list.append(b)
        self.pending = []
        self.free_list.sort()
        out = []
        for b in self.free_list:
            if out and out[-1][0] + out[-1][1] == b[0]:
                out[-1] = (out[-1][0], out[-1][1] + b[1])
            else:
                out.append(list(b))
        self.free_list = [tuple(b) for b in out]

    def f32(self, name, shape):
        off, sz = self.blocks[name]
        n = int(np.prod(shape[1:]))
        assert n <= sz, (name, shape, sz)
        ap = self.h[:, off:off + n]
        if len(shape) == 2:
            return ap
        return ap.rearrange("p (a b) -> p a b", b=shape[-1]) if len(shape) == 3 else \
            ap.rearrange("p (a b c) -> p a b c", b=shape[-2], c=shape[-1])

    def bf16(self, name, shape):
        off, sz = self.blocks[name]
        n = int(np.prod(shape[1:]))
        assert n <= 2 * sz, (name, shape, sz)
        ap = self.h[:, off:off + (n + 1) // 2].bitcast(BF16)[:, 0:n]
        if len(shape) == 2:
            return ap
        return ap.rearrange("p (a b) -> p a b", b=shape[-1]) if len(shape) == 3 else \
            ap.rearrange("p (a b c) -> p a b c", b=shape[-2], c=shape[-1])

    def f16(self, name, shape):
        off, sz = self.blocks[name]
        n = int(np.prod(shape[1:]))
        assert n <= 2 * sz
        ap = self.h[:, off:off + (n + 1) // 2].bitcast(F16)[:, 0:n]
        if len(shape) == 2:
            return ap
        return ap.rearrange("p (a b) -> p a b", b=shape[-1])

    def i32(self, name, shape):
        off, sz = self.blocks[name]
        n = int(np.prod(shape[1:]))
        assert n <= sz
        ap = self.h[:, off:off + n].bitcast(I32)
        if len(shape) == 2:
            return ap
        return ap.rearrange("p (a b) -> p a b", b=shape[-1])


class Bank:
    __slots__ = ("ap", "free_tok", "idx")

    def __init__(self, ap, idx):
        self.ap = ap
        self.free_tok = None
        self.idx = idx


class R:

    def __init__(self, nc, sems):
        self.nc = nc
        self.q = {e: [] for e in ENGS}
        self.water = {e: {} for e in ENGS}
        self.sems = {}
        for name, handle in sems.items():
            self.sems[name] = Sem(name, handle)
        self.prog = {e: self.sems["p_" + e] for e in ENGS}
        allp = [v for k, v in sorted(self.sems.items()) if k.startswith("tok")]
        self.pool = {"tensor": allp[0:10], "vector": allp[10:20],
                     "scalar": allp[20:26], "gpsimd": allp[26:28]}
        self.dpool = allp[28:]
        self.pool_i = {e: 0 for e in self.pool}
        self.dpool_i = 0
        self.opcount = {e: 0 for e in ENGS}
        self.last_barrier = {e: 0 for e in ENGS}
        self.last_incs = {e: None for e in ENGS}
        self.last_tok = {e: (None, -1) for e in ENGS}
        self.dma_toks = []

    def _next_tok_sem(self, eng):
        p = self.pool[eng]
        s = p[self.pool_i[eng] % len(p)]
        self.pool_i[eng] += 1
        return s

    def _next_dma_sem(self):
        s = self.dpool[self.dpool_i % len(self.dpool)]
        self.dpool_i += 1
        return s

    def _wait(self, eng, tok):
        if tok is None:
            return
        w = self.water[eng]
        if w.get(tok.sem.name, -1) >= tok.val:
            return
        w[tok.sem.name] = tok.val
        h, v = tok.sem.handle, tok.val

        def fn(e):
            e.wait_ge(h, v)
        self.q[eng].append(fn)

    def op(self, eng, method, *args, wait=(), inc=None, inc_k=1, **kw):
        for t in wait:
            self._wait(eng, t)
        tok = None
        incs = []
        if inc is not None:
            sem = self._next_tok_sem(eng) if inc is True else inc
            sem.count += inc_k
            tok = Tok(sem, sem.count)
            incs.append((sem.handle, inc_k))

        import traceback as _tb
        site = "".join(_tb.format_stack(limit=4)[:-2])

        def fn(e):
            try:
                ins = getattr(e, method)(*args, **kw)
            except Exception:
                print("EMIT FAIL:", eng, method, "recorded at:\n", site)
                raise
            for (h, k) in incs:
                ins.then_inc(h, k)
        self.q[eng].append(fn)
        self.opcount[eng] += 1
        if method != "nop":
            self.last_incs[eng] = incs
            if tok is not None:
                self.last_tok[eng] = (tok, self.opcount[eng])
        return tok

    def tok(self, eng):
        lt, at = self.last_tok[eng]
        if lt is not None and at == self.opcount[eng]:
            return lt
        incs = self.last_incs[eng]
        assert incs is not None, f"no instruction to attach token on {eng}"
        sem = self._next_tok_sem(eng)
        sem.count += 1
        incs.append((sem.handle, 1))
        t = Tok(sem, sem.count)
        self.last_tok[eng] = (t, self.opcount[eng])
        return t

    def dma(self, out, in_, eng="sync", wait=(), **kw):
        sem = self._next_dma_sem()
        for t in wait:
            self._wait(eng, t)
        prev = sem.last_tok
        if prev is not None:
            self._wait(eng, prev)
        sem.count += 16
        tok = Tok(sem, sem.count)
        incs = [(sem.handle, 16)]

        def fn(e):
            ins = e.dma_start(out=out, in_=in_, **kw)
            for (h, k) in incs:
                ins.then_inc(h, k)
        self.q[eng].append(fn)
        self.opcount[eng] += 1
        self.dma_toks.append(tok)
        sem.last_tok = tok
        return tok

    def dmat(self, out, in_, eng="sync", wait=()):
        sem = self._next_dma_sem()
        for t in wait:
            self._wait(eng, t)
        prev = sem.last_tok
        if prev is not None:
            self._wait(eng, prev)
        sem.count += 16
        tok = Tok(sem, sem.count)
        incs = [(sem.handle, 16)]

        def fn(e):
            ins = e.dma_start_transpose(out, in_)
            for (h, k) in incs:
                ins.then_inc(h, k)
        self.q[eng].append(fn)
        self.opcount[eng] += 1
        self.dma_toks.append(tok)
        sem.last_tok = tok
        return tok

    def barrier(self):
        for hook in getattr(self, "hooks", []):
            hook()
        toks = []
        for e in ENGS:
            if self.opcount[e] > self.last_barrier[e] and self.last_incs[e]:
                toks.append((e, self.tok(e)))
        for e in ENGS:
            for (src, t) in toks:
                self._wait(e, t)
            for t in self.dma_toks:
                self._wait(e, t)
            self.last_barrier[e] = self.opcount[e]
        self.dma_toks = []

    def te(self, m, *a, **k):
        return self.op("tensor", m, *a, **k)

    def v(self, m, *a, **k):
        return self.op("vector", m, *a, **k)

    def s(self, m, *a, **k):
        return self.op("scalar", m, *a, **k)

    def g(self, m, *a, **k):
        return self.op("gpsimd", m, *a, **k)

    def emit_into(self, block):
        for ename in ENGS:
            fns = self.q[ename]
            if not fns:
                continue
            dec = getattr(block, "sync" if ename == "sync" else ename)

            def make(fns):
                def body(e):
                    for f in fns:
                        f(e)
                return body
            dec(make(fns))



def build_program(stop_stage=None, taps=()):
    nc = bass.Bass("TRN2", target_bir_lowering=False)

    IN = {}

    def inp(name, shape, dtype=F32):
        IN[name] = nc.dram_tensor(name, list(shape), dtype, kind="ExternalInput")
        return IN[name]

    inp("xshard", [FW, DM])
    inp("mask_full_bf", [128, FW], BF16)
    inp("mask_tm_bf", [128, NT], BF16)
    inp("mask_tm_f32", [128, NT])
    inp("ascale", [128, 1])
    inp("mska", [128, 4 * 96], BF16)
    inp("cpe1_w", [128, NC_CH, 3]); inp("cpe1_b", [128, NC_CH])
    inp("norm1_g", [128, NC_CH]); inp("norm1_b", [128, NC_CH])
    inp("in_w", [DM, DM]); inp("in_b", [128, NC_CH])
    inp("actp_w", [DM, DM]); inp("actp_b", [128, NC_CH])
    inp("dwc_w", [128, NC_CH, 3]); inp("dwc_b", [128, NC_CH])
    inp("qk_w", [DM, 2 * DM]); inp("qk_b", [128, 16])
    inp("lepe_w", [128, NC_CH, 3]); inp("lepe_b", [128, NC_CH])
    inp("out_w", [DM, DM]); inp("out_b", [128, NC_CH])
    inp("cpe2_w", [128, NC_CH, 3]); inp("cpe2_b", [128, NC_CH])
    inp("norm2_g", [128, NC_CH]); inp("norm2_b", [128, NC_CH])
    inp("fc1_w", [DM, 4 * DM]); inp("fc1_b", [128, 32])
    inp("fc2_w", [4 * DM, DM]); inp("fc2_b", [128, NC_CH])
    inp("tern", [DM, DM], I32); inp("ffn_bias", [128, NC_CH])
    for w in ("Wq", "Wk", "Wv", "Wo"):
        inp(w, [DM, DM])
    for b in ("bq", "bk", "bv", "bo"):
        inp(b, [128, NC_CH])
    inp("w_gate", [4 * DM, DM]); inp("w_up", [4 * DM, DM]); inp("w_down", [DM, 4 * DM])
    inp("wg_h", [2 * DM, DM]); inp("wu_h", [2 * DM, DM]); inp("wd_h", [DM // 2, 4 * DM])

    y_out = nc.dram_tensor("y", [FW, DM], F32, kind="ExternalOutput")

    SC = {}

    def scr(name, shape, dtype=F32):
        SC[name] = nc.dram_tensor("sc_" + name, list(shape), dtype)
        return SC[name]

    scr("x0fm", [DM, FW])
    scr("c1", [DM, FW])
    scr("xn", [DM, FW], BF16)
    scr("ares", [DM, FW], BF16)
    scr("xi", [DM, FW], BF16)
    scr("qel", [DM, FW], BF16)
    scr("krtm", [FW, DM], BF16)
    scr("vtm", [FW, DM], BF16)
    scr("kvred", [128, NC_CH * 132])
    scr("sel2", [NC_CH * 128, 128], BF16)
    scr("ksumsc", [128, NC_CH])
    scr("zrep", [DM, FW], BF16)
    scr("rintsc", [DM, FW], BF16)
    scr("woT", [DM, DM], BF16)
    scr("mw", [128, 8])
    scr("xqfm_sc", [DM, FW], BF16)
    scr("axinv", [128, NT])
    scr("ahinv", [128, NT])
    scr("kvbf", [NC_CH * 128, 132], BF16)
    scr("qrope", [DM, FW], BF16)
    scr("cc_in", [128, NC_CH * 132])
    scr("cc_out", [128, NC_CH * 132])
    scr("cc2_in", [128, 4])
    scr("cc2_out", [128, 4])
    scr("top", [DM, FW], BF16)
    scr("x1", [DM, FW])
    scr("x2", [DM, FW])
    scr("xn2", [DM, FW], BF16)
    scr("ps1", [DM, FW])
    scr("x1o", [DM, FW])
    scr("ps2", [DM, FW])
    scr("x2o", [DM, FW])
    scr("x2ob", [DM, FW], BF16)
    scr("qfm", [DM, FW], BF16)
    scr("kfm", [DM, FW], BF16)
    scr("vfm", [DM, FW], BF16)
    scr("vtml", [FW, DM], BF16)
    scr("vtma", [128 * 22, DM], BF16)
    scr("ofm", [DM, FW], BF16)
    scr("ps3", [DM, FW])
    scr("x3o", [DM, FW])
    scr("x3tm", [FW, DM])
    scr("xqfm", [DM, FW], BF16)
    scr("hq", [FW, 4 * DM], BF16)
    scr("wgT", [DM, 4 * DM], BF16)
    scr("wuT", [DM, 4 * DM], BF16)
    scr("wdT", [4 * DM, DM], BF16)
    scr("ps4tm", [FW, DM])

    taps_out = {}
    for t in taps:
        src = SC[t]
        taps_out[t] = nc.dram_tensor("tap_" + t, list(src.shape),
                                     src.dtype, kind="ExternalOutput")

    stack = contextlib.ExitStack()
    pool_sems = {}
    with (
        nc.sbuf_tensor("arena", [128, 52224], F32) as arena_h,
        nc.psum_tensor("pb0", [128, 512], F32) as pb0,
        nc.psum_tensor("pb1", [128, 512], F32) as pb1,
        nc.psum_tensor("pb2", [128, 512], F32) as pb2,
        nc.psum_tensor("pb3", [128, 512], F32) as pb3,
        nc.psum_tensor("pb4", [128, 512], F32) as pb4,
        nc.psum_tensor("pb5", [128, 512], F32) as pb5,
        nc.psum_tensor("pt0", [128, 1024], BF16) as pt0,
        nc.psum_tensor("pt1", [128, 1024], BF16) as pt1,
        nc.semaphore("s_dma") as h_dma,
        nc.semaphore("s_dmag") as h_dmag,
        nc.semaphore("s_pt") as h_pt,
        nc.semaphore("s_pv") as h_pv,
        nc.semaphore("s_ps") as h_ps,
        nc.semaphore("s_pg") as h_pg,
        nc.semaphore("s_pq") as h_pq,
        nc.semaphore("s_cc") as h_cc,
        nc.Block() as block,
    ):
        for i in range(64):
            pool_sems["tok%02d" % i] = stack.enter_context(nc.semaphore("tk%02d" % i))
        sems = {
            "dma_sync": h_dma, "dma_gpsimd": h_dmag,
            "p_tensor": h_pt, "p_vector": h_pv, "p_scalar": h_ps,
            "p_gpsimd": h_pg, "p_sync": h_pq, "cc": h_cc,
        }
        sems.update(pool_sems)
        r = R(nc, sems)
        ar = Arena(arena_h, 52224)
        r.hooks = [ar.flush_frees]
        fbanks = [Bank(b[:], i) for i, b in enumerate([pb0, pb1, pb2, pb3, pb4, pb5])]
        tbanks = [Bank(b[:], i) for i, b in enumerate([pt0, pt1])]

        ctx = Ctx(nc, r, ar, fbanks, tbanks, IN, SC, y_out)
        emit_stages(ctx, stop_stage)

        r.barrier()
        toks = []
        for t, dst in taps_out.items():
            toks.append(r.dma(dst.ap(), SC[t].ap()))
        for t in toks:
            r._wait("sync", t)

        r.emit_into(block)
        stack.close()
    return nc


class Ctx:
    def __init__(self, nc, r, ar, fbanks, tbanks, IN, SC, y_out):
        self.carry = {}
        self.nc = nc
        self.r = r
        self.ar = ar
        self.fbanks = fbanks
        self.tbanks = tbanks
        self.IN = IN
        self.SC = SC
        self.y = y_out
        self.fb_i = 0
        self.tb_i = 0

    def fbank(self):
        b = self.fbanks[self.fb_i % len(self.fbanks)]
        self.fb_i += 1
        return b

    def tbank(self):
        b = self.tbanks[self.tb_i % len(self.tbanks)]
        self.tb_i += 1
        return b



def load_fm_f32(ctx, dst_ap3, dram, ncols=FW):
    src = dram.ap().rearrange("(c p) t -> p c t", p=128)
    return ctx.r.dma(dst_ap3, src)


def store_fm(ctx, dram, src_ap3, wait=()):
    dst = dram.ap().rearrange("(c p) t -> p c t", p=128)
    return ctx.r.dma(dst, src_ap3, wait=wait)


def load_wtiles(ctx, name, dram, rows, cols, dtype=BF16, scale_col=None,
                col_ap=None, sub1=False):
    r, ar = ctx.r, ctx.ar
    nch = rows // 128
    ar.alloc(name + "_stg", nch * cols)
    stg = ar.f32(name + "_stg", [128, nch, cols]) if dram.dtype == F32 else \
        ar.i32(name + "_stg", [128, nch, cols])
    src = dram.ap().rearrange("(c p) n -> p c n", p=128)
    if col_ap is not None:
        src = col_ap
    t_ld = r.dma(stg, src)
    ar.alloc(name, (nch * cols + 1) // 2)
    wb = ar.bf16(name, [128, nch, cols])
    if sub1:
        tok = r.v("tensor_scalar", wb, stg, 1.0, None, ALU.subtract,
                  wait=(t_ld,), inc=True)
    elif scale_col is not None:
        tok = r.v("tensor_scalar", wb, stg, scale_col, None, ALU.mult,
                  wait=(t_ld,), inc=True)
    else:
        tok = r.v("tensor_copy", wb, stg, wait=(t_ld,), inc=True)
    ar.free(name + "_stg")
    return wb, tok


def load_col(ctx, name, dram):
    r, ar = ctx.r, ctx.ar
    shape = list(dram.shape)
    n = int(np.prod(shape[1:]))
    if dram.dtype == BF16:
        ar.alloc(name, (n + 1) // 2)
        dst = ar.bf16(name, [128] + shape[1:])
    else:
        ar.alloc(name, n)
        dst = ar.f32(name, [128] + shape[1:])
    tok = r.dma(dst, dram.ap())
    return dst, tok


def gemm_fm(ctx, w_ap3, x_ap3_bf, nin, psum_consumer, wait_ops=(),
            groups=GRPS, nout=8):
    r = ctx.r
    for oc in range(nout):
        for (g0, gw) in groups:
            bank = ctx.fbank()
            tok = None
            for ic in range(nin):
                first = ic == 0
                tok = r.te("matmul", bank.ap[:, 0:gw],
                           w_ap3[:, ic, 128 * oc:128 * oc + 128],
                           x_ap3_bf[:, ic, g0:g0 + gw],
                           start=first, stop=(ic == nin - 1),
                           wait=(((bank.free_tok,) if bank.free_tok else ())
                                 + tuple(wait_ops)) if first else (),
                           inc=(True if ic == nin - 1 else None))
            free_tok = psum_consumer(oc, g0, gw, bank, tok)
            bank.free_tok = free_tok


def conv3_fm(ctx, out_bf3, in_ap3, w_cols, b_col, w1_plus1_col=None,
             in_is_bf=True, wait=(), silu_after=False, out_f32=None):
    r, ar = ctx.r, ctx.ar
    ar.alloc("conv_u", FW + 2)
    ub = ar.bf16("conv_u", [128, 2, FW])
    dst0 = out_bf3 if out_f32 is None else out_f32
    r.v("memset", dst0[:, :, 0:1], 0.0, inc=True)
    r.v("memset", dst0[:, :, FW - 1:FW], 0.0, inc=True)
    toks = []
    for c in range(NC_CH):
        par = c % 2
        u = ub[:, par:par + 1, :]
        w0 = w_cols[:, c, 0:1]
        w1 = w1_plus1_col[:, c:c + 1] if w1_plus1_col is not None else w_cols[:, c, 1:2]
        w2 = w_cols[:, c, 2:3]
        war = (toks[c - 2],) if c >= 2 else ()
        t0 = r.s("activation", u[:, 0, 1:FW], in_ap3[:, c, 0:FW - 1], AF.Identity,
                 bias=b_col[:, c:c + 1], scale=w0, wait=wait + war, inc=True)
        t1 = r.v("scalar_tensor_tensor", u[:, 0, 1:FW], in_ap3[:, c, 1:FW], w1,
                 u[:, 0, 1:FW], ALU.mult, ALU.add, wait=(t0,), inc=True)
        dst = out_bf3 if out_f32 is None else out_f32
        t2 = r.v("scalar_tensor_tensor", dst[:, c, 1:FW - 1], in_ap3[:, c, 2:FW], w2,
                 u[:, 0, 1:FW - 1], ALU.mult, ALU.add, wait=(t1,), inc=True)
        toks.append(t2)
        if silu_after:
            toks[-1] = r.s("activation", dst[:, c, 1:FW - 1], dst[:, c, 1:FW - 1],
                           AF.Silu, wait=(t2,), inc=True)
    ar.free("conv_u")
    return toks


def ln_fm(ctx, x_ap3_f32, sq_src_bf=None, ones_bf=None, name="ln",
          x_bf_for_sums=None, wait=()):
    r, ar = ctx.r, ctx.ar
    tmp_names = []
    if x_bf_for_sums is None:
        ar.alloc(name + "_xb", (NC_CH * FW + 1) // 2)
        x_bf_for_sums = ar.bf16(name + "_xb", [128, NC_CH, FW])
        t_cast = r.v("tensor_copy", x_bf_for_sums, x_ap3_f32, wait=wait, inc=True)
        tmp_names.append(name + "_xb")
    else:
        t_cast = None
    ar.alloc(name + "_sq", (NC_CH * FW + 1) // 2)
    sq = ar.bf16(name + "_sq", [128, NC_CH, FW])
    t_sq = r.s("activation", sq, x_ap3_f32, AF.Square, wait=wait, inc=True)
    tmp_names.append(name + "_sq")

    ar.alloc(name + "_r", FW)
    ar.alloc(name + "_mu", FW)
    r_full = ctx.ar.f32(name + "_r", [128, FW])
    mu_full = ctx.ar.f32(name + "_mu", [128, FW])
    ar.alloc(name + "_t", 512)
    ttile = ctx.ar.f32(name + "_t", [128, 512])

    last = None
    for (g0, gw) in GRPS:
        b_s = ctx.fbank()
        b_q = ctx.fbank()
        for ic in range(NC_CH):
            w = (b_s.free_tok, t_cast) if ic == 0 else ()
            r.te("matmul", b_s.ap[:, 0:gw], ones_bf, x_bf_for_sums[:, ic, g0:g0 + gw],
                 start=(ic == 0), stop=(ic == NC_CH - 1), wait=w)
        t_s = r.tok("tensor")
        for ic in range(NC_CH):
            w = (b_q.free_tok, t_sq) if ic == 0 else ()
            r.te("matmul", b_q.ap[:, 0:gw], ones_bf, sq[:, ic, g0:g0 + gw],
                 start=(ic == 0), stop=(ic == NC_CH - 1), wait=w)
        t_q = r.tok("tensor")
        t_mu = r.s("activation", mu_full[:, g0:g0 + gw], b_s.ap[:, 0:gw],
                   AF.Identity, scale=1.0 / DM, wait=(t_s,), inc=True)
        t_m2 = r.s("activation", ttile[:, 0:gw], b_s.ap[:, 0:gw],
                   AF.Square, scale=1.0 / DM,
                   wait=(last,) if last is not None else (), inc=True)
        t_var = r.v("scalar_tensor_tensor", ttile[:, 0:gw], b_q.ap[:, 0:gw],
                    1.0 / DM, ttile[:, 0:gw], ALU.mult, ALU.subtract,
                    wait=(t_q, t_m2), inc=True)
        b_s.free_tok = t_var
        b_q.free_tok = t_var
        t_e = r.v("tensor_scalar", ttile[:, 0:gw], ttile[:, 0:gw], EPS, None,
                  ALU.add, wait=(t_var,), inc=True)
        t_rc = r.v("reciprocal", ttile[:, 0:gw], ttile[:, 0:gw], wait=(t_e,), inc=True)
        last = r.s("activation", r_full[:, g0:g0 + gw], ttile[:, 0:gw], AF.Sqrt,
                   wait=(t_rc,), inc=True)
    ar.free(name + "_t")
    for t in tmp_names:
        ar.free(t)
    return r_full, mu_full, last



def emit_stages(ctx, stop_stage):
    order = ["prep", "s1_cpe1", "s2_actp", "s2b_in", "s3_qk", "s4_kv",
             "s5a_z", "s5b_attn", "s6_mlp", "s7_ffn0", "s8_l1qkv", "s9_l1attn",
             "s10a_wq", "s10b_ffn"]
    for st in order:
        globals()["stage_" + st](ctx)
        ctx.r.barrier()
        if stop_stage == st:
            return


def stage_prep(ctx):
    r, ar, IN = ctx.r, ctx.ar, ctx.IN
    r.v("memset", ar.h[:, 0:26112], 0.0, inc=True)
    r.g("memset", ar.h[:, 26112:52224], 0.0, inc=True)
    r.barrier()
    ar.alloc("ident", 64)
    ar.alloc("ones128", 64)
    ar.alloc("iota", 128)
    ar.alloc("pid", 1)
    ident = ar.bf16("ident", [128, 128])
    ones128 = ar.bf16("ones128", [128, 128])
    iota = ar.f32("iota", [128, 128])
    pid = ar.f32("pid", [128, 1])
    r.g("iota", iota, pattern=[[1, 128]], base=0, channel_multiplier=0,
        allow_small_or_imprecise_dtypes=True, inc=True)
    r.g("iota", pid, pattern=[[0, 1]], base=0, channel_multiplier=1,
        allow_small_or_imprecise_dtypes=True, inc=True)
    ar.alloc("rope", 4 * 4)
    rope = ar.f32("rope", [128, 16])
    cos_t, sin_t = rope[:, 0:4], rope[:, 4:8]
    ar.alloc("ropei", 4)
    ri = ar.f32("ropei", [128, 4])
    r.g("iota", ri, pattern=[[128, 4]], base=0, channel_multiplier=1,
        allow_small_or_imprecise_dtypes=True, inc=True)
    r.barrier()
    r.v("tensor_scalar", ident, iota, pid, None, ALU.is_equal, inc=True)
    r.v("memset", ones128, 1.0, inc=True)
    t_th = r.s("activation", rope[:, 8:12], ri, AF.Exp,
               scale=-float(np.log(10000.0) / 512.0), inc=True)
    ar.alloc("halfpi", 1)
    hp = ar.f32("halfpi", [128, 1])
    t_hp = r.v("memset", hp, float(np.pi / 2), inc=True)
    t_sn = r.s("activation", sin_t, rope[:, 8:12], AF.Sin, wait=(t_th,), inc=True)
    t_cs = r.s("activation", cos_t, rope[:, 8:12], AF.Sin, scale=-1.0, bias=hp,
               wait=(t_th, t_hp), inc=True)
    ar.alloc("rope2", 16)
    rope2 = ar.f32("rope2", [128, 16])
    c2v = rope2[:, 0:8].rearrange("p (a b) -> p a b", b=2)
    s2v = rope2[:, 8:16].rearrange("p (a b) -> p a b", b=2)
    prev = (t_sn, t_cs)
    for par in range(2):
        for half in range(2):
            tA = r.v("tensor_copy", c2v[64 * half:64 * half + 64, :, par],
                     cos_t[64 * par:64 * par + 64, 0:4], wait=prev, inc=True)
            tB = r.v("tensor_copy", s2v[64 * half:64 * half + 64, :, par],
                     sin_t[64 * par:64 * par + 64, 0:4], wait=(tA,), inc=True)
            prev = (tB,)
    ar.free("ropei")
    ar.free("halfpi")


def stage_s1_cpe1(ctx):
    r, ar, IN, SC = ctx.r, ctx.ar, ctx.IN, ctx.SC
    ident = ar.bf16("ident", [128, 128])
    ones128 = ar.bf16("ones128", [128, 128])

    ar.alloc("x0tm", NT * DM)
    x0tm = ar.f32("x0tm", [128, NT, DM])
    t_x0 = r.dma(x0tm, IN["xshard"].ap().rearrange("(t p) c -> p t c", p=128))
    ar.alloc("hitm", (NT * DM + 1) // 2)
    ar.alloc("lotm", (NT * DM + 1) // 2)
    hi = ar.bf16("hitm", [128, NT, DM])
    lo = ar.bf16("lotm", [128, NT, DM])
    t_hi = r.v("tensor_copy", hi, x0tm, wait=(t_x0,), inc=True)
    t_lo = r.v("tensor_tensor", lo, x0tm, hi, ALU.subtract, wait=(t_hi,), inc=True)
    ar.free("x0tm")
    r.barrier()

    ar.alloc("x0fm", NC_CH * FW)
    x0fm = ar.f32("x0fm", [128, NC_CH, FW])
    for tile in range(NT):
        bank = ctx.tbank()
        toks = []
        for cc in range(NC_CH):
            w = (bank.free_tok, t_hi) if cc == 0 else ()
            toks.append(r.te("transpose", bank.ap[:, 128 * cc:128 * cc + 128],
                             hi[:, tile, 128 * cc:128 * cc + 128], ident,
                             wait=w, inc=(True if cc == NC_CH - 1 else None)))
        t_tr = toks[-1]
        t_ev = r.v("tensor_copy",
                   x0fm[:, :, 128 * tile:128 * tile + 128],
                   bank.ap.rearrange("p (c t) -> p c t", c=NC_CH),
                   wait=(t_tr,), inc=True)
        bank.free_tok = t_ev
    for tile in range(NT):
        bank = ctx.tbank()
        toks = []
        for cc in range(NC_CH):
            w = (bank.free_tok, t_lo) if cc == 0 else ()
            toks.append(r.te("transpose", bank.ap[:, 128 * cc:128 * cc + 128],
                             lo[:, tile, 128 * cc:128 * cc + 128], ident,
                             wait=w, inc=(True if cc == NC_CH - 1 else None)))
        t_tr = toks[-1]
        t_ev = r.v("tensor_tensor",
                   x0fm[:, :, 128 * tile:128 * tile + 128],
                   x0fm[:, :, 128 * tile:128 * tile + 128],
                   bank.ap.rearrange("p (c t) -> p c t", c=NC_CH),
                   ALU.add, wait=(t_tr,), inc=True)
        bank.free_tok = t_ev
    ar.free("hitm")
    ar.free("lotm")
    t_x0fm = r.tok("vector")
    t_sp = store_fm(ctx, SC["x0fm"], x0fm, wait=(t_x0fm,))
    r.barrier()

    cpe1w, t_w = load_col(ctx, "cpe1w", IN["cpe1_w"])
    cpe1b, t_b = load_col(ctx, "cpe1b", IN["cpe1_b"])
    ar.alloc("cpe1w1p", NC_CH)
    w1p = ar.f32("cpe1w1p", [128, NC_CH])
    t_w1p = r.v("tensor_scalar", w1p, cpe1w[:, :, 1], 1.0, None, ALU.add,
                wait=(t_w,), inc=True)

    ar.alloc("c1", NC_CH * FW)
    c1 = ar.f32("c1", [128, NC_CH, FW])
    conv_toks = conv3_fm(ctx, None, x0fm, cpe1w, cpe1b, w1_plus1_col=w1p,
                         wait=(t_x0fm, t_w, t_b, t_w1p), out_f32=c1)
    t_c1 = r.tok("vector")
    t_c1s = store_fm(ctx, SC["c1"], c1, wait=(t_c1,))
    ar.free("x0fm")
    ar.free("cpe1w"); ar.free("cpe1b"); ar.free("cpe1w1p")
    r.barrier()

    rfull, mufull, t_stats = ln_fm(ctx, c1, ones_bf=ones128, name="n1",
                                   wait=(t_c1,))
    r.barrier()
    n1g, t_g = load_col(ctx, "n1g", IN["norm1_g"])
    n1b, t_bb = load_col(ctx, "n1b", IN["norm1_b"])
    ar.alloc("xn_t", NC_CH * FW)
    xt = ar.f32("xn_t", [128, NC_CH, FW])
    ar.alloc("xn", (NC_CH * FW + 1) // 2, top=True)
    xn = ar.bf16("xn", [128, NC_CH, FW])
    mub = mufull.unsqueeze(1).broadcast_to((128, NC_CH, FW))
    rb = rfull.unsqueeze(1).broadcast_to((128, NC_CH, FW))
    t1 = r.v("tensor_tensor", xt, c1, mub, ALU.subtract,
             wait=(t_stats, t_c1), inc=True)
    t2 = r.v("tensor_tensor", xt, xt, rb, ALU.mult, wait=(t1,), inc=True)
    for c in range(NC_CH):
        r.s("activation", xn[:, c, :], xt[:, c, :], AF.Identity,
            scale=n1g[:, c:c + 1], bias=n1b[:, c:c + 1],
            wait=(t2, t_g, t_bb), inc=True)
    t_xn = r.tok("scalar")
    ctx.carry["xn"] = (xn, t_xn)
    ar.free("xn_t")
    ar.free("c1")
    ar.free("n1_r"); ar.free("n1_mu")
    ar.free("n1g"); ar.free("n1b")



def load_fm_bf(ctx, name, dram, wait=()):
    r, ar = ctx.r, ctx.ar
    ar.alloc(name, (NC_CH * FW + 1) // 2)
    dst = ar.bf16(name, [128, NC_CH, FW])
    tok = r.dma(dst, dram.ap().rearrange("(c p) t -> p c t", p=128), wait=wait)
    return dst, tok


def load_fm_f32_t(ctx, name, dram, wait=()):
    r, ar = ctx.r, ctx.ar
    ar.alloc(name, NC_CH * FW)
    dst = ar.f32(name, [128, NC_CH, FW])
    tok = r.dma(dst, dram.ap().rearrange("(c p) t -> p c t", p=128), wait=wait)
    return dst, tok


def stage_s2_actp(ctx):
    r, ar, IN, SC = ctx.r, ctx.ar, ctx.IN, ctx.SC
    xn, t_xn = ctx.carry["xn"]
    ar.alloc("qk_stg", (NC_CH * FW + 1) // 2, top=True)
    stg = ar.f32("qk_stg", [128, NC_CH, 1024])
    t_ld = r.dma(stg, IN["qk_w"].ap().rearrange("(c p) n -> p c n", p=128)[:, :, 0:1024])
    ctx.carry["qk_stg"] = (stg, t_ld)
    w, t_w = load_wtiles(ctx, "actpw", IN["actp_w"], DM, DM)
    b, t_b = load_col(ctx, "actpb", IN["actp_b"])
    ar.alloc("ares", (NC_CH * FW + 1) // 2)
    ares = ar.bf16("ares", [128, NC_CH, FW])

    def consumer(oc, g0, gw, bank, te_tok):
        return r.s("activation", ares[:, oc, g0:g0 + gw], bank.ap[:, 0:gw],
                   AF.Silu, bias=b[:, oc:oc + 1], wait=(te_tok, t_b), inc=True)

    gemm_fm(ctx, w, xn, NC_CH, consumer, wait_ops=(t_xn, t_w))
    t_done = r.tok("scalar")
    store_fm(ctx, SC["ares"], ares, wait=(t_done,))
    for n in ("actpw", "actpb", "ares"):
        ar.free(n)


def stage_s2b_in(ctx):
    r, ar, IN, SC = ctx.r, ctx.ar, ctx.IN, ctx.SC
    xn, t_xn = ctx.carry.pop("xn")
    w, t_w = load_wtiles(ctx, "inw", IN["in_w"], DM, DM)
    b, t_b = load_col(ctx, "inb", IN["in_b"])
    dwcw, t_dw = load_col(ctx, "dwcw", IN["dwc_w"])
    dwcb, t_db = load_col(ctx, "dwcb", IN["dwc_b"])
    ar.alloc("mrow", (FW + 1) // 2)
    mv = ar.bf16("mrow", [128, FW])
    t_m = r.dma(mv, IN["mask_full_bf"].ap())
    ar.alloc("xipre", (NC_CH * FW + 1) // 2)
    xipre = ar.bf16("xipre", [128, NC_CH, FW])
    evac_toks = []

    def consumer(oc, g0, gw, bank, te_tok):
        t = r.s("activation", xipre[:, oc, g0:g0 + gw], bank.ap[:, 0:gw],
                AF.Identity, bias=b[:, oc:oc + 1], wait=(te_tok, t_b), inc=True)
        evac_toks.append(t)
        return t

    gemm_fm(ctx, w, xn, NC_CH, consumer, wait_ops=(t_xn, t_w))
    t_pre = r.tok("scalar")
    VLO, VHI = HALO + OWN - 32, HALO + OWN
    t_fix = t_pre
    for c in range(NC_CH):
        t_fix = r.v("tensor_tensor", xipre[:, c, VLO:VHI], xipre[:, c, VLO:VHI],
                    mv[:, VLO:VHI], ALU.mult, wait=(t_pre, t_m), inc=True)
    ar.alloc("xi", (NC_CH * FW + 1) // 2, top=True)
    xi = ar.bf16("xi", [128, NC_CH, FW])
    toks = conv3_fm(ctx, xi, xipre, dwcw, dwcb, wait=(t_fix, t_dw, t_db),
                    silu_after=True)
    t_xi = r.tok("scalar")
    ctx.carry["xi"] = (xi, t_xi)
    for n in ("xn", "inw", "inb", "dwcw", "dwcb", "mrow", "xipre"):
        ar.free(n)


def _rope_fm(ctx, out_bf, in_bf, cos_t, sin_t, t_in, name, eng="vector"):
    r, ar = ctx.r, ctx.ar
    ar.alloc(name + "_t", FW)
    tmp = ar.f32(name + "_t", [128, FW])
    toks = []
    prev = None
    for cc in range(NC_CH):
        cos0 = cos_t[0:64, cc:cc + 1]
        cos64 = cos_t[64:128, cc:cc + 1]
        sin0 = sin_t[0:64, cc:cc + 1]
        sin64 = sin_t[64:128, cc:cc + 1]
        e = in_bf[0:64, cc, :]
        o = in_bf[64:128, cc, :]
        if eng == "vector":
            t1 = r.v("tensor_scalar", tmp[0:64, :], o, sin64, None, ALU.mult,
                     wait=(t_in,) + ((prev,) if prev else ()), inc=True)
            t2 = r.v("scalar_tensor_tensor", out_bf[0:64, cc, :], e, cos0,
                     tmp[0:64, :], ALU.mult, ALU.subtract, wait=(t1,), inc=True)
            t3 = r.v("tensor_scalar", tmp[64:128, :], e, sin0, None, ALU.mult,
                     wait=(t2,), inc=True)
            t4 = r.v("scalar_tensor_tensor", out_bf[64:128, cc, :], o, cos64,
                     tmp[64:128, :], ALU.mult, ALU.add, wait=(t3,), inc=True)
        else:
            W = in_bf.shape[-1]
            t1 = r.op(eng, "tensor_tensor", tmp[0:64, :], o,
                      sin64.broadcast_to((64, W)), ALU.mult,
                      wait=(t_in,) + ((prev,) if prev else ()), inc=True)
            ta = r.op(eng, "tensor_tensor", out_bf[0:64, cc, :], e,
                      cos0.broadcast_to((64, W)), ALU.mult, wait=(t1,), inc=True)
            t2 = r.op(eng, "tensor_tensor", out_bf[0:64, cc, :],
                      out_bf[0:64, cc, :], tmp[0:64, :], ALU.subtract,
                      wait=(ta,), inc=True)
            t3 = r.op(eng, "tensor_tensor", tmp[64:128, :], e,
                      sin0.broadcast_to((64, W)), ALU.mult, wait=(t2,), inc=True)
            tb = r.op(eng, "tensor_tensor", out_bf[64:128, cc, :], o,
                      cos64.broadcast_to((64, W)), ALU.mult, wait=(t3,), inc=True)
            t4 = r.op(eng, "tensor_tensor", out_bf[64:128, cc, :],
                      out_bf[64:128, cc, :], tmp[64:128, :], ALU.add,
                      wait=(tb,), inc=True)
        prev = t4
        toks.append(t4)
    ar.free(name + "_t")
    return toks


def _fm_to_tm(ctx, tm_dst_bf, fm_src_bf, ident, t_src, mask_col=None, extra_wait=()):
    r = ctx.r
    toks = []
    for t in range(NT):
        col0 = HALO + 128 * t
        roff = 0
        if col0 + 128 > FW:
            roff = 32
            col0 = HALO + 128 * t - 32
        bank = ctx.tbank()
        tok = None
        for cc in range(NC_CH):
            w = ((bank.free_tok,) if bank.free_tok else ()) + ((t_src,) if cc == 0 else ())
            tok = r.te("transpose", bank.ap[:, 128 * cc:128 * cc + 128],
                       fm_src_bf[:, cc, col0:col0 + 128], ident,
                       wait=w if cc == 0 else (),
                       inc=(True if cc == NC_CH - 1 else None))
        nrows = 128 - roff if t < NT - 1 else OWN - 128 * t
        src = bank.ap.rearrange("p (c t2) -> p c t2", c=NC_CH)[roff:roff + nrows, :, :]
        dst = tm_dst_bf[0:nrows, t, :].rearrange("p (c t2) -> p c t2", c=NC_CH)
        if mask_col is not None:
            ev = r.v("tensor_scalar", dst, src, mask_col[0:nrows, t:t + 1], None,
                     ALU.mult, wait=(tok,) + tuple(extra_wait), inc=True)
        else:
            ev = r.v("tensor_copy", dst, src, wait=(tok,) + tuple(extra_wait), inc=True)
        bank.free_tok = ev
    return r.tok("vector")


def stage_s3_qk(ctx):
    r, ar, IN, SC = ctx.r, ctx.ar, ctx.IN, ctx.SC
    ident = ar.bf16("ident", [128, 128])
    rope2 = ar.f32("rope2", [128, 16])
    cos_t, sin_t = rope2[:, 0:8], rope2[:, 8:16]
    xi, t_xi = ctx.carry["xi"]
    qkb, t_qb = load_col(ctx, "qkb", IN["qk_b"])
    mtm, t_mtm = load_col(ctx, "mtm", IN["mask_tm_f32"])

    stg, t_ld = ctx.carry.pop("qk_stg")
    ar.alloc("qk_stg2", (NC_CH * FW + 1) // 2)
    stg2 = ar.f32("qk_stg2", [128, NC_CH, 1024])
    t_ld2 = r.dma(stg2, IN["qk_w"].ap().rearrange("(c p) n -> p c n", p=128)[:, :, 1024:2048])
    ar.alloc("qkw", NC_CH * 2048 // 2)
    wqk = ar.bf16("qkw", [128, NC_CH, 2048])
    t_w = None
    for oc2 in range(16):
        half, oc2h = (stg, oc2) if oc2 < 8 else (stg2, oc2 - 8)
        psrc = half[:, :, 128 * oc2h:128 * oc2h + 128].rearrange(
            "p c (j par) -> p c par j", par=2)
        pdst = wqk[:, :, 128 * oc2:128 * oc2 + 128].rearrange(
            "p c (par j) -> p c par j", par=2)
        t_w = r.v("tensor_copy", pdst, psrc,
                  wait=(t_ld,) if oc2 < 8 else (t_ld2,), inc=True)
    t_w = r.tok("vector")
    ar.free("qk_stg"); ar.free("qk_stg2")
    r.barrier()

    ar.alloc("tmpA0", 512)
    ar.alloc("tmpA1", 512)
    ar.alloc("tmpB0", 512)
    ar.alloc("tmpB1", 512)
    tmpAs = [ar.f32("tmpA0", [128, 512]), ar.f32("tmpA1", [128, 512])]
    tmpBs = [ar.f32("tmpB0", [128, 512]), ar.f32("tmpB1", [128, 512])]

    prev = {"p4": {0: None, 1: None}, "i": 0}

    def elu_consumer(out_bf, boff):
        def consumer(oc, g0, gw, bank, te_tok):
            par = prev["i"] % 2
            prev["i"] += 1
            tmpA, tmpB = tmpAs[par], tmpBs[par]
            lastp = prev["p4"][par]
            bcol = qkb[:, boff + oc:boff + oc + 1]
            w1 = (te_tok, t_qb) + ((lastp,) if lastp else ())
            p1 = r.v("tensor_scalar", tmpA[:, 0:gw], bank.ap[:, 0:gw], bcol, 0.0,
                     ALU.add, ALU.min, wait=w1, inc=True)
            p2 = r.s("activation", tmpB[:, 0:gw], tmpA[:, 0:gw], AF.Exp,
                     wait=(p1,) + ((lastp,) if lastp else ()), inc=True)
            p3 = r.v("tensor_scalar", tmpA[:, 0:gw], bank.ap[:, 0:gw], bcol, 0.0,
                     ALU.add, ALU.max, wait=(p2,), inc=True)
            p4 = r.v("tensor_tensor", out_bf[:, oc, g0:g0 + gw], tmpB[:, 0:gw],
                     tmpA[:, 0:gw], ALU.add, wait=(p3,), inc=True)
            prev["p4"][par] = p4
            return p3
        return consumer

    ar.alloc("qel", (NC_CH * FW + 1) // 2)
    qel = ar.bf16("qel", [128, NC_CH, FW])
    gemm_fm(ctx, wqk[:, :, 0:1024], xi, NC_CH, elu_consumer(qel, 0),
            wait_ops=(t_xi, t_w))
    t_q = r.tok("vector")
    ar.alloc("qrope", (NC_CH * FW + 1) // 2, top=True)
    qrope = ar.bf16("qrope", [128, NC_CH, FW])
    toks = _rope_fm(ctx, qrope, qel, cos_t, sin_t, t_q, "rq")
    t_qr = r.tok("vector")
    ctx.carry["qrope"] = (qrope, t_qr)
    r.barrier()
    ar.free("qel")
    r.barrier()

    ar.alloc("kel", (NC_CH * FW + 1) // 2)
    kel = ar.bf16("kel", [128, NC_CH, FW])
    gemm_fm(ctx, wqk[:, :, 1024:2048], xi, NC_CH, elu_consumer(kel, 8),
            wait_ops=(t_xi, t_w))
    t_k = r.tok("vector")
    ar.alloc("krope", (NC_CH * FW + 1) // 2)
    krope = ar.bf16("krope", [128, NC_CH, FW])
    toks = _rope_fm(ctx, krope, kel, cos_t, sin_t, t_k, "rk")
    t_kr = r.tok("vector")
    r.barrier()
    ar.free("kel"); ar.free("qkw")
    ar.free("tmpA0"); ar.free("tmpB0")
    ar.free("tmpA1"); ar.free("tmpB1")
    r.barrier()

    ar.alloc("krtm", (NT * DM + 1) // 2, top=True)
    krtm = ar.bf16("krtm", [128, NT, DM])
    tzk = r.v("memset", krtm[:, NT - 1, :], 0.0, inc=True)
    t1 = _fm_to_tm(ctx, krtm, krope, ident, t_kr, extra_wait=(tzk,))
    ctx.carry["krtm"] = (krtm, t1)
    ar.alloc("vtm", (NT * DM + 1) // 2, top=True)
    vtm = ar.bf16("vtm", [128, NT, DM])
    tzv = r.v("memset", vtm[:, NT - 1, :], 0.0, inc=True)
    t2 = _fm_to_tm(ctx, vtm, xi, ident, t_xi, mask_col=mtm, extra_wait=(tzv,))
    ctx.carry["vtm"] = (vtm, t2)
    r.barrier()
    for n in ("qkb", "mtm", "krope"):
        ar.free(n)


def stage_s4_kv(ctx):
    r, ar, IN, SC = ctx.r, ctx.ar, ctx.IN, ctx.SC
    krtm, t_k = ctx.carry.pop("krtm")
    vtm, t_v = ctx.carry.pop("vtm")
    mtmb, t_m = load_col(ctx, "mtmb", IN["mask_tm_bf"])
    ar.alloc("kvsb", 128 * NC_CH + 64)
    kvsb = ar.f32("kvsb", [128, NC_CH, 132])

    for hp in range(NC_CH):
        bank = ctx.fbank()
        tok = None
        for t in range(NT):
            w0 = ((bank.free_tok,) if bank.free_tok else ()) + (t_k, t_v, t_m)
            r.te("matmul", bank.ap[:, 0:128],
                 krtm[:, t, 128 * hp:128 * hp + 128],
                 vtm[:, t, 128 * hp:128 * hp + 128],
                 start=(t == 0), stop=False,
                 wait=w0 if t == 0 else ())
            tok = r.te("matmul", bank.ap[:, 128:129],
                       krtm[:, t, 128 * hp:128 * hp + 128],
                       mtmb[:, t:t + 1],
                       start=False, stop=(t == NT - 1),
                       inc=(True if t == NT - 1 else None))
        ev = r.v("tensor_copy", kvsb[:, hp, 0:129], bank.ap[:, 0:129],
                 wait=(tok,), inc=True)
        bank.free_tok = ev
    t_kv = r.tok("vector")
    t_st = r.dma(SC["cc_in"].ap().rearrange("p (c n) -> p c n", n=132),
                 kvsb, wait=(t_kv,))
    cc = r.sems["cc"]
    t_cc = r.op("gpsimd", "collective_compute", "AllReduce", ALU.add,
                replica_groups=[[0, 1], [2, 3], [4, 5], [6, 7]],
                ins=[SC["cc_in"].ap().opt()], outs=[SC["cc_out"].ap().opt()],
                wait=(t_st,), inc=cc)
    t_rd = r.dma(kvsb, SC["cc_out"].ap().rearrange("p (c n) -> p c n", n=132),
                 wait=(t_cc,))
    ar.alloc("kvbf", (NC_CH * 132 + 1) // 2, top=True)
    kvbf = ar.bf16("kvbf", [128, NC_CH, 132])
    t_cast = r.v("tensor_copy", kvbf, kvsb, wait=(t_rd,), inc=True)
    ctx.carry["kvbf"] = (kvbf, t_cast)
    ar.alloc("sel2", NC_CH * 64, top=True)
    sel2 = ar.bf16("sel2", [128, NC_CH, 128])
    tz = r.v("memset", sel2, 0.0, wait=(t_rd,), inc=True)
    stoks = []
    for cc_i in range(NC_CH):
        for m in range(2):
            srcap = kvsb[64 * m:64 * m + 64, cc_i, 128:129].broadcast_to((64, 64))
            stoks.append(r.v("tensor_copy",
                             sel2[64 * m:64 * m + 64, cc_i, 64 * m:64 * m + 64],
                             srcap, wait=(tz,), inc=True))
    t_sel = r.tok("vector")
    ctx.carry["sel2"] = (sel2, t_sel)
    r.barrier()
    for n in ("krtm", "vtm", "mtmb", "kvsb"):
        ar.free(n)



def stage_s5a_z(ctx):
    r, ar, IN, SC = ctx.r, ctx.ar, ctx.IN, ctx.SC
    qel, t_q = ctx.carry["qrope"]
    sel2, t_s = ctx.carry.pop("sel2")
    ar.alloc("epsrow", 64)
    epsl = ar.bf16("epsrow", [128, 128])
    ar.alloc("onesr", FW // 2 + 1)
    onesr = ar.bf16("onesr", [128, FW])
    t_e1 = r.v("memset", epsl[0:1, :], float(4096 * 1e-6), inc=True)
    t_e2 = r.v("memset", onesr[0:1, :], 1.0, inc=True)
    ar.alloc("zrep", (NC_CH * FW + 1) // 2, top=True)
    zrep = ar.bf16("zrep", [128, NC_CH, FW])
    ar.alloc("tmpZ0", 512)
    ar.alloc("tmpZ1", 512)
    tmpZs = [ar.f32("tmpZ0", [128, 512]), ar.f32("tmpZ1", [128, 512])]
    prevc = {0: None, 1: None}
    it = 0
    for cc in range(NC_CH):
        for (g0, gw) in GRPS:
            par = it % 2
            it += 1
            tmpZ = tmpZs[par]
            bank = ctx.fbank()
            w0 = ((bank.free_tok,) if bank.free_tok else ()) + (t_q, t_s, t_e1, t_e2)
            r.te("matmul", bank.ap[:, 0:gw], sel2[:, cc, :],
                 qel[:, cc, g0:g0 + gw], start=True, stop=False, wait=w0)
            tok = r.te("matmul", bank.ap[:, 0:gw], epsl[0:1, :],
                       onesr[0:1, g0:g0 + gw], start=False, stop=True, inc=True)
            ev = r.v("reciprocal", tmpZ[:, 0:gw], bank.ap[:, 0:gw],
                     wait=(tok,) + ((prevc[par],) if prevc[par] else ()), inc=True)
            prevc[par] = r.v("tensor_copy", zrep[:, cc, g0:g0 + gw], tmpZ[:, 0:gw],
                             wait=(ev,), inc=True)
            bank.free_tok = ev
    t_z = r.tok("vector")
    ctx.carry["zrep"] = (zrep, t_z)
    for n in ("sel2", "epsrow", "onesr", "tmpZ0", "tmpZ1"):
        ar.free(n)
    ar.alloc("ares_l", (NC_CH * FW + 1) // 2, top=True)
    ares = ar.bf16("ares_l", [128, NC_CH, FW])
    t_ar = r.dma(ares, SC["ares"].ap().rearrange("(c p) t -> p c t", p=128))
    ctx.carry["ares_l"] = (ares, t_ar)


def stage_s5b_attn(ctx):
    r, ar, IN, SC = ctx.r, ctx.ar, ctx.IN, ctx.SC
    qr, t_qr = ctx.carry.pop("qrope")
    xi, t_xi = ctx.carry.pop("xi")
    ares, t_ar = ctx.carry.pop("ares_l")
    zrep, t_z = ctx.carry.pop("zrep")
    kvbf, t_kv = ctx.carry.pop("kvbf")
    outw, t_w = load_wtiles(ctx, "outw", IN["out_w"], DM, DM)
    outb, t_b = load_col(ctx, "outb", IN["out_b"])
    lw, t_lw = load_col(ctx, "lepew", IN["lepe_w"])
    lb, t_lb = load_col(ctx, "lepeb", IN["lepe_b"])
    ar.alloc("mrow", 16)
    mv = ar.bf16("mrow", [128, 32])
    t_m = r.dma(mv, IN["mask_full_bf"].ap()[:, HALO + OWN - 32:HALO + OWN])
    r.barrier()
    ar.alloc("tmpT0", 512)
    ar.alloc("tmpT1", 512)
    tmpTs = [ar.f32("tmpT0", [128, 512]), ar.f32("tmpT1", [128, 512])]
    ar.alloc("tmpU0", 512)
    ar.alloc("tmpU1", 512)
    tmpUs = [ar.bf16("tmpU0", [128, 1024]), ar.bf16("tmpU1", [128, 1024])]
    ar.alloc("topg", 2048)
    topg = ar.bf16("topg", [128, NC_CH, 512])
    ar.alloc("c1g", 4096)
    c1g = ar.f32("c1g", [128, NC_CH, 512])
    x1g = c1g
    VLO, VHI = HALO + OWN - 32, HALO + OWN
    t_fix = t_xi
    for c in range(NC_CH):
        t_fix = r.v("tensor_tensor", xi[:, c, VLO:VHI], xi[:, c, VLO:VHI],
                    mv, ALU.mult, wait=(t_xi, t_m), inc=True)
    prev = {"x1s": None}
    lastT = {0: None, 1: None}
    lastU = {0: None, 1: None}

    c1r = SC["c1"].ap().rearrange("(c p) t -> p c t", p=128)
    x1r = SC["x1"].ap().rearrange("(c p) t -> p c t", p=128)
    for gi, (g0, gw) in enumerate(GRPS):
        t_c1 = r.dma(c1g[:, :, 0:gw], c1r[:, :, g0:g0 + gw],
                     wait=(prev["x1s"],) if prev["x1s"] else ())
        lo = max(g0, 1)
        hi = min(g0 + gw, FW - 1)
        mz = []
        if g0 == 0:
            mz.append(r.v("memset", topg[:, :, 0:1], 0.0, inc=True))
        if g0 + gw == FW:
            mz.append(r.v("memset", topg[:, :, gw - 1:gw], 0.0, inc=True))
        for cc in range(NC_CH):
            par = cc % 2
            tmpT = tmpTs[par]
            tmpU = tmpUs[par]
            bank = ctx.fbank()
            w0 = ((bank.free_tok,) if bank.free_tok else ()) + (t_qr, t_kv)
            r.te("matmul", bank.ap[0:64, 0:gw], kvbf[0:64, cc, 0:64],
                 qr[0:64, cc, g0:g0 + gw], start=True, stop=True, wait=w0)
            tok = r.te("matmul", bank.ap[64:128, 0:gw], kvbf[64:128, cc, 64:128],
                       qr[64:128, cc, g0:g0 + gw], start=True, stop=True, inc=True)
            wp1 = (tok, t_z) + ((lastT[par],) if lastT[par] else ())
            p1 = r.v("tensor_tensor", tmpT[:, 0:gw], bank.ap[:, 0:gw],
                     zrep[:, cc, g0:g0 + gw], ALU.mult, wait=wp1, inc=True)
            bank.free_tok = p1
            wu0 = (t_fix, t_lw, t_lb) + ((lastU[par],) if lastU[par] else ())
            u0 = r.s("activation", tmpU[:, lo - g0:hi - g0], xi[:, cc, lo - 1:hi - 1],
                     AF.Identity, scale=lw[:, cc, 0:1], bias=lb[:, cc:cc + 1],
                     wait=wu0, inc=True)
            u1 = r.v("scalar_tensor_tensor", tmpU[:, lo - g0:hi - g0],
                     xi[:, cc, lo:hi], lw[:, cc, 1:2], tmpU[:, lo - g0:hi - g0],
                     ALU.mult, ALU.add, wait=(u0, p1), inc=True)
            u2 = r.v("scalar_tensor_tensor", tmpU[:, lo - g0:hi - g0],
                     xi[:, cc, lo + 1:hi + 1], lw[:, cc, 2:3], tmpU[:, lo - g0:hi - g0],
                     ALU.mult, ALU.add, wait=(u1,), inc=True)
            p2 = r.v("tensor_tensor", tmpT[:, lo - g0:hi - g0], tmpT[:, lo - g0:hi - g0],
                     tmpU[:, lo - g0:hi - g0], ALU.add, wait=(u2,), inc=True)
            p3 = r.v("tensor_tensor", topg[:, cc, lo - g0:hi - g0],
                     tmpT[:, lo - g0:hi - g0], ares[:, cc, lo - g0 + g0:hi],
                     ALU.mult, wait=(p2, t_ar), inc=True)
            lastU[par] = p2
            lastT[par] = p3
        t_top = r.tok("vector")
        evs = []
        for oc in range(NC_CH):
            bank = ctx.fbank()
            tok = None
            for ic in range(NC_CH):
                w0 = ((bank.free_tok,) if bank.free_tok else ()) + (t_top, t_w)
                tok = r.te("matmul", bank.ap[:, 0:gw],
                           outw[:, ic, 128 * oc:128 * oc + 128],
                           topg[:, ic, 0:gw], start=(ic == 0), stop=(ic == NC_CH - 1),
                           wait=w0 if ic == 0 else (),
                           inc=(True if ic == NC_CH - 1 else None))
            ev = r.v("scalar_tensor_tensor", x1g[:, oc, 0:gw], bank.ap[:, 0:gw],
                     outb[:, oc:oc + 1], c1g[:, oc, 0:gw], ALU.add, ALU.add,
                     wait=(tok, t_b, t_c1), inc=True)
            evs.append(ev)
            bank.free_tok = ev
        t_x1 = r.tok("vector")
        prev["x1s"] = r.dma(x1r[:, :, g0:g0 + gw], x1g[:, :, 0:gw], wait=(t_x1,))
    r.barrier()
    for n in ("qrope", "xi", "ares_l", "zrep", "kvbf", "outw", "outb",
              "lepew", "lepeb", "mrow", "tmpT0", "tmpT1", "tmpU0", "tmpU1",
              "topg", "c1g"):
        ar.free(n)


def stage_s6_mlp(ctx):
    r, ar, IN, SC = ctx.r, ctx.ar, ctx.IN, ctx.SC
    ones128 = ar.bf16("ones128", [128, 128])
    x1, t_x1 = load_fm_f32_t(ctx, "x1_l", SC["x1"])
    cw, t_w = load_col(ctx, "cpe2w", IN["cpe2_w"])
    cb, t_b = load_col(ctx, "cpe2b", IN["cpe2_b"])
    ar.alloc("mrow", (FW + 1) // 2)
    mv = ar.bf16("mrow", [128, FW])
    t_m = r.dma(mv, IN["mask_full_bf"].ap())
    VLO, VHI = HALO + OWN - 32, HALO + OWN
    t_fix = t_x1
    for c in range(NC_CH):
        t_fix = r.v("tensor_tensor", x1[:, c, VLO:VHI], x1[:, c, VLO:VHI],
                    mv[:, VLO:VHI], ALU.mult, wait=(t_x1, t_m), inc=True)
    ar.alloc("u2", (NC_CH * FW + 1) // 2)
    u2 = ar.bf16("u2", [128, NC_CH, FW])
    toks = conv3_fm(ctx, u2, x1, cw, cb, wait=(t_fix, t_w, t_b))
    t_x2 = None
    for c in range(NC_CH):
        t_x2 = r.v("tensor_tensor", x1[:, c, :], x1[:, c, :], u2[:, c, :],
                   ALU.add, wait=(toks[c],), inc=True)
    t_x2 = r.tok("vector")
    x2 = x1
    ar.free("u2"); ar.free("cpe2w"); ar.free("cpe2b"); ar.free("mrow")
    r.barrier()
    rfull, mufull, t_stats = ln_fm(ctx, x2, ones_bf=ones128, name="n2",
                                   wait=(t_x2,))
    r.barrier()
    n2g, t_g = load_col(ctx, "n2g", IN["norm2_g"])
    n2b, t_bb = load_col(ctx, "n2b", IN["norm2_b"])
    ar.alloc("xn2", (NC_CH * FW + 1) // 2)
    xn2 = ar.bf16("xn2", [128, NC_CH, FW])
    ar.alloc("xt2", 4 * FW)
    xt = ar.f32("xt2", [128, 4, FW])
    mub = mufull.unsqueeze(1).broadcast_to((128, 4, FW))
    rb = rfull.unsqueeze(1).broadcast_to((128, 4, FW))
    t3 = None
    for hh in range(2):
        c0 = 4 * hh
        war = (t3,) if t3 else ()
        t1 = r.v("tensor_tensor", xt, x2[:, c0:c0 + 4, :], mub, ALU.subtract,
                 wait=(t_stats, t_x2) + war, inc=True)
        t2 = r.v("tensor_tensor", xt, xt, rb, ALU.mult, wait=(t1,), inc=True)
        for c in range(4):
            t3 = r.s("activation", xn2[:, c0 + c, :], xt[:, c, :], AF.Identity,
                     scale=n2g[:, c0 + c:c0 + c + 1],
                     bias=n2b[:, c0 + c:c0 + c + 1],
                     wait=(t2, t_g, t_bb), inc=True)
    t_xn2 = r.tok("scalar")
    ar.free("xt2"); ar.free("n2_r"); ar.free("n2_mu")
    ar.free("n2g"); ar.free("n2b")
    r.barrier()
    fc2b, t_f2b = load_col(ctx, "fc2b", IN["fc2_b"])
    t_pb = None
    for c in range(NC_CH):
        t_pb = r.v("tensor_scalar", x2[:, c, :], x2[:, c, :],
                   fc2b[:, c:c + 1], None, ALU.add,
                   wait=(t_f2b, t_xn2), inc=True)
    t_pb = r.tok("vector")
    x0r = SC["x0fm"].ap().rearrange("(c p) t -> p c t", p=128)
    ar.alloc("x0g0", 4096)
    x0g = ar.f32("x0g0", [128, NC_CH, 512])
    prev_add = None
    for gi, (g0, gw) in enumerate(GRPS):
        tl = r.dma(x0g[:, :, 0:gw], x0r[:, :, g0:g0 + gw],
                   wait=(prev_add,) if prev_add else ())
        prev_add = r.v("tensor_tensor", x2[:, :, g0:g0 + gw],
                       x2[:, :, g0:g0 + gw], x0g[:, :, 0:gw],
                       ALU.add, wait=(tl, t_pb), inc=True)
    t_x0a = r.tok("vector")
    fc1b, t_f1b = load_col(ctx, "fc1b", IN["fc1_b"])
    ar.alloc("hg0", 2048)
    ar.alloc("hg1", 2048)
    hgs = [ar.bf16("hg0", [128, NC_CH, 512]), ar.bf16("hg1", [128, NC_CH, 512])]
    f1r = IN["fc1_w"].ap().rearrange("(c p) n -> p c n", p=128)
    f2r = IN["fc2_w"].ap().rearrange("(c p) n -> p c n", p=128)
    ar.alloc("fstg", NC_CH * 512)
    stg = ar.f32("fstg", [128, NC_CH, 512])
    f1qs = [None, None]
    f2qs = [None, None]
    ar.alloc("f1q0", NC_CH * 512)
    ar.alloc("f1q1", NC_CH * 512)
    f1qs = [ar.bf16("f1q0", [128, NC_CH, 1024]), ar.bf16("f1q1", [128, NC_CH, 1024])]
    ar.alloc("f2q0", NC_CH * 512)
    _f2q = ar.bf16("f2q0", [128, NC_CH, 1024])
    f2qs = [_f2q, _f2q]
    t_cast = None
    t_wq = {}
    load_plan = []
    for q in range(4):
        qp = q % 2
        load_plan.append(("f1", q, qp))
        load_plan.append(("f2", q, qp))

    def emit_wload(which, q, qp):
        nonlocal t_cast
        dst = f1qs[qp] if which == "f1" else f2qs[qp]
        war = t_wq.get(("mmdone", which, q - 2 if which == "f1" else q - 1))
        toks = []
        for hhalf in range(2):
            if which == "f1":
                srcap = f1r[:, :, 1024 * q + 512 * hhalf:1024 * q + 512 * hhalf + 512]
            else:
                srcap = f2r[:, 8 * q:8 * q + 8, 512 * hhalf:512 * hhalf + 512]
            tl = r.dma(stg, srcap, wait=((t_cast,) if t_cast else ()))
            t_cast = r.v("tensor_copy",
                         dst.rearrange("p c (a b) -> p c a b", a=2)[:, :, hhalf, :],
                         stg, wait=(tl,) + ((war,) if (war and hhalf == 0) else ()),
                         inc=True)
            toks.append(t_cast)
        t_wq[(which, q)] = toks[-1]

    emit_wload(*load_plan[0])
    emit_wload(*load_plan[1])
    nload = 2
    for q in range(4):
        qp = q % 2
        f1q, f2q, hg = f1qs[qp], f2qs[qp], hgs[qp]
        tc1 = t_wq[("f1", q)]
        tc2 = t_wq[("f2", q)]
        last_f1mm = None
        last_f2mm = None
        for (g0, gw) in GRPS:
            gtoks = []
            for hc in range(NC_CH):
                bank = ctx.fbank()
                tok = None
                for ic in range(NC_CH):
                    w0 = ()
                    if ic == 0:
                        w0 = ((bank.free_tok,) if bank.free_tok else ()) + \
                            (t_xn2, tc1)
                    tok = r.te("matmul", bank.ap[:, 0:gw],
                               f1q[:, ic, 128 * hc:128 * hc + 128],
                               xn2[:, ic, g0:g0 + gw],
                               start=(ic == 0), stop=(ic == NC_CH - 1),
                               wait=w0, inc=(True if ic == NC_CH - 1 else None))
                last_f1mm = tok
                ge = r.s("activation", hg[:, hc, 0:gw], bank.ap[:, 0:gw], AF.Gelu,
                         bias=fc1b[:, 8 * q + hc:8 * q + hc + 1],
                         wait=(tok, t_f1b) +
                         ((t_wq.get(("hgw", qp)),) if t_wq.get(("hgw", qp)) else ()),
                         inc=True)
                gtoks.append(ge)
                bank.free_tok = ge
            for oc in range(NC_CH):
                bank = ctx.fbank()
                tok = None
                for hc in range(NC_CH):
                    w0 = ()
                    if hc == 0:
                        w0 = ((bank.free_tok,) if bank.free_tok else ()) + \
                            tuple(gtoks) + (tc2,)
                    tok = r.te("matmul", bank.ap[:, 0:gw],
                               f2q[:, hc, 128 * oc:128 * oc + 128],
                               hg[:, hc, 0:gw],
                               start=(hc == 0), stop=(hc == NC_CH - 1),
                               wait=w0, inc=(True if hc == NC_CH - 1 else None))
                last_f2mm = tok
                ev = r.v("scalar_tensor_tensor", x2[:, oc, g0:g0 + gw],
                         bank.ap[:, 0:gw], 1.0, x2[:, oc, g0:g0 + gw],
                         ALU.mult, ALU.add, wait=(tok, t_x0a), inc=True)
                bank.free_tok = ev
            t_wq[("hgw", qp)] = last_f2mm
        t_wq[("mmdone", "f1", q)] = last_f1mm
        t_wq[("mmdone", "f2", q)] = last_f2mm
        while nload < len(load_plan) and nload <= 2 * q + 3:
            emit_wload(*load_plan[nload])
            nload += 1
    t_ps = r.tok("vector")
    ctx.carry["ps1"] = (x2, t_ps)
    r.barrier()
    for n in ("hg0", "hg1", "fc1b", "fc2b", "xn2", "x0g0", "fstg",
              "f1q0", "f1q1", "f2q0"):
        ar.free(n)


def outer_ln(ctx, x_ap3, t_in, ones_bf, name):
    r, ar = ctx.r, ctx.ar
    rfull, mufull, t_stats = ln_fm(ctx, x_ap3, ones_bf=ones_bf, name=name,
                                   wait=(t_in,))
    r.barrier()
    nch = x_ap3.shape[1]
    W = x_ap3.shape[2]
    mub = mufull.unsqueeze(1).broadcast_to((128, nch, W))
    rb = rfull.unsqueeze(1).broadcast_to((128, nch, W))
    t1 = r.v("tensor_tensor", x_ap3, x_ap3, mub, ALU.subtract,
             wait=(t_stats, t_in), inc=True)
    tok = r.v("tensor_tensor", x_ap3, x_ap3, rb, ALU.mult, wait=(t1,),
              inc=True)
    ar.free(name + "_r"); ar.free(name + "_mu")
    return tok


def stage_s7_ffn0(ctx):
    r, ar, IN, SC = ctx.r, ctx.ar, ctx.IN, ctx.SC
    ones128 = ar.bf16("ones128", [128, 128])
    x, t_x = ctx.carry.pop("ps1")
    t_ln = outer_ln(ctx, x, t_x, ones128, "ol1")
    r.barrier()
    asc, t_a = load_col(ctx, "asc", IN["ascale"])
    ar.alloc("scol", 2)
    scol = ar.f32("scol", [128, 2])
    t1 = r.v("tensor_scalar", scol[:, 0:1], asc, 1e-5, None, ALU.max,
             wait=(t_a,), inc=True)
    t2 = r.v("reciprocal", scol[:, 1:2], scol[:, 0:1], wait=(t1,), inc=True)
    ar.alloc("rint", (NC_CH * FW + 1) // 2)
    rint = ar.bf16("rint", [128, NC_CH, FW])
    ar.alloc("qtmp", (NC_CH * FW + 1) // 2)
    qh = ar.f16("qtmp", [128, NC_CH, FW])
    a1 = r.v("tensor_scalar", qh, x, scol[:, 1:2], 1536.0, ALU.mult, ALU.add,
             wait=(t2, t_ln), inc=True)
    a2 = r.v("tensor_scalar", rint, qh, 1536.0, 127.0, ALU.subtract, ALU.min,
             wait=(a1,), inc=True)
    t_q = r.v("tensor_scalar", rint, rint, -127.0, None, ALU.max,
              wait=(a2,), inc=True)
    r.barrier()
    ar.free("qtmp")
    ar.alloc("tstg", NC_CH * DM)
    tstg = ar.i32("tstg", [128, NC_CH, DM])
    t_tl = r.dma(tstg, IN["tern"].ap().rearrange("(c p) n -> p c n", p=128))
    ar.alloc("tnat", (NC_CH * DM + 1) // 2)
    tnat = ar.bf16("tnat", [128, NC_CH, DM])
    t_tc = r.v("tensor_scalar", tnat, tstg, 1.0, None, ALU.subtract,
               wait=(t_tl,), inc=True)
    ar.alloc("ternT", (NC_CH * DM + 1) // 2)
    ternT = ar.bf16("ternT", [128, NC_CH, DM])
    ident = ar.bf16("ident", [128, 128])
    for oc in range(NC_CH):
        bank = ctx.tbank()
        tok = None
        for ic in range(NC_CH):
            w0 = ((bank.free_tok,) if bank.free_tok else ()) + ((t_tc,) if ic == 0 else ())
            tok = r.te("transpose", bank.ap[:, 128 * ic:128 * ic + 128],
                       tnat[:, oc, 128 * ic:128 * ic + 128], ident,
                       wait=w0 if ic == 0 else (),
                       inc=(True if ic == NC_CH - 1 else None))
        ev = r.v("tensor_copy",
                 ternT[:, :, 128 * oc:128 * oc + 128],
                 bank.ap.rearrange("p (c t) -> p c t", c=NC_CH),
                 wait=(tok,), inc=True)
        bank.free_tok = ev
    r.barrier()
    ar.free("tstg"); ar.free("tnat")
    ffnb, t_fb = load_col(ctx, "ffnb", IN["ffn_bias"])
    prev_ev = {"t": None}

    def consumer(oc, g0, gw, bank, te_tok):
        w0 = (te_tok, t_fb, t2) + ((prev_ev["t"],) if prev_ev["t"] else ())
        e1 = r.v("scalar_tensor_tensor", x[:, oc, g0:g0 + gw], bank.ap[:, 0:gw],
                 scol[:, 0:1], x[:, oc, g0:g0 + gw], ALU.mult, ALU.add,
                 wait=w0, inc=True)
        e2 = r.v("tensor_scalar", x[:, oc, g0:g0 + gw], x[:, oc, g0:g0 + gw],
                 ffnb[:, oc:oc + 1], None, ALU.add, wait=(e1,), inc=True)
        prev_ev["t"] = e2
        return e1

    gemm_fm(ctx, ternT, rint, NC_CH, consumer, wait_ops=(t_q,))
    t_ps2 = r.tok("vector")
    r.barrier()
    ar.free("rint"); ar.free("ternT"); ar.free("ffnb")
    ar.free("asc"); ar.free("scol")
    r.barrier()
    t_ln2 = outer_ln(ctx, x, t_ps2, ones128, "ol2")
    store_fm(ctx, SC["x2o"], x, wait=(t_ln2,))
    ar.alloc("mrow", (FW + 1) // 2)
    mv = ar.bf16("mrow", [128, FW])
    t_m = r.dma(mv, IN["mask_full_bf"].ap())
    ar.alloc("x2ob", (NC_CH * FW + 1) // 2, top=True)
    xb = ar.bf16("x2ob", [128, NC_CH, FW])
    tb = None
    for c in range(NC_CH):
        tb = r.v("tensor_tensor", xb[:, c, :], x[:, c, :], mv, ALU.mult,
                 wait=(t_ln2, t_m), inc=True)
    tb = r.tok("vector")
    ctx.carry["x2ob"] = (xb, tb)
    r.barrier()
    for n in ("x1_l", "mrow"):
        ar.free(n)



def transpose_w(ctx, dst_bf3, src_bf3, ident, t_src, scale=None):
    r = ctx.r
    for oc in range(NC_CH):
        bank = ctx.tbank()
        tok = None
        for ic in range(NC_CH):
            w0 = ((bank.free_tok,) if bank.free_tok else ()) + ((t_src,) if ic == 0 else ())
            tok = r.te("transpose", bank.ap[:, 128 * ic:128 * ic + 128],
                       src_bf3[:, oc, 128 * ic:128 * ic + 128], ident,
                       wait=w0 if ic == 0 else (),
                       inc=(True if ic == NC_CH - 1 else None))
        ev = r.v("tensor_copy", dst_bf3[:, :, 128 * oc:128 * oc + 128],
                 bank.ap.rearrange("p (c t) -> p c t", c=NC_CH),
                 wait=(tok,), inc=True)
        bank.free_tok = ev
    return r.tok("vector")


def load_wT(ctx, name, dram, ident, scale=None):
    r, ar = ctx.r, ctx.ar
    ar.alloc(name + "_stg", NC_CH * DM)
    stg = ar.f32(name + "_stg", [128, NC_CH, DM])
    t_ld = r.dma(stg, dram.ap().rearrange("(c p) n -> p c n", p=128))
    ar.alloc(name + "_nat", (NC_CH * DM + 1) // 2)
    nat = ar.bf16(name + "_nat", [128, NC_CH, DM])
    if scale is not None:
        t_c = r.v("tensor_scalar", nat, stg, scale, None, ALU.mult,
                  wait=(t_ld,), inc=True)
    else:
        t_c = r.v("tensor_copy", nat, stg, wait=(t_ld,), inc=True)
    ar.free(name + "_stg")
    ar.alloc(name, (NC_CH * DM + 1) // 2)
    dst = ar.bf16(name, [128, NC_CH, DM])
    tok = transpose_w(ctx, dst, nat, ident, t_c)
    ar.free(name + "_nat")
    return dst, tok


def stage_s8_l1qkv(ctx):
    r, ar, IN, SC = ctx.r, ctx.ar, ctx.IN, ctx.SC
    ident = ar.bf16("ident", [128, 128])
    xb, t_x = ctx.carry.pop("x2ob")
    SCALE = 0.125
    ar.alloc("pstg", (NC_CH * FW + 1) // 2, top=True)
    pstg = ar.f32("pstg", [128, NC_CH, DM])
    wseq = ["Wq", "Wk", "Wv", "Wo"]
    pend = {"t": r.dma(pstg, IN["Wq"].ap().rearrange("(c p) n -> p c n", p=128))}

    def wT_from_stg(idx, wname, scale=None):
        t_ld = pend["t"]
        ar.alloc(wname + "T_nat", (NC_CH * DM + 1) // 2)
        nat = ar.bf16(wname + "T_nat", [128, NC_CH, DM])
        if scale is not None:
            t_c = r.v("tensor_scalar", nat, pstg, scale, None, ALU.mult,
                      wait=(t_ld,), inc=True)
        else:
            t_c = r.v("tensor_copy", nat, pstg, wait=(t_ld,), inc=True)
        if idx + 1 < 4:
            pend["t"] = r.dma(pstg,
                              IN[wseq[idx + 1]].ap().rearrange("(c p) n -> p c n", p=128),
                              wait=(t_c,))
        ar.alloc(wname + "T", (NC_CH * DM + 1) // 2)
        dst = ar.bf16(wname + "T", [128, NC_CH, DM])
        tok = transpose_w(ctx, dst, nat, ident, t_c)
        ar.free(wname + "T_nat")
        return dst, tok

    def proj(idx, wname, bname, out_sc, bscale=None):
        w, t_w = wT_from_stg(idx, wname, scale=(SCALE if bscale else None))
        b, t_b = load_col(ctx, "b_" + bname, IN[bname])
        if bscale:
            t_b = r.v("tensor_scalar", b, b, SCALE, None, ALU.mult,
                      wait=(t_b,), inc=True)
        ar.alloc("o_" + wname, (NC_CH * FW + 1) // 2)
        out = ar.bf16("o_" + wname, [128, NC_CH, FW])

        def consumer(oc, g0, gw, bank, te_tok):
            return r.s("activation", out[:, oc, g0:g0 + gw], bank.ap[:, 0:gw],
                       AF.Identity, bias=b[:, oc:oc + 1], wait=(te_tok, t_b),
                       inc=True)

        gemm_fm(ctx, w, xb, NC_CH, consumer, wait_ops=(t_x, t_w))
        t_d = r.tok("scalar")
        if out_sc is not None:
            store_fm(ctx, SC[out_sc], out, wait=(t_d,))
        r.barrier()
        ar.free(wname + "T"); ar.free("b_" + bname)
        return out, t_d

    qo, _ = proj(0, "Wq", "bq", "qfm", bscale=True)
    ar.free("o_Wq")
    r.barrier()
    ko, _ = proj(1, "Wk", "bk", "kfm")
    ar.free("o_Wk")
    r.barrier()
    vo, t_v = proj(2, "Wv", "bv", None)
    ar.free("x2ob")
    r.barrier()
    ar.alloc("vtml", (NT * DM + 1) // 2, top=True)
    vtml = ar.bf16("vtml", [128, NT, DM])
    tz1 = r.v("memset", vtml[:, NT - 1, :], 0.0, inc=True)
    t1 = _fm_to_tm(ctx, vtml, vo, ident, t_v, extra_wait=(tz1,))
    ctx.carry["vtml"] = (vtml, t1)
    ar.alloc("vtma", (22 * DM + 1) // 2, top=True)
    vtma = ar.bf16("vtma", [128, 22, DM])
    tz2 = r.v("memset", vtma[96:128, :, :], 0.0, inc=True)
    tzb = r.v("memset", vtma[64:96, 21, :], 0.0, wait=(tz2,), inc=True)
    tz3 = r.v("memset", vtma[32:64, 21, :], 0.0, wait=(tzb,), inc=True)
    for a in range(22):
        off = HALO + 96 * a
        nb = 2 if a < 21 else 1
        rows = 48 * nb
        bank = ctx.tbank()
        tok = None
        for cc in range(NC_CH):
            inap = vo[:, cc, off:off + rows]
            w0 = ((bank.free_tok,) if bank.free_tok else ()) + ((t_v,) if cc == 0 else ())
            tok = r.te("transpose", bank.ap[0:rows, 128 * cc:128 * cc + 128],
                       inap, ident, wait=w0 if cc == 0 else (),
                       inc=(True if cc == NC_CH - 1 else None))
        ev = r.v("tensor_copy",
                 vtma[0:rows, a, :].rearrange("p (c t) -> p c t", c=NC_CH),
                 bank.ap.rearrange("p (c t) -> p c t", c=NC_CH)[0:rows, :, :],
                 wait=(tok, tz3), inc=True)
        bank.free_tok = ev
    t2 = r.tok("vector")
    ctx.carry["vtma"] = (vtma, t2)
    r.barrier()
    ar.free("o_Wv")
    r.barrier()
    woT, t_wo = wT_from_stg(3, "Wo")
    store_fm(ctx, SC["woT"], woT, wait=(t_wo,))
    r.barrier()
    ar.free("WoT"); ar.free("pstg")



def _build_blockmask(ctx, name, width, nblk):
    r, ar = ctx.r, ctx.ar
    iota = ar.f32("iota", [128, 128])
    pid = ar.f32("pid", [128, 1])
    ar.alloc(name, 2 * width + 8)
    mm = ar.bf16(name, [128, 4 * width + 16])
    mL = mm[:, 0:width]
    mR = mm[:, width:2 * width]
    crow = mm[:, 2 * width:3 * width]
    ones1 = mm[:, 3 * width:4 * width]
    lo = mm[:, 4 * width:4 * width + 8]
    ar.alloc(name + "_t", 2 * width)
    ta = ar.f32(name + "_t", [128, 2 * width])
    a, b = ta[:, 0:width], ta[:, width:2 * width]
    t0 = r.v("tensor_scalar", lo[:, 0:2].bitcast(F32)[:, 0:1], pid, 16.0, None,
             ALU.mult, inc=True)
    locol = lo[:, 0:2].bitcast(F32)[:, 0:1]
    t1 = r.v("tensor_scalar", a, iota[:, 0:width], locol, None, ALU.is_ge,
             wait=(t0,), inc=True)
    t2 = r.v("tensor_scalar", b, iota[:, 0:width], locol, 16.0, ALU.subtract,
             ALU.is_lt, wait=(t1,), inc=True)
    t3 = r.v("tensor_tensor", a, a, b, ALU.mult, wait=(t2,), inc=True)
    t4 = r.v("tensor_scalar", mL, a, float(2 ** 30), None, ALU.mult,
             wait=(t3,), inc=True)
    t5 = r.v("tensor_copy", mR, a, wait=(t4,), inc=True)
    t6 = r.v("memset", crow[0:1, :], -float(2 ** 30), wait=(t5,), inc=True)
    t7 = r.v("memset", ones1[0:1, :], 1.0, wait=(t6,), inc=True)
    ar.free(name + "_t")
    tok = ctx.r.tok("vector")
    return (mL, mR, crow, ones1), tok


def _load_atrous_mask(ctx):
    r, ar, IN = ctx.r, ctx.ar, ctx.IN
    width = 96
    ar.alloc("mska", 2 * width + 8)
    mm = ar.bf16("mska", [128, 4 * width + 16])
    t = r.dma(mm[:, 0:4 * width], IN["mska"].ap())
    return (mm[:, 0:width], mm[:, width:2 * width],
            mm[:, 2 * width:3 * width], mm[:, 3 * width:4 * width]), t


def _l1_scores(ctx, h, qf, kf, mpanels, t_mask, widths, pslab, zcols,
               ident, t_in, atrous, ps_war):
    r = ctx.r
    m, hc = h % 2, h // 2
    ntile = len(widths)
    nslot = 4
    W = pslab.shape[2]
    mL, mR, crow, ones1 = mpanels
    bank = None
    for ti, w in enumerate(widths):
        slot = ti % nslot
        if slot == 0:
            bank = ctx.fbank()
        off = HALO + (96 if atrous else 128) * ti
        qap = qf[64 * m:64 * m + 64, hc, off:off + w]
        kap = kf[64 * m:64 * m + 64, hc, off:off + w]
        w0 = ((bank.free_tok,) if bank.free_tok else ()) + ((t_in, t_mask) if slot == 0 else ())
        r.te("matmul", bank.ap[0:w, 128 * slot:128 * slot + w],
             mL[0:8, 0:w], mR[0:8, 0:w], start=True, stop=False, wait=w0)
        r.te("matmul", bank.ap[0:w, 128 * slot:128 * slot + w],
             crow[0:1, 0:w], ones1[0:1, 0:w], start=False, stop=False)
        tok = r.te("matmul", bank.ap[0:w, 128 * slot:128 * slot + w], qap, kap,
                   start=False, stop=True,
                   inc=(True if (slot == nslot - 1 or ti == ntile - 1) else None))
        if slot == nslot - 1 or ti == ntile - 1:
            ev = None
            for k in range(slot + 1):
                tk = ti - slot + k
                wk = widths[tk]
                ev = r.s("activation", pslab[0:wk, tk, 0:wk],
                         bank.ap[0:wk, 128 * k:128 * k + wk], AF.Exp,
                         wait=(tok,) + ((ps_war,) if ps_war else ()), inc=True)
            bank.free_tok = ev
    t_exp = r.tok("scalar")
    ntile = len(widths)
    zr = zcols[:, 0:ntile]
    t4 = r.v("tensor_reduce", zr, pslab, AX.X, ALU.add, wait=(t_exp,), inc=True)
    t4b = r.v("tensor_scalar", zr, zr, 1e-30, None, ALU.max, wait=(t4,),
              inc=True)
    t5 = r.v("reciprocal", zr, zr, wait=(t4b,), inc=True)
    zexp = zr.unsqueeze(-1).broadcast_to((128, ntile, W))
    t6 = r.v("scalar_tensor_tensor", pslab, pslab, 1.0, zexp, ALU.mult, ALU.mult,
             wait=(t5,), inc=True)
    return t6


def _l1_av(ctx, h, vt, widths, pslab, ptslab, orow, t_war, ident, t6,
           atrous, row0):
    r = ctx.r
    ntile = len(widths)
    nslot = 4
    W = pslab.shape[2]
    tbank = None
    last_tr = None
    for ti, w in enumerate(widths):
        slot = ti % nslot
        if slot == 0:
            tbank = ctx.tbank()
        w0 = ((tbank.free_tok,) if tbank.free_tok else ()) + ((t6,) if slot == 0 else ())
        tok = r.te("transpose", tbank.ap[0:W, 128 * slot:128 * slot + 128],
                   pslab[:, ti, :], ident, wait=w0,
                   inc=(True if (slot == nslot - 1 or ti == ntile - 1) else None))
        if slot == nslot - 1 or ti == ntile - 1:
            n = slot + 1
            t0 = ti - slot
            qn = ptslab.shape[2]
            src = tbank.ap.rearrange("p (s q) -> p s q", q=128)[0:W, 0:n, 0:qn]
            ev = r.v("tensor_copy", ptslab[0:W, t0:t0 + n, :], src,
                     wait=(tok,), inc=True)
            tbank.free_tok = ev
            last_tr = tok
    t_pt = r.tok("vector")
    abank = None
    for ti, w in enumerate(widths):
        slot = ti % nslot
        if slot == 0:
            abank = ctx.fbank()
        krows = w
        w0 = ((abank.free_tok,) if abank.free_tok else ()) + ((t_pt,) if slot == 0 else ())
        tok = r.te("matmul", abank.ap[0:64, 128 * slot:128 * slot + w],
                   vt[0:krows, ti, 64 * h:64 * h + 64],
                   ptslab[0:krows, ti, 0:w], start=True, stop=True,
                   wait=w0, inc=(True if (slot == nslot - 1 or ti == ntile - 1) else None))
        if slot == nslot - 1 or ti == ntile - 1:
            n = slot + 1
            t0 = ti - slot
            last_ev = None
            for k in range(n):
                tk = t0 + k
                wk = widths[tk]
                off = HALO + (96 if atrous else 128) * tk
                srcap = abank.ap[0:64, 128 * k:128 * k + wk]
                dstap = orow[row0:row0 + 64, off:off + wk]
                if atrous:
                    last_ev = r.v("tensor_tensor", dstap, dstap, srcap,
                                  ALU.add, wait=(tok, t_war), inc=True)
                else:
                    last_ev = r.s("activation", dstap, srcap,
                                  AF.Identity, wait=(tok, t_war), inc=True)
            abank.free_tok = last_ev
    t_done = r.tok("vector" if atrous else "scalar")
    return t_done, last_tr


def stage_s9_l1attn(ctx):
    r, ar, IN, SC = ctx.r, ctx.ar, ctx.IN, ctx.SC
    ident = ar.bf16("ident", [128, 128])
    qf, t_q = load_fm_bf(ctx, "qf_l", SC["qfm"])
    kf, t_k = load_fm_bf(ctx, "kf_l", SC["kfm"])
    vtml, t_vl = ctx.carry.pop("vtml")
    vtma, t_va = ctx.carry.pop("vtma")
    ar.alloc("ofm_sb", (NC_CH * FW + 1) // 2, top=True)
    ofm_sb = ar.bf16("ofm_sb", [128, NC_CH, FW])
    mpl, t_m1 = _build_blockmask(ctx, "mskl", 128, 8)
    mpa, t_m2 = _load_atrous_mask(ctx)
    ar.alloc("pslabL", (NT * 128 + 1) // 2)
    pslabL = ar.bf16("pslabL", [128, NT, 128])
    ar.alloc("pslabA", (22 * 96 + 1) // 2)
    pslabA = ar.bf16("pslabA", [128, 22, 96])
    ar.alloc("ptslabL", (NT * 128 + 1) // 2)
    ptslabL = ar.bf16("ptslabL", [128, NT, 128])
    ar.alloc("ptslabA", (22 * 96 + 1) // 2)
    ptslabA = ar.bf16("ptslabA", [128, 22, 96])
    ar.alloc("zcolsL", 2 * NT)
    zcolsL = ar.f32("zcolsL", [128, 2 * NT])
    ar.alloc("zcolsA", 44)
    zcolsA = ar.f32("zcolsA", [128, 44])
    t_z0 = r.v("memset", ofm_sb, 0.0, inc=True)
    t_z1 = r.v("memset", pslabL, 0.0, wait=(t_z0,), inc=True)
    t_z2 = r.v("memset", pslabA, 0.0, wait=(t_z1,), inc=True)
    t_z3 = r.v("memset", ptslabA[:, :, :], 0.0, wait=(t_z2,), inc=True)
    r.barrier()
    widthsL = [128] * 16 + [16]
    widthsA = [96] * 21 + [48]
    psL_war = None
    psA_war = None
    for h in range(16):
        m = h % 2
        orow = ofm_sb[:, h // 2, :]
        t6L = _l1_scores(ctx, h, qf, kf, mpl, t_m1, widthsL, pslabL,
                         zcolsL, ident, t_q, False, psL_war)
        t6A = _l1_scores(ctx, h, qf, kf, mpa, t_m2, widthsA, pslabA,
                         zcolsA, ident, t_k, True, psA_war)
        t_loc, psL_war = _l1_av(ctx, h, vtml, widthsL, pslabL, ptslabL,
                                orow, t_z3, ident, t6L, False, 64 * m)
        t_atr, psA_war = _l1_av(ctx, h, vtma, widthsA, pslabA, ptslabA,
                                orow, t_loc, ident, t6A, True, 64 * m)
    r.barrier()
    for n in ("qf_l", "kf_l", "vtml", "vtma", "mskl", "mska",
              "pslabL", "pslabA", "ptslabL", "ptslabA", "zcolsL", "zcolsA"):
        ar.free(n)
    t_ofm = ctx.r.tok("vector")
    r.barrier()
    ones128 = ar.bf16("ones128", [128, 128])
    x, t_x = load_fm_f32_t(ctx, "x2o_l", SC["x2o"])
    ofm, t_o = ofm_sb, t_ofm
    ar.alloc("woT", (NC_CH * DM + 1) // 2)
    woT = ar.bf16("woT", [128, NC_CH, DM])
    t_w = r.dma(woT, SC["woT"].ap().rearrange("(c p) n -> p c n", p=128))
    bo, t_b = load_col(ctx, "bo_l", IN["bo"])
    prev_ev = {"t": None}

    def consumer(oc, g0, gw, bank, te_tok):
        w0 = (te_tok, t_b, t_x) + ((prev_ev["t"],) if prev_ev["t"] else ())
        ev = r.v("scalar_tensor_tensor", x[:, oc, g0:g0 + gw], bank.ap[:, 0:gw],
                 bo[:, oc:oc + 1], x[:, oc, g0:g0 + gw], ALU.add, ALU.add,
                 wait=w0, inc=True)
        prev_ev["t"] = ev
        return ev

    gemm_fm(ctx, woT, ofm, NC_CH, consumer, wait_ops=(t_o, t_w))
    t_ps3 = r.tok("vector")
    r.barrier()
    ar.free("ofm_sb"); ar.free("woT"); ar.free("bo_l")
    r.barrier()
    t_ln3 = outer_ln(ctx, x, t_ps3, ones128, "ol3")
    store_fm(ctx, SC["x3o"], x, wait=(t_ln3,))
    r.barrier()
    ar.free("x2o_l")



def _round_f16(ctx, dst, src, f16tmp, wait=(), post=None):
    r = ctx.r
    t1 = r.v("tensor_scalar", f16tmp, src, 1536.0, None, ALU.add,
             wait=wait, inc=True)
    if post is None:
        return r.v("tensor_scalar", dst, f16tmp, 1536.0, None, ALU.subtract,
                   wait=(t1,), inc=True)
    s2, op1 = post
    return r.v("tensor_scalar", dst, f16tmp, 1536.0, s2, ALU.subtract, op1,
               wait=(t1,), inc=True)


def wq_pass1_half(ctx, dram, nrows_h, ncols, acc, t_acc):
    r, ar = ctx.r, ctx.ar
    ones128 = ar.bf16("ones128", [128, 128])
    wr = dram.ap().rearrange("(c p) n -> p c n", p=128)
    ncs = nrows_h // 128
    chunks = []
    for c0 in range(0, ncs, 8):
        cw = min(8, ncs - c0)
        for n0 in range(0, ncols, 1024):
            chunks.append((c0, cw, n0))
    stgs = [ar.f32("wstg0", [128, 8, 1024]), ar.f32("wstg1", [128, 8, 1024])]
    wabs = [ar.bf16("wabs0", [128, 8, 1024]), ar.bf16("wabs1", [128, 8, 1024])]
    t_ld = [None, None]
    pend = ctx.carry.pop("wq1_pend", {})
    t_ab = [pend.get(0), pend.get(1)]
    t_mm = [pend.get("m0"), pend.get("m1")]
    for i, (c0, cw, n0) in enumerate(chunks):
        par = i % 2
        t_ld[par] = r.dma(stgs[par][:, 0:cw, :],
                          wr[:, c0:c0 + cw, n0:n0 + 1024],
                          wait=((t_ab[par],) if t_ab[par] else ()))
        t_ab[par] = r.s("activation", wabs[par][:, 0:cw, :], stgs[par][:, 0:cw, :],
                        AF.Abs,
                        wait=(t_ld[par],) + ((t_mm[par],) if t_mm[par] else ()),
                        inc=True)
        bank = ctx.fbank()
        tok = None
        for ic in range(cw):
            for gg in range(2):
                first = ic == 0 and gg == 0
                w0 = ((((bank.free_tok,) if bank.free_tok else ()) + (t_ab[par],))
                      if first else ())
                tok = r.te("matmul", bank.ap[:, 0:512], ones128,
                           wabs[par][:, ic, 512 * gg:512 * gg + 512],
                           start=first, stop=(ic == cw - 1 and gg == 1),
                           wait=w0,
                           inc=(True if (ic == cw - 1 and gg == 1) else None))
        t_mm[par] = tok
        t_acc = r.v("tensor_tensor", acc, acc, bank.ap[:, 0:512], ALU.add,
                    wait=(tok, t_acc), inc=True)
        bank.free_tok = t_acc
    ctx.carry["wq1_pend"] = {0: t_ab[0], 1: t_ab[1],
                             "m0": t_mm[0], "m1": t_mm[1]}
    return t_acc


def wq_pass2(ctx, dram, rows, cols, out_sc, sinv, t_si):
    r, ar, SC = ctx.r, ctx.ar, ctx.SC
    ident = ar.bf16("ident", [128, 128])
    nrc = rows // 1024
    ncc = cols // 1024
    nit = nrc * ncc
    wr = dram.ap().rearrange("(c p) n -> p c n", p=128)
    out_r = SC[out_sc].ap().rearrange("(c p) n -> p c n", p=128)
    stgs = [ar.f32("wstg0", [128, NC_CH, 1024]), ar.f32("wstg1", [128, NC_CH, 1024])]
    w16 = ar.f16("wq16", [128, NC_CH, 1024])
    wqb = ar.bf16("wqb", [128, NC_CH, 1024])
    wts = [ar.bf16("wabs0", [128, NC_CH, 1024]), ar.bf16("wabs1", [128, NC_CH, 1024])]
    t_ld = [None, None]
    t_q3p = [None, None]
    t_q1 = None
    t_q3 = None
    last_tr = None
    t_ev = [None, None]
    pend = ctx.carry.pop("wq_pend", {})
    t_q3p[0] = pend.get(0)
    t_q3p[1] = pend.get(1)
    t_wdma = [pend.get("d0"), pend.get("d1")]
    for rc in range(nit):
        par = rc % 2
        rr, cc2 = rc // ncc, rc % ncc
        t_ld[par] = r.dma(stgs[par], wr[:, 8 * rr:8 * rr + 8,
                                       1024 * cc2:1024 * cc2 + 1024],
                          wait=((t_q3p[par],) if t_q3p[par] else ()))
        t_q1 = r.v("tensor_scalar", w16, stgs[par], sinv, 1536.0,
                   ALU.mult, ALU.add,
                   wait=(t_ld[par], t_si) + ((t_q3,) if t_q3 else ()), inc=True)
        t_q2 = r.v("tensor_scalar", stgs[par], w16, 1536.0, 1.0, ALU.subtract,
                   ALU.min, wait=(t_q1,), inc=True)
        t_q3 = r.v("tensor_scalar", wqb, stgs[par], -1.0, None, ALU.max,
                   wait=(t_q2,) + ((last_tr,) if last_tr else ()), inc=True)
        t_q3p[par] = t_q3
        wpar = rc % 2
        for ch in range(NC_CH):
            tb = ctx.tbank()
            tok2 = None
            for sub in range(NC_CH):
                w0 = ((tb.free_tok,) if tb.free_tok else ()) + \
                    ((t_q3,) if sub == 0 else ()) + \
                    ((t_wdma[wpar],) if (sub == 0 and ch == 0 and t_wdma[wpar])
                     else ())
                tok2 = r.te("transpose", tb.ap[:, 128 * sub:128 * sub + 128],
                            wqb[:, sub, 128 * ch:128 * ch + 128], ident,
                            wait=w0 if sub == 0 else (),
                            inc=(True if sub == NC_CH - 1 else None))
            t_ev[wpar] = r.v("tensor_copy", wts[wpar][:, ch, :], tb.ap,
                             wait=(tok2,) + ((t_wdma[wpar],) if (ch == 0 and
                                             t_wdma[wpar]) else ()), inc=True)
            tb.free_tok = t_ev[wpar]
            last_tr = tok2
        t_wdma[wpar] = r.dma(
            out_r[:, 8 * cc2:8 * cc2 + 8, 1024 * rr:1024 * rr + 1024],
            wts[wpar], wait=(t_ev[wpar],))
    ctx.carry["wq_pend"] = {0: t_q3p[0], 1: t_q3p[1],
                            "d0": t_wdma[0], "d1": t_wdma[1]}


def stage_s10a_wq(ctx):
    r, ar, IN, SC = ctx.r, ctx.ar, ctx.IN, ctx.SC
    ar.alloc("x3_l", NC_CH * FW, top=True)
    x3 = ar.f32("x3_l", [128, NC_CH, FW])
    t_x3 = r.dma(x3, SC["x3o"].ap().rearrange("(c p) t -> p c t", p=128))
    ctx.carry["x3_l"] = (x3, t_x3)
    ar.alloc("wacc", 512)
    acc3 = ar.f32("wacc", [128, 512])
    ar.alloc("mc3", 16)
    mc = ar.f32("mc3", [128, 16])
    ar.alloc("wstg0", NC_CH * 1024)
    ar.alloc("wstg1", NC_CH * 1024)
    ar.alloc("wabs0", (NC_CH * 1024 + 1) // 2)
    ar.alloc("wabs1", (NC_CH * 1024 + 1) // 2)
    specs = [("wg_h", 2 * DM, DM, "w_gate", 4 * DM, DM, "wgT", 0),
             ("wu_h", 2 * DM, DM, "w_up", 4 * DM, DM, "wuT", 1),
             ("wd_h", DM // 2, 4 * DM, "w_down", DM, 4 * DM, "wdT", 2)]
    t_red = None
    for i, (hname, hr, hc, fname, fr, fc, osc, mi) in enumerate(specs):
        t0 = r.v("memset", acc3, 0.0,
                 wait=((t_red,) if t_red else ()), inc=True)
        t_acc = wq_pass1_half(ctx, IN[hname], hr, hc, acc3, t0)
        t_red = r.v("tensor_reduce", mc[:, mi:mi + 1], acc3, AX.X, ALU.add,
                    wait=(t_acc,), inc=True)
    t_p1 = r.tok("vector")
    t_absd = r.tok("scalar")
    ctx.carry["wq_pend"] = {0: t_absd, 1: t_absd}
    ctx.carry.pop("wq1_pend", None)
    ar.free("wacc")
    t_st = r.dma(SC["cc2_in"].ap(), mc[:, 0:4], wait=(t_p1,))
    cc = r.sems["cc"]
    t_cc = r.op("gpsimd", "collective_compute", "AllReduce", ALU.add,
                replica_groups=[[0, 1], [2, 3], [4, 5], [6, 7]],
                ins=[SC["cc2_in"].ap().opt()], outs=[SC["cc2_out"].ap().opt()],
                wait=(t_st,), inc=cc)
    t_rd = r.dma(mc[:, 0:4], SC["cc2_out"].ap(), wait=(t_cc,))
    t_si = {}
    tprev = t_rd
    for i, (hname, hr, hc, fname, fr, fc, osc, mi) in enumerate(specs):
        t_m = r.v("tensor_scalar", mc[:, 4 + mi:5 + mi], mc[:, mi:mi + 1],
                  1.0 / (fr * fc), 1e-5, ALU.mult, ALU.max, wait=(tprev,),
                  inc=True)
        tprev = r.v("reciprocal", mc[:, 8 + mi:9 + mi], mc[:, 4 + mi:5 + mi],
                    wait=(t_m,), inc=True)
        t_si[mi] = tprev
        r.dma(SC["mw"].ap()[:, mi:mi + 1], mc[:, 4 + mi:5 + mi],
              wait=(t_m,), allow_slow_non_contiguous=True)
    ar.alloc("wq16", NC_CH * 512)
    ar.alloc("wqb", (NC_CH * 1024 + 1) // 2)
    wq_pass2(ctx, IN["w_gate"], 4 * DM, DM, "wgT", mc[:, 8:9], t_si[0])
    wq_pass2(ctx, IN["w_up"], 4 * DM, DM, "wuT", mc[:, 9:10], t_si[1])
    wq_pass2(ctx, IN["w_down"], DM, 4 * DM, "wdT", mc[:, 10:11], t_si[2])
    ctx.carry.pop("wq_pend", None)
    for n in ("wstg0", "wstg1", "mc3", "wq16", "wqb", "wabs0", "wabs1"):
        ar.free(n)
    r.barrier()


def stage_s10b_ffn(ctx):
    r, ar, IN, SC = ctx.r, ctx.ar, ctx.IN, ctx.SC
    ident = ar.bf16("ident", [128, 128])
    OWNW = OWN
    x3, t_x3 = ctx.carry.pop("x3_l")
    ar.alloc("hifm", (NC_CH * FW + 1) // 2)
    hi = ar.bf16("hifm", [128, NC_CH, FW])
    ar.alloc("lofm", (NC_CH * FW + 1) // 2)
    lo = ar.bf16("lofm", [128, NC_CH, FW])
    t_hi = r.v("tensor_copy", hi, x3, wait=(t_x3,), inc=True)
    t_lo = None
    for c in range(NC_CH):
        t_lo = r.v("tensor_tensor", lo[:, c, :], x3[:, c, :], hi[:, c, :],
                   ALU.subtract, wait=(t_hi,), inc=True)
    t_lo = r.tok("vector")
    ar.free("x3_l")
    r.barrier()
    ar.alloc("hitm", (NT * DM + 1) // 2)
    hitm = ar.bf16("hitm", [128, NT, DM])
    tzh = r.v("memset", hitm[:, NT - 1, :], 0.0, inc=True)
    t1 = _fm_to_tm(ctx, hitm, hi, ident, t_hi, extra_wait=(tzh,))
    ar.alloc("lotm", (NT * DM + 1) // 2)
    lotm = ar.bf16("lotm", [128, NT, DM])
    tzl = r.v("memset", lotm[:, NT - 1, :], 0.0, inc=True)
    t2 = _fm_to_tm(ctx, lotm, lo, ident, t_lo, extra_wait=(tzl,))
    ar.free("hifm"); ar.free("lofm")
    r.barrier()
    ar.alloc("xtall", NT * DM)
    xta = ar.f32("xtall", [128, NT, DM])
    ta = r.v("tensor_tensor", xta, hitm, lotm, ALU.add, wait=(t1, t2), inc=True)
    td = r.dma(SC["x3tm"].ap().rearrange("(t p) c -> p t c", p=128), xta,
               wait=(ta,))
    ar.free("hitm"); ar.free("lotm")
    ar.alloc("qcols", 4 * NT)
    qc = ar.f32("qcols", [128, 4 * NT])
    a0, a1 = qc[:, 0:NT], qc[:, NT:2 * NT]
    asc, axv = qc[:, 2 * NT:3 * NT], qc[:, 3 * NT:4 * NT]
    m1 = r.v("tensor_reduce", a0, xta, AX.X, ALU.max,
             apply_absolute_value=True, wait=(ta,), inc=True)
    m2 = r.v("tensor_scalar", a1, a0, 1e-5, None, ALU.max, wait=(m1,), inc=True)
    m3 = r.v("reciprocal", asc, a1, wait=(m2,), inc=True)
    m4 = r.v("tensor_scalar", asc, asc, 127.0, None, ALU.mult, wait=(m3,),
             inc=True)
    m5 = r.v("tensor_scalar", axv, a1, 1.0 / 127.0, None, ALU.mult,
             wait=(m2,), inc=True)
    r.barrier()
    ar.alloc("x16", (NT * DM + 1) // 2)
    x16 = ar.f16("x16", [128, NT, DM])
    ar.alloc("xqtm", (NT * DM + 1) // 2)
    xqa = ar.bf16("xqtm", [128, NT, DM])
    tq = None
    for t in range(NT):
        tq = r.v("tensor_scalar", x16[:, t, :], xta[:, t, :], asc[:, t:t + 1],
                 1536.0, ALU.mult, ALU.add, wait=(m4, td),
                 inc=(True if t == NT - 1 else None))
    t_xq = r.v("tensor_scalar", xqa, x16, 1536.0, None, ALU.subtract,
               wait=(tq,), inc=True)
    ar.free("xtall"); ar.free("x16")
    r.barrier()
    ar.alloc("xqfm", (NC_CH * OWNW + 1) // 2)
    xqf = ar.bf16("xqfm", [128, NC_CH, OWNW])
    ar.alloc("wgT_l", NC_CH * 2048)
    wg = ar.bf16("wgT_l", [128, NC_CH, 4 * DM])
    t_wg = r.dma(wg, SC["wgT"].ap().rearrange("(c p) n -> p c n", p=128))
    ar.alloc("wuT_l", NC_CH * 2048)
    wu = ar.bf16("wuT_l", [128, NC_CH, 4 * DM])
    t_wu = r.dma(wu, SC["wuT"].ap().rearrange("(c p) n -> p c n", p=128))
    ar.alloc("mw_l", 8)
    mw = ar.f32("mw_l", [128, 8])
    t_mw = r.dma(mw[:, 0:3], SC["mw"].ap()[:, 0:3],
                 allow_slow_non_contiguous=True)
    prevc = None
    for t in range(NT):
        nr = 128 if t < NT - 1 else OWN - 128 * t
        bank = ctx.tbank()
        tok = None
        for cc in range(NC_CH):
            w0 = ((bank.free_tok,) if bank.free_tok else ()) + ((t_xq,) if cc == 0 else ())
            tok = r.te("transpose", bank.ap[:, 128 * cc:128 * cc + 128],
                       xqa[:, t, 128 * cc:128 * cc + 128], ident,
                       wait=w0 if cc == 0 else (),
                       inc=(True if cc == NC_CH - 1 else None))
        prevc = r.v("tensor_copy",
                    xqf[:, :, 128 * t:128 * t + nr],
                    bank.ap.rearrange("p (c q) -> p c q", c=NC_CH)[:, :, 0:nr],
                    wait=(tok,), inc=True)
        bank.free_tok = prevc
    t_xqf = r.tok("vector")
    ar.free("xqtm")
    r.barrier()
    ar.alloc("scg", 2 * NT)
    scg = ar.f32("scg", [128, 2 * NT])
    t_s1 = r.v("tensor_scalar", scg[:, 0:NT], axv, mw[:, 0:1], None, ALU.mult,
               wait=(t_mw, m5), inc=True)
    t_s2 = r.v("tensor_scalar", scg[:, NT:2 * NT], axv, mw[:, 1:2], None,
               ALU.mult, wait=(t_s1,), inc=True)
    ar.alloc("ht0", 2048)
    ar.alloc("ht1", 2048)
    hts = [ar.bf16("ht0", [128, 4 * DM]), ar.bf16("ht1", [128, 4 * DM])]
    ar.alloc("sg0", 256)
    ar.alloc("sg1", 256)
    sgs = [ar.bf16("sg0", [128, 512]), ar.bf16("sg1", [128, 512])]
    ar.alloc("h16", 2048)
    h16 = ar.f16("h16", [128, 4 * DM])
    ar.alloc("hqt", 2048)
    hqt = ar.bf16("hqt", [128, 4 * DM])
    ar.alloc("hqc", 8)
    hqc = ar.f32("hqc", [128, 8])
    ar.alloc("ahinv", NT)
    ahv = ar.f32("ahinv", [128, NT])
    hq_r = SC["hq"].ap().rearrange("(t p) n -> p t n", p=128)
    prev_dma = None
    prev_q6 = None
    q5_done = {0: None, 1: None}
    sg_last = {0: None, 1: None}
    qa_last = None
    for t in range(NT):
        par = t % 2
        ht = hts[par]
        ntok = 128 if t < NT - 1 else OWN - 128 * t
        for hs in range(8):
            spar = hs % 2
            sgb = sgs[spar]
            bg = ctx.fbank()
            bu = ctx.fbank()
            tg = tu = None
            for ic in range(NC_CH):
                w0 = (((bg.free_tok,) if bg.free_tok else ()) +
                      ((t_xqf, t_wg) if ic == 0 else ()))
                tg = r.te("matmul", bg.ap[0:ntok, 0:512],
                          xqf[:, ic, 128 * t:128 * t + ntok],
                          wg[:, ic, 512 * hs:512 * hs + 512],
                          start=(ic == 0), stop=(ic == NC_CH - 1),
                          wait=w0, inc=(True if ic == NC_CH - 1 else None))
            for ic in range(NC_CH):
                w0 = (((bu.free_tok,) if bu.free_tok else ()) +
                      ((t_xqf, t_wu) if ic == 0 else ()))
                tu = r.te("matmul", bu.ap[0:ntok, 0:512],
                          xqf[:, ic, 128 * t:128 * t + ntok],
                          wu[:, ic, 512 * hs:512 * hs + 512],
                          start=(ic == 0), stop=(ic == NC_CH - 1),
                          wait=w0, inc=(True if ic == NC_CH - 1 else None))
            w1 = (tg, t_s1) + ((sg_last[spar],) if sg_last[spar] else ())
            e1 = r.s("activation", sgb, bg.ap[0:128, 0:512], AF.Silu,
                     scale=scg[:, t:t + 1], wait=w1, inc=True)
            bg.free_tok = e1
            w2 = (tu, e1, t_s2) + ((q5_done[par],) if q5_done[par] else ())
            e2 = r.v("scalar_tensor_tensor", ht[:, 512 * hs:512 * hs + 512],
                     bu.ap[0:128, 0:512], scg[:, NT + t:NT + t + 1], sgb,
                     ALU.mult, ALU.mult, wait=w2, inc=True)
            bu.free_tok = e2
            sg_last[spar] = e2
        t_h = r.tok("vector")
        cs = 4 * par
        wq1 = (t_h,) + ((qa_last,) if qa_last else ())
        q1 = r.v("tensor_reduce", hqc[:, cs:cs + 1], ht, AX.X, ALU.max,
                 apply_absolute_value=True, wait=wq1, inc=True)
        q2 = r.v("tensor_scalar", hqc[:, cs + 1:cs + 2], hqc[:, cs:cs + 1],
                 1e-5, None, ALU.max, wait=(q1,), inc=True)
        q3 = r.v("reciprocal", hqc[:, cs + 2:cs + 3], hqc[:, cs + 1:cs + 2],
                 wait=(q2,), inc=True)
        q4 = r.v("tensor_scalar", hqc[:, cs + 3:cs + 4], hqc[:, cs + 2:cs + 3],
                 127.0, None, ALU.mult, wait=(q3,), inc=True)
        qa_last = r.v("tensor_scalar", ahv[:, t:t + 1], hqc[:, cs + 1:cs + 2],
                      1.0 / 127.0, None, ALU.mult, wait=(q2,), inc=True)
        w5 = (q4,) + ((prev_q6,) if prev_q6 else ())
        q5 = r.v("tensor_scalar", h16, ht, hqc[:, cs + 3:cs + 4], 1536.0,
                 ALU.mult, ALU.add, wait=w5, inc=True)
        q5_done[par] = q5
        w6 = (q5,) + ((prev_dma,) if prev_dma else ())
        prev_q6 = r.v("tensor_scalar", hqt, h16, 1536.0, None, ALU.subtract,
                      wait=w6, inc=True)
        prev_dma = r.dma(hq_r[:, t, :], hqt, wait=(prev_q6,))
    t_ah = r.tok("vector")
    r.barrier()
    for n in ("wgT_l", "wuT_l", "scg", "ht0", "ht1", "sg0", "sg1", "h16",
              "hqt", "hqc", "xqfm", "qcols"):
        ar.free(n)
    r.barrier()
    ar.alloc("wdT_l", 16 * 1024)
    wd = ar.bf16("wdT_l", [128, 32, DM])
    t_wd = r.dma(wd, SC["wdT"].ap().rearrange("(c p) n -> p c n", p=128))
    ar.alloc("scd", NT)
    scd = ar.f32("scd", [128, NT])
    t_sd = r.v("tensor_scalar", scd, ahv, mw[:, 2:3], None, ALU.mult,
               wait=(t_ah, t_mw), inc=True)
    ar.free("ahinv")
    ar.alloc("hq_t0", 2048)
    ar.alloc("hq_t1", 2048)
    hqLs = [ar.bf16("hq_t0", [128, 4 * DM]), ar.bf16("hq_t1", [128, 4 * DM])]
    ar.alloc("hqT0", 16 * 128)
    ar.alloc("hqT1", 16 * 128)
    hqTs = [ar.bf16("hqT0", [128, 32, 128]), ar.bf16("hqT1", [128, 32, 128])]
    ar.alloc("x3t0", DM)
    ar.alloc("x3t1", DM)
    ar.alloc("x3t2", DM)
    x3ts = [ar.f32("x3t0", [128, DM]), ar.f32("x3t1", [128, DM]),
            ar.f32("x3t2", [128, DM])]
    ar.alloc("pst0", DM)
    ar.alloc("pst1", DM)
    ar.alloc("pst2", DM)
    psts = [ar.f32("pst0", [128, DM]), ar.f32("pst1", [128, DM]),
            ar.f32("pst2", [128, DM])]
    ar.alloc("sqt", DM)
    sqt = ar.f32("sqt", [128, DM])
    ar.alloc("lncol", 24)
    lnc_all = ar.f32("lncol", [128, 24])
    x3tm_r = SC["x3tm"].ap().rearrange("(t p) c -> p t c", p=128)
    y_r = ctx.y.ap().rearrange("(t p) c -> p t c", p=128)
    prev_y = {0: None, 1: None, 2: None}
    hql_free = {0: (t_sd,), 1: (t_sd,)}
    hqT_free = {0: None, 1: None}
    x3t_free = {0: (), 1: (), 2: ()}
    ly_prev = {0: None, 1: None, 2: None}
    prev_l1 = None
    for t in range(NT):
        par = t % 2
        p3 = t % 3
        hqL, hqT, x3t, pst = hqLs[par], hqTs[par], x3ts[p3], psts[p3]
        lnc = lnc_all[:, 8 * p3:8 * p3 + 8]
        ntok = 128 if t < NT - 1 else OWN - 128 * t
        t_h = r.dma(hqL, hq_r[:, t, :], wait=hql_free[par])
        t_x = r.dma(x3t, x3tm_r[:, t, :], wait=x3t_free[p3])
        t_tr = None
        for half in range(4):
            bank = ctx.tbank()
            tok = None
            for sub in range(8):
                hc = 8 * half + sub
                w0 = ((bank.free_tok,) if bank.free_tok else ()) + ((t_h,) if sub == 0 else ())
                tok = r.te("transpose", bank.ap[:, 128 * sub:128 * sub + 128],
                           hqL[:, 128 * hc:128 * hc + 128], ident,
                           wait=w0 if sub == 0 else (),
                           inc=(True if sub == 7 else None))
            ev = r.v("tensor_copy", hqT[:, 8 * half:8 * half + 8, :],
                     bank.ap.rearrange("p (c q) -> p c q", c=NC_CH),
                     wait=(tok,), inc=True)
            bank.free_tok = ev
        t_tr = r.tok("vector")
        hql_free[par] = (t_tr,)
        evs = []
        last_mm = None
        for fs in range(2):
            bank = ctx.fbank()
            tok = None
            for hc in range(32):
                w0 = (((bank.free_tok,) if bank.free_tok else ()) +
                      ((t_tr, t_wd) if hc == 0 else ()))
                tok = r.te("matmul", bank.ap[0:ntok, 0:512],
                           hqT[:, hc, 0:ntok],
                           wd[:, hc, 512 * fs:512 * fs + 512],
                           start=(hc == 0), stop=(hc == 31),
                           wait=w0, inc=(True if hc == 31 else None))
            last_mm = tok
            wev = (tok, t_x, t_sd) + ((prev_y[p3],) if prev_y[p3] else ())
            ev = r.v("scalar_tensor_tensor", pst[:, 512 * fs:512 * fs + 512],
                     bank.ap[0:128, 0:512], scd[:, t:t + 1],
                     x3t[:, 512 * fs:512 * fs + 512], ALU.mult, ALU.add,
                     wait=wev, inc=True)
            evs.append(ev)
            bank.free_tok = ev
        x3t_free[p3] = (evs[-1],)
        hqT_free[par] = (last_mm,)
        wl1 = tuple(evs) + ((prev_l1,) if prev_l1 else ()) + \
            ((ly_prev[p3],) if ly_prev[p3] else ())
        l1 = r.s("activation", sqt, pst, AF.Square, accum_out=lnc[:, 0:1],
                 wait=wl1, inc=True)
        prev_l1 = l1
        wl2 = tuple(evs) + ((ly_prev[p3],) if ly_prev[p3] else ())
        l2 = r.v("tensor_reduce", lnc[:, 1:2], pst, AX.X, ALU.add,
                 wait=wl2, inc=True)
        l3 = r.v("tensor_scalar", lnc[:, 2:3], lnc[:, 1:2], 1.0 / DM, None,
                 ALU.mult, wait=(l2,), inc=True)
        l4 = r.v("tensor_scalar", lnc[:, 3:4], lnc[:, 2:3], lnc[:, 2:3], EPS,
                 ALU.mult, ALU.subtract, wait=(l3,), inc=True)
        l5 = r.v("scalar_tensor_tensor", lnc[:, 4:5], lnc[:, 0:1], 1.0 / DM,
                 lnc[:, 3:4], ALU.mult, ALU.subtract, wait=(l1, l4), inc=True)
        l6a = r.v("reciprocal", lnc[:, 5:6], lnc[:, 4:5], wait=(l5,), inc=True)
        l6 = r.s("activation", lnc[:, 6:7], lnc[:, 5:6], AF.Sqrt,
                 wait=(l6a,), inc=True)
        l7 = r.v("tensor_scalar", lnc[:, 7:8], lnc[:, 2:3], lnc[:, 6:7], -1.0,
                 ALU.mult, ALU.mult, wait=(l6,), inc=True)
        ly = r.s("activation", pst, pst, AF.Identity, scale=lnc[:, 6:7],
                 bias=lnc[:, 7:8], wait=(l7,), inc=True)
        ly_prev[p3] = ly
        prev_y[p3] = r.dma(y_r[:, t, :], pst, wait=(ly,))
    r.barrier()



_PROG_CACHE = {}


def _get_prog(stop_stage, taps):
    key = (stop_stage, taps)
    if key not in _PROG_CACHE:
        _PROG_CACHE[key] = build_program(stop_stage, taps)
    return _PROG_CACHE[key]


_PERM1 = np.zeros(1024, np.int64)
for _c in range(8):
    for _j in range(64):
        _PERM1[128 * _c + _j] = 128 * _c + 2 * _j
        _PERM1[128 * _c + 64 + _j] = 128 * _c + 2 * _j + 1
_PERM2 = np.concatenate([_PERM1, _PERM1 + 1024])


def _colvec(v):
    v = np.asarray(v, np.float32).reshape(-1)
    n = v.shape[0] // 128
    return np.ascontiguousarray(v.reshape(n, 128).T)


def _colmat(v):
    v = np.asarray(v, np.float32)
    v = v.reshape(v.shape[0] // 128, 128, -1)
    return np.ascontiguousarray(v.transpose(1, 0, 2))


def _atr_mask():
    import ml_dtypes
    ind = np.zeros((128, 96), np.float32)
    for j in range(96):
        ind[(j // 48) * 3 + (j % 3), j] = 1.0
    mm = np.zeros((128, 4 * 96), np.float32)
    mm[:, 0:96] = ind * float(2 ** 30)
    mm[:, 96:192] = ind
    mm[0, 192:288] = -float(2 ** 30)
    mm[0, 288:384] = 1.0
    return np.ascontiguousarray(mm.astype(ml_dtypes.bfloat16))


_ATR_MASK = None


def make_inmaps(inputs):
    import ml_dtypes
    global _ATR_MASK
    if _ATR_MASK is None:
        _ATR_MASK = _atr_mask()
    x = np.asarray(inputs["L0_attn_x"])
    in_maps = []
    for c in range(8):
        s, h = c // 2, c % 2
        start = OWN * h
        lo, hi = start - HALO, start + OWN + HALO
        shard = np.zeros((FW, DM), np.float32)
        a, b = max(lo, 0), min(hi, 4096)
        shard[a - lo:a - lo + (b - a)] = x[s, a:b]
        n_real = 4096 - start if h == 1 else OWN
        n_real = min(n_real, OWN)
        mrow = np.zeros((FW,), np.float32)
        mrow[HALO:HALO + n_real] = 1.0
        mask_full = np.broadcast_to(mrow, (128, FW)).astype(ml_dtypes.bfloat16)
        mact = np.zeros((NT * 128,), np.float32)
        mact[:n_real] = 1.0
        mask_tm = mact.reshape(NT, 128).T.copy()
        m = {
            "xshard": shard,
            "mask_full_bf": np.ascontiguousarray(mask_full),
            "mask_tm_bf": mask_tm.astype(ml_dtypes.bfloat16),
            "mask_tm_f32": mask_tm,
            "ascale": np.full((128, 1), np.asarray(inputs["L0_ffn_act_scale"]),
                              np.float32),
            "mska": _ATR_MASK,
            "cpe1_w": _colmat(np.asarray(inputs["L0_attn_cpe1_w"]).reshape(DM, 3)),
            "cpe1_b": _colvec(inputs["L0_attn_cpe1_b"]),
            "norm1_g": _colvec(inputs["L0_attn_norm1_g"]),
            "norm1_b": _colvec(inputs["L0_attn_norm1_b"]),
            "in_w": np.asarray(inputs["L0_attn_in_w"]),
            "in_b": _colvec(inputs["L0_attn_in_b"]),
            "actp_w": np.asarray(inputs["L0_attn_actp_w"]),
            "actp_b": _colvec(inputs["L0_attn_actp_b"]),
            "dwc_w": _colmat(np.asarray(inputs["L0_attn_dwc_w"]).reshape(DM, 3)),
            "dwc_b": _colvec(inputs["L0_attn_dwc_b"]),
            "qk_w": np.asarray(inputs["L0_attn_qk_w"]),
            "qk_b": _colvec(np.asarray(inputs["L0_attn_qk_b"])[_PERM2]),
            "lepe_w": _colmat(np.asarray(inputs["L0_attn_lepe_w"]).reshape(DM, 3)),
            "lepe_b": _colvec(inputs["L0_attn_lepe_b"]),
            "out_w": np.asarray(inputs["L0_attn_out_w"]),
            "out_b": _colvec(inputs["L0_attn_out_b"]),
            "cpe2_w": _colmat(np.asarray(inputs["L0_attn_cpe2_w"]).reshape(DM, 3)),
            "cpe2_b": _colvec(inputs["L0_attn_cpe2_b"]),
            "norm2_g": _colvec(inputs["L0_attn_norm2_g"]),
            "norm2_b": _colvec(inputs["L0_attn_norm2_b"]),
            "fc1_w": np.asarray(inputs["L0_attn_fc1_w"]),
            "fc1_b": _colvec(inputs["L0_attn_fc1_b"]),
            "fc2_w": np.asarray(inputs["L0_attn_fc2_w"]),
            "fc2_b": _colvec(inputs["L0_attn_fc2_b"]),
            "tern": np.asarray(inputs["L0_ffn_ternary_weight"]),
            "ffn_bias": _colvec(inputs["L0_ffn_bias"]),
            "Wq": np.asarray(inputs["L1_attn_Wq"]), "bq": _colvec(inputs["L1_attn_bq"]),
            "Wk": np.asarray(inputs["L1_attn_Wk"]), "bk": _colvec(inputs["L1_attn_bk"]),
            "Wv": np.asarray(inputs["L1_attn_Wv"]), "bv": _colvec(inputs["L1_attn_bv"]),
            "Wo": np.asarray(inputs["L1_attn_Wo"]), "bo": _colvec(inputs["L1_attn_bo"]),
            "w_gate": np.asarray(inputs["L1_ffn_w_gate"]),
            "w_up": np.asarray(inputs["L1_ffn_w_up"]),
            "w_down": np.asarray(inputs["L1_ffn_w_down"]),
            "wg_h": np.ascontiguousarray(
                np.asarray(inputs["L1_ffn_w_gate"])[2048 * h:2048 * (h + 1)]),
            "wu_h": np.ascontiguousarray(
                np.asarray(inputs["L1_ffn_w_up"])[2048 * h:2048 * (h + 1)]),
            "wd_h": np.ascontiguousarray(
                np.asarray(inputs["L1_ffn_w_down"])[512 * h:512 * (h + 1)]),
        }
        in_maps.append(m)
    return in_maps


def kernel(**inputs):
    stop = os.environ.get("K_STOP_STAGE") or None
    taps = tuple(t for t in os.environ.get("K_TAPS", "").split(",") if t)
    nc = _get_prog(stop, taps)
    in_maps = make_inmaps(inputs)
    res = run_bass_kernel_spmd(nc, in_maps, core_ids=list(range(8)))
    out = np.zeros((4, 4096, DM), np.float32)
    for c in range(8):
        s, h = c // 2, c % 2
        start = OWN * h
        n_real = min(4096 - start, OWN)
        y = res.results[c]["y"]
        out[s, start:start + n_real] = y[0:n_real]
    if taps or stop:
        kernel.last_results = res
    return out


# version 23
# speedup vs baseline: 1.0660x; 1.0660x over previous
import os
import contextlib
import numpy as np
import concourse.bass as bass
import concourse.mybir as mybir
from concourse.bass_utils import run_bass_kernel_spmd

F32 = mybir.dt.float32
F16 = mybir.dt.float16
BF16 = mybir.dt.bfloat16
I32 = mybir.dt.int32
AF = mybir.ActivationFunctionType
ALU = mybir.AluOpType
AX = mybir.AxisListType

OWN = 2064
HALO = 4
WIN = OWN + 2 * HALO
FW = 2176
NT = 17
NC_CH = 8
DM = 1024
GRPS = [(0, 512), (512, 512), (1024, 512), (1536, 512), (2048, 128)]
EPS = 1e-5

ENGS = ("sync", "tensor", "vector", "scalar", "gpsimd")


class Sem:
    __slots__ = ("name", "handle", "count", "last_tok")

    def __init__(self, name, handle):
        self.name = name
        self.handle = handle
        self.count = 0
        self.last_tok = None


class Tok:
    __slots__ = ("sem", "val")

    def __init__(self, sem, val):
        self.sem = sem
        self.val = val


class Arena:

    def __init__(self, handle, n):
        self.h = handle
        self.n = n
        self.blocks = {}
        self.free_list = [(0, n)]
        self.peak = 0

    def alloc(self, name, nf32, top=False):
        nf32 = (nf32 + 1) & ~1
        assert name not in self.blocks
        it = enumerate(self.free_list)
        if top:
            it = reversed(list(it))
        for i, (off, sz) in it:
            if sz >= nf32:
                if top:
                    self.blocks[name] = (off + sz - nf32, nf32)
                    if sz == nf32:
                        self.free_list.pop(i)
                    else:
                        self.free_list[i] = (off, sz - nf32)
                else:
                    self.blocks[name] = (off, nf32)
                    if sz == nf32:
                        self.free_list.pop(i)
                    else:
                        self.free_list[i] = (off + nf32, sz - nf32)
                used = sum(s for (_, s) in self.blocks.values())
                self.peak = max(self.peak, used)
                return self.blocks[name][0]
        raise MemoryError(f"SBUF arena OOM allocating {name} ({nf32} f32); "
                          f"live={ {k: v[1] for k, v in self.blocks.items()} }")

    def free(self, name):
        off, sz = self.blocks.pop(name)
        if not hasattr(self, "pending"):
            self.pending = []
        self.pending.append((off, sz))

    def flush_frees(self):
        for b in getattr(self, "pending", []):
            self.free_list.append(b)
        self.pending = []
        self.free_list.sort()
        out = []
        for b in self.free_list:
            if out and out[-1][0] + out[-1][1] == b[0]:
                out[-1] = (out[-1][0], out[-1][1] + b[1])
            else:
                out.append(list(b))
        self.free_list = [tuple(b) for b in out]

    def f32(self, name, shape):
        off, sz = self.blocks[name]
        n = int(np.prod(shape[1:]))
        assert n <= sz, (name, shape, sz)
        ap = self.h[:, off:off + n]
        if len(shape) == 2:
            return ap
        return ap.rearrange("p (a b) -> p a b", b=shape[-1]) if len(shape) == 3 else \
            ap.rearrange("p (a b c) -> p a b c", b=shape[-2], c=shape[-1])

    def bf16(self, name, shape):
        off, sz = self.blocks[name]
        n = int(np.prod(shape[1:]))
        assert n <= 2 * sz, (name, shape, sz)
        ap = self.h[:, off:off + (n + 1) // 2].bitcast(BF16)[:, 0:n]
        if len(shape) == 2:
            return ap
        return ap.rearrange("p (a b) -> p a b", b=shape[-1]) if len(shape) == 3 else \
            ap.rearrange("p (a b c) -> p a b c", b=shape[-2], c=shape[-1])

    def f16(self, name, shape):
        off, sz = self.blocks[name]
        n = int(np.prod(shape[1:]))
        assert n <= 2 * sz
        ap = self.h[:, off:off + (n + 1) // 2].bitcast(F16)[:, 0:n]
        if len(shape) == 2:
            return ap
        return ap.rearrange("p (a b) -> p a b", b=shape[-1])

    def i32(self, name, shape):
        off, sz = self.blocks[name]
        n = int(np.prod(shape[1:]))
        assert n <= sz
        ap = self.h[:, off:off + n].bitcast(I32)
        if len(shape) == 2:
            return ap
        return ap.rearrange("p (a b) -> p a b", b=shape[-1])


class Bank:
    __slots__ = ("ap", "free_tok", "idx")

    def __init__(self, ap, idx):
        self.ap = ap
        self.free_tok = None
        self.idx = idx


class R:

    def __init__(self, nc, sems):
        self.nc = nc
        self.q = {e: [] for e in ENGS}
        self.water = {e: {} for e in ENGS}
        self.sems = {}
        for name, handle in sems.items():
            self.sems[name] = Sem(name, handle)
        self.prog = {e: self.sems["p_" + e] for e in ENGS}
        allp = [v for k, v in sorted(self.sems.items()) if k.startswith("tok")]
        self.pool = {"tensor": allp[0:10], "vector": allp[10:20],
                     "scalar": allp[20:26], "gpsimd": allp[26:28]}
        self.dpool = {"sync": allp[28:52], "gpsimd": allp[52:64],
                      "scalar": allp[52:64]}
        self.pool_i = {e: 0 for e in self.pool}
        self.dpool_i = {"sync": 0, "gpsimd": 0, "scalar": 0}
        self.opcount = {e: 0 for e in ENGS}
        self.last_barrier = {e: 0 for e in ENGS}
        self.last_incs = {e: None for e in ENGS}
        self.last_tok = {e: (None, -1) for e in ENGS}
        self.dma_toks = []

    def _next_tok_sem(self, eng):
        p = self.pool[eng]
        s = p[self.pool_i[eng] % len(p)]
        self.pool_i[eng] += 1
        return s

    def _next_dma_sem(self, eng="sync"):
        p = self.dpool[eng]
        s = p[self.dpool_i[eng] % len(p)]
        self.dpool_i[eng] += 1
        return s

    def _wait(self, eng, tok):
        if tok is None:
            return
        w = self.water[eng]
        if w.get(tok.sem.name, -1) >= tok.val:
            return
        w[tok.sem.name] = tok.val
        h, v = tok.sem.handle, tok.val

        def fn(e):
            e.wait_ge(h, v)
        self.q[eng].append(fn)

    def op(self, eng, method, *args, wait=(), inc=None, inc_k=1, **kw):
        for t in wait:
            self._wait(eng, t)
        tok = None
        incs = []
        if inc is not None:
            sem = self._next_tok_sem(eng) if inc is True else inc
            sem.count += inc_k
            tok = Tok(sem, sem.count)
            incs.append((sem.handle, inc_k))

        import traceback as _tb
        site = "".join(_tb.format_stack(limit=4)[:-2])

        def fn(e):
            try:
                ins = getattr(e, method)(*args, **kw)
            except Exception:
                print("EMIT FAIL:", eng, method, "recorded at:\n", site)
                raise
            for (h, k) in incs:
                ins.then_inc(h, k)
        self.q[eng].append(fn)
        self.opcount[eng] += 1
        if method != "nop":
            self.last_incs[eng] = incs
            if tok is not None:
                self.last_tok[eng] = (tok, self.opcount[eng])
        return tok

    def tok(self, eng):
        lt, at = self.last_tok[eng]
        if lt is not None and at == self.opcount[eng]:
            return lt
        incs = self.last_incs[eng]
        assert incs is not None, f"no instruction to attach token on {eng}"
        sem = self._next_tok_sem(eng)
        sem.count += 1
        incs.append((sem.handle, 1))
        t = Tok(sem, sem.count)
        self.last_tok[eng] = (t, self.opcount[eng])
        return t

    def dma(self, out, in_, eng="sync", wait=(), **kw):
        sem = self._next_dma_sem(eng)
        for t in wait:
            self._wait(eng, t)
        prev = sem.last_tok
        if prev is not None:
            self._wait(eng, prev)
        sem.count += 16
        tok = Tok(sem, sem.count)
        incs = [(sem.handle, 16)]

        def fn(e):
            ins = e.dma_start(out=out, in_=in_, **kw)
            for (h, k) in incs:
                ins.then_inc(h, k)
        self.q[eng].append(fn)
        self.opcount[eng] += 1
        self.dma_toks.append(tok)
        sem.last_tok = tok
        self.last_incs[eng] = incs
        self.last_tok[eng] = (tok, self.opcount[eng])
        return tok

    def dmat(self, out, in_, eng="sync", wait=()):
        sem = self._next_dma_sem(eng)
        for t in wait:
            self._wait(eng, t)
        prev = sem.last_tok
        if prev is not None:
            self._wait(eng, prev)
        sem.count += 16
        tok = Tok(sem, sem.count)
        incs = [(sem.handle, 16)]

        def fn(e):
            ins = e.dma_start_transpose(out, in_)
            for (h, k) in incs:
                ins.then_inc(h, k)
        self.q[eng].append(fn)
        self.opcount[eng] += 1
        self.dma_toks.append(tok)
        sem.last_tok = tok
        self.last_incs[eng] = incs
        self.last_tok[eng] = (tok, self.opcount[eng])
        return tok

    def barrier(self):
        for hook in getattr(self, "hooks", []):
            hook()
        toks = []
        for e in ENGS:
            if self.opcount[e] > self.last_barrier[e] and self.last_incs[e]:
                toks.append((e, self.tok(e)))
        for e in ENGS:
            for (src, t) in toks:
                self._wait(e, t)
            for t in self.dma_toks:
                self._wait(e, t)
            self.last_barrier[e] = self.opcount[e]
        self.dma_toks = []

    def te(self, m, *a, **k):
        return self.op("tensor", m, *a, **k)

    def v(self, m, *a, **k):
        return self.op("vector", m, *a, **k)

    def s(self, m, *a, **k):
        return self.op("scalar", m, *a, **k)

    def g(self, m, *a, **k):
        return self.op("gpsimd", m, *a, **k)

    def emit_into(self, block):
        for ename in ENGS:
            fns = self.q[ename]
            if not fns:
                continue
            dec = getattr(block, "sync" if ename == "sync" else ename)

            def make(fns):
                def body(e):
                    for f in fns:
                        f(e)
                return body
            dec(make(fns))



def build_program(stop_stage=None, taps=()):
    nc = bass.Bass("TRN2", target_bir_lowering=False)

    IN = {}

    def inp(name, shape, dtype=F32):
        IN[name] = nc.dram_tensor(name, list(shape), dtype, kind="ExternalInput")
        return IN[name]

    inp("xshard", [FW, DM])
    inp("mask_full_bf", [128, FW], BF16)
    inp("mask_tm_bf", [128, NT], BF16)
    inp("mask_tm_f32", [128, NT])
    inp("ascale", [128, 1])
    inp("mska", [128, 4 * 96], BF16)
    inp("cpe1_w", [128, NC_CH, 3]); inp("cpe1_b", [128, NC_CH])
    inp("norm1_g", [128, NC_CH]); inp("norm1_b", [128, NC_CH])
    inp("in_w", [DM, DM]); inp("in_b", [128, NC_CH])
    inp("actp_w", [DM, DM]); inp("actp_b", [128, NC_CH])
    inp("dwc_w", [128, NC_CH, 3]); inp("dwc_b", [128, NC_CH])
    inp("qk_w", [DM, 2 * DM]); inp("qk_b", [128, 16])
    inp("lepe_w", [128, NC_CH, 3]); inp("lepe_b", [128, NC_CH])
    inp("out_w", [DM, DM]); inp("out_b", [128, NC_CH])
    inp("cpe2_w", [128, NC_CH, 3]); inp("cpe2_b", [128, NC_CH])
    inp("norm2_g", [128, NC_CH]); inp("norm2_b", [128, NC_CH])
    inp("fc1_w", [DM, 4 * DM]); inp("fc1_b", [128, 32])
    inp("fc2_w", [4 * DM, DM]); inp("fc2_b", [128, NC_CH])
    inp("tern", [DM, DM], I32); inp("ffn_bias", [128, NC_CH])
    for w in ("Wq", "Wk", "Wv", "Wo"):
        inp(w, [DM, DM])
    for b in ("bq", "bk", "bv", "bo"):
        inp(b, [128, NC_CH])
    inp("w_gate", [4 * DM, DM]); inp("w_up", [4 * DM, DM]); inp("w_down", [DM, 4 * DM])
    inp("wg_h", [2 * DM, DM]); inp("wu_h", [2 * DM, DM]); inp("wd_h", [DM // 2, 4 * DM])

    y_out = nc.dram_tensor("y", [FW, DM], F32, kind="ExternalOutput")

    SC = {}

    def scr(name, shape, dtype=F32):
        SC[name] = nc.dram_tensor("sc_" + name, list(shape), dtype)
        return SC[name]

    scr("x0fm", [DM, FW])
    scr("c1", [DM, FW])
    scr("xn", [DM, FW], BF16)
    scr("ares", [DM, FW], BF16)
    scr("xi", [DM, FW], BF16)
    scr("qel", [DM, FW], BF16)
    scr("krtm", [FW, DM], BF16)
    scr("vtm", [FW, DM], BF16)
    scr("kvred", [128, NC_CH * 132])
    scr("sel2", [NC_CH * 128, 128], BF16)
    scr("ksumsc", [128, NC_CH])
    scr("zrep", [DM, FW], BF16)
    scr("rintsc", [DM, FW], BF16)
    scr("woT", [DM, DM], BF16)
    scr("mw", [128, 8])
    scr("xqfm_sc", [DM, FW], BF16)
    scr("axinv", [128, NT])
    scr("ahinv", [128, NT])
    scr("kvbf", [NC_CH * 128, 132], BF16)
    scr("qrope", [DM, FW], BF16)
    scr("cc_in", [128, NC_CH * 132])
    scr("cc_out", [128, NC_CH * 132])
    scr("cc2_in", [128, 4])
    scr("cc2_out", [128, 4])
    scr("top", [DM, FW], BF16)
    scr("x1", [DM, FW])
    scr("x2", [DM, FW])
    scr("xn2", [DM, FW], BF16)
    scr("ps1", [DM, FW])
    scr("x1o", [DM, FW])
    scr("ps2", [DM, FW])
    scr("x2o", [DM, FW])
    scr("x2ob", [DM, FW], BF16)
    scr("qfm", [DM, FW], BF16)
    scr("kfm", [DM, FW], BF16)
    scr("vfm", [DM, FW], BF16)
    scr("vtml", [FW, DM], BF16)
    scr("vtma", [128 * 22, DM], BF16)
    scr("ofm", [DM, FW], BF16)
    scr("ps3", [DM, FW])
    scr("x3o", [DM, FW])
    scr("x3tm", [FW, DM])
    scr("xqfm", [DM, FW], BF16)
    scr("hq", [FW, 4 * DM], BF16)
    scr("wgT", [DM, 4 * DM], BF16)
    scr("wuT", [DM, 4 * DM], BF16)
    scr("wdT", [4 * DM, DM], BF16)
    scr("ps4tm", [FW, DM])

    taps_out = {}
    for t in taps:
        src = SC[t]
        taps_out[t] = nc.dram_tensor("tap_" + t, list(src.shape),
                                     src.dtype, kind="ExternalOutput")

    stack = contextlib.ExitStack()
    pool_sems = {}
    with (
        nc.sbuf_tensor("arena", [128, 52224], F32) as arena_h,
        nc.psum_tensor("pb0", [128, 512], F32) as pb0,
        nc.psum_tensor("pb1", [128, 512], F32) as pb1,
        nc.psum_tensor("pb2", [128, 512], F32) as pb2,
        nc.psum_tensor("pb3", [128, 512], F32) as pb3,
        nc.psum_tensor("pb4", [128, 512], F32) as pb4,
        nc.psum_tensor("pb5", [128, 512], F32) as pb5,
        nc.psum_tensor("pt0", [128, 1024], BF16) as pt0,
        nc.psum_tensor("pt1", [128, 1024], BF16) as pt1,
        nc.semaphore("s_dma") as h_dma,
        nc.semaphore("s_dmag") as h_dmag,
        nc.semaphore("s_pt") as h_pt,
        nc.semaphore("s_pv") as h_pv,
        nc.semaphore("s_ps") as h_ps,
        nc.semaphore("s_pg") as h_pg,
        nc.semaphore("s_pq") as h_pq,
        nc.semaphore("s_cc") as h_cc,
        nc.Block() as block,
    ):
        for i in range(64):
            pool_sems["tok%02d" % i] = stack.enter_context(nc.semaphore("tk%02d" % i))
        sems = {
            "dma_sync": h_dma, "dma_gpsimd": h_dmag,
            "p_tensor": h_pt, "p_vector": h_pv, "p_scalar": h_ps,
            "p_gpsimd": h_pg, "p_sync": h_pq, "cc": h_cc,
        }
        sems.update(pool_sems)
        r = R(nc, sems)
        ar = Arena(arena_h, 52224)
        r.hooks = [ar.flush_frees]
        fbanks = [Bank(b[:], i) for i, b in enumerate([pb0, pb1, pb2, pb3, pb4, pb5])]
        tbanks = [Bank(b[:], i) for i, b in enumerate([pt0, pt1])]

        ctx = Ctx(nc, r, ar, fbanks, tbanks, IN, SC, y_out)
        emit_stages(ctx, stop_stage)

        r.barrier()
        toks = []
        for t, dst in taps_out.items():
            toks.append(r.dma(dst.ap(), SC[t].ap()))
        for t in toks:
            r._wait("sync", t)

        r.emit_into(block)
        stack.close()
    return nc


class Ctx:
    def __init__(self, nc, r, ar, fbanks, tbanks, IN, SC, y_out):
        self.carry = {}
        self.nc = nc
        self.r = r
        self.ar = ar
        self.fbanks = fbanks
        self.tbanks = tbanks
        self.IN = IN
        self.SC = SC
        self.y = y_out
        self.fb_i = 0
        self.tb_i = 0

    def fbank(self):
        b = self.fbanks[self.fb_i % len(self.fbanks)]
        self.fb_i += 1
        return b

    def tbank(self):
        b = self.tbanks[self.tb_i % len(self.tbanks)]
        self.tb_i += 1
        return b



def load_fm_f32(ctx, dst_ap3, dram, ncols=FW):
    src = dram.ap().rearrange("(c p) t -> p c t", p=128)
    return ctx.r.dma(dst_ap3, src)


def store_fm(ctx, dram, src_ap3, wait=()):
    dst = dram.ap().rearrange("(c p) t -> p c t", p=128)
    return ctx.r.dma(dst, src_ap3, wait=wait, eng="gpsimd")


def load_wtiles(ctx, name, dram, rows, cols, dtype=BF16, scale_col=None,
                col_ap=None, sub1=False):
    r, ar = ctx.r, ctx.ar
    nch = rows // 128
    ar.alloc(name + "_stg", nch * cols)
    stg = ar.f32(name + "_stg", [128, nch, cols]) if dram.dtype == F32 else \
        ar.i32(name + "_stg", [128, nch, cols])
    src = dram.ap().rearrange("(c p) n -> p c n", p=128)
    if col_ap is not None:
        src = col_ap
    t_ld = r.dma(stg, src)
    ar.alloc(name, (nch * cols + 1) // 2)
    wb = ar.bf16(name, [128, nch, cols])
    if sub1:
        tok = r.v("tensor_scalar", wb, stg, 1.0, None, ALU.subtract,
                  wait=(t_ld,), inc=True)
    elif scale_col is not None:
        tok = r.v("tensor_scalar", wb, stg, scale_col, None, ALU.mult,
                  wait=(t_ld,), inc=True)
    else:
        tok = r.v("tensor_copy", wb, stg, wait=(t_ld,), inc=True)
    ar.free(name + "_stg")
    return wb, tok


def load_col(ctx, name, dram):
    r, ar = ctx.r, ctx.ar
    shape = list(dram.shape)
    n = int(np.prod(shape[1:]))
    if dram.dtype == BF16:
        ar.alloc(name, (n + 1) // 2)
        dst = ar.bf16(name, [128] + shape[1:])
    else:
        ar.alloc(name, n)
        dst = ar.f32(name, [128] + shape[1:])
    tok = r.dma(dst, dram.ap())
    return dst, tok


def gemm_fm(ctx, w_ap3, x_ap3_bf, nin, psum_consumer, wait_ops=(),
            groups=GRPS, nout=8):
    r = ctx.r
    for oc in range(nout):
        for (g0, gw) in groups:
            bank = ctx.fbank()
            tok = None
            for ic in range(nin):
                first = ic == 0
                tok = r.te("matmul", bank.ap[:, 0:gw],
                           w_ap3[:, ic, 128 * oc:128 * oc + 128],
                           x_ap3_bf[:, ic, g0:g0 + gw],
                           start=first, stop=(ic == nin - 1),
                           wait=(((bank.free_tok,) if bank.free_tok else ())
                                 + tuple(wait_ops)) if first else (),
                           inc=(True if ic == nin - 1 else None))
            free_tok = psum_consumer(oc, g0, gw, bank, tok)
            bank.free_tok = free_tok


def conv3_fm(ctx, out_bf3, in_ap3, w_cols, b_col, w1_plus1_col=None,
             in_is_bf=True, wait=(), silu_after=False, out_f32=None):
    r, ar = ctx.r, ctx.ar
    ar.alloc("conv_u", FW + 2)
    ub = ar.bf16("conv_u", [128, 2, FW])
    dst0 = out_bf3 if out_f32 is None else out_f32
    r.v("memset", dst0[:, :, 0:1], 0.0, inc=True)
    r.v("memset", dst0[:, :, FW - 1:FW], 0.0, inc=True)
    toks = []
    for c in range(NC_CH):
        par = c % 2
        u = ub[:, par:par + 1, :]
        w0 = w_cols[:, c, 0:1]
        w1 = w1_plus1_col[:, c:c + 1] if w1_plus1_col is not None else w_cols[:, c, 1:2]
        w2 = w_cols[:, c, 2:3]
        war = (toks[c - 2],) if c >= 2 else ()
        t0 = r.s("activation", u[:, 0, 1:FW], in_ap3[:, c, 0:FW - 1], AF.Identity,
                 bias=b_col[:, c:c + 1], scale=w0, wait=wait + war, inc=True)
        t1 = r.v("scalar_tensor_tensor", u[:, 0, 1:FW], in_ap3[:, c, 1:FW], w1,
                 u[:, 0, 1:FW], ALU.mult, ALU.add, wait=(t0,), inc=True)
        dst = out_bf3 if out_f32 is None else out_f32
        t2 = r.v("scalar_tensor_tensor", dst[:, c, 1:FW - 1], in_ap3[:, c, 2:FW], w2,
                 u[:, 0, 1:FW - 1], ALU.mult, ALU.add, wait=(t1,), inc=True)
        toks.append(t2)
        if silu_after:
            toks[-1] = r.s("activation", dst[:, c, 1:FW - 1], dst[:, c, 1:FW - 1],
                           AF.Silu, wait=(t2,), inc=True)
    ar.free("conv_u")
    return toks


def ln_fm(ctx, x_ap3_f32, sq_src_bf=None, ones_bf=None, name="ln",
          x_bf_for_sums=None, wait=()):
    r, ar = ctx.r, ctx.ar
    tmp_names = []
    if x_bf_for_sums is None:
        ar.alloc(name + "_xb", (NC_CH * FW + 1) // 2)
        x_bf_for_sums = ar.bf16(name + "_xb", [128, NC_CH, FW])
        t_cast = r.v("tensor_copy", x_bf_for_sums, x_ap3_f32, wait=wait, inc=True)
        tmp_names.append(name + "_xb")
    else:
        t_cast = None
    ar.alloc(name + "_sq", (NC_CH * FW + 1) // 2)
    sq = ar.bf16(name + "_sq", [128, NC_CH, FW])
    t_sq = r.s("activation", sq, x_ap3_f32, AF.Square, wait=wait, inc=True)
    tmp_names.append(name + "_sq")

    ar.alloc(name + "_r", FW)
    ar.alloc(name + "_mu", FW)
    r_full = ctx.ar.f32(name + "_r", [128, FW])
    mu_full = ctx.ar.f32(name + "_mu", [128, FW])
    ar.alloc(name + "_t", 512)
    ttile = ctx.ar.f32(name + "_t", [128, 512])

    last = None
    for (g0, gw) in GRPS:
        b_s = ctx.fbank()
        b_q = ctx.fbank()
        for ic in range(NC_CH):
            w = (b_s.free_tok, t_cast) if ic == 0 else ()
            r.te("matmul", b_s.ap[:, 0:gw], ones_bf, x_bf_for_sums[:, ic, g0:g0 + gw],
                 start=(ic == 0), stop=(ic == NC_CH - 1), wait=w)
        t_s = r.tok("tensor")
        for ic in range(NC_CH):
            w = (b_q.free_tok, t_sq) if ic == 0 else ()
            r.te("matmul", b_q.ap[:, 0:gw], ones_bf, sq[:, ic, g0:g0 + gw],
                 start=(ic == 0), stop=(ic == NC_CH - 1), wait=w)
        t_q = r.tok("tensor")
        t_mu = r.s("activation", mu_full[:, g0:g0 + gw], b_s.ap[:, 0:gw],
                   AF.Identity, scale=1.0 / DM, wait=(t_s,), inc=True)
        t_m2 = r.s("activation", ttile[:, 0:gw], b_s.ap[:, 0:gw],
                   AF.Square, scale=1.0 / DM,
                   wait=(last,) if last is not None else (), inc=True)
        t_var = r.v("scalar_tensor_tensor", ttile[:, 0:gw], b_q.ap[:, 0:gw],
                    1.0 / DM, ttile[:, 0:gw], ALU.mult, ALU.subtract,
                    wait=(t_q, t_m2), inc=True)
        b_s.free_tok = t_var
        b_q.free_tok = t_var
        t_e = r.v("tensor_scalar", ttile[:, 0:gw], ttile[:, 0:gw], EPS, None,
                  ALU.add, wait=(t_var,), inc=True)
        t_rc = r.v("reciprocal", ttile[:, 0:gw], ttile[:, 0:gw], wait=(t_e,), inc=True)
        last = r.s("activation", r_full[:, g0:g0 + gw], ttile[:, 0:gw], AF.Sqrt,
                   wait=(t_rc,), inc=True)
    ar.free(name + "_t")
    for t in tmp_names:
        ar.free(t)
    return r_full, mu_full, last



def emit_stages(ctx, stop_stage):
    order = ["prep", "s1_cpe1", "s2_actp", "s2b_in", "s3_qk", "s4_kv",
             "s5a_z", "s5b_attn", "s6_mlp", "s7_ffn0", "s8_l1qkv", "s9_l1attn",
             "s10a_wq", "s10b_ffn"]
    for st in order:
        globals()["stage_" + st](ctx)
        ctx.r.barrier()
        if stop_stage == st:
            return


def stage_prep(ctx):
    r, ar, IN = ctx.r, ctx.ar, ctx.IN
    r.v("memset", ar.h[:, 0:26112], 0.0, inc=True)
    r.g("memset", ar.h[:, 26112:52224], 0.0, inc=True)
    r.barrier()
    ar.alloc("ident", 64)
    ar.alloc("ones128", 64)
    ar.alloc("iota", 128)
    ar.alloc("pid", 1)
    ident = ar.bf16("ident", [128, 128])
    ones128 = ar.bf16("ones128", [128, 128])
    iota = ar.f32("iota", [128, 128])
    pid = ar.f32("pid", [128, 1])
    r.g("iota", iota, pattern=[[1, 128]], base=0, channel_multiplier=0,
        allow_small_or_imprecise_dtypes=True, inc=True)
    r.g("iota", pid, pattern=[[0, 1]], base=0, channel_multiplier=1,
        allow_small_or_imprecise_dtypes=True, inc=True)
    ar.alloc("rope", 4 * 4)
    rope = ar.f32("rope", [128, 16])
    cos_t, sin_t = rope[:, 0:4], rope[:, 4:8]
    ar.alloc("ropei", 4)
    ri = ar.f32("ropei", [128, 4])
    r.g("iota", ri, pattern=[[128, 4]], base=0, channel_multiplier=1,
        allow_small_or_imprecise_dtypes=True, inc=True)
    r.barrier()
    r.v("tensor_scalar", ident, iota, pid, None, ALU.is_equal, inc=True)
    r.v("memset", ones128, 1.0, inc=True)
    t_th = r.s("activation", rope[:, 8:12], ri, AF.Exp,
               scale=-float(np.log(10000.0) / 512.0), inc=True)
    ar.alloc("halfpi", 1)
    hp = ar.f32("halfpi", [128, 1])
    t_hp = r.v("memset", hp, float(np.pi / 2), inc=True)
    t_sn = r.s("activation", sin_t, rope[:, 8:12], AF.Sin, wait=(t_th,), inc=True)
    t_cs = r.s("activation", cos_t, rope[:, 8:12], AF.Sin, scale=-1.0, bias=hp,
               wait=(t_th, t_hp), inc=True)
    ar.alloc("rope2", 16)
    rope2 = ar.f32("rope2", [128, 16])
    c2v = rope2[:, 0:8].rearrange("p (a b) -> p a b", b=2)
    s2v = rope2[:, 8:16].rearrange("p (a b) -> p a b", b=2)
    prev = (t_sn, t_cs)
    for par in range(2):
        for half in range(2):
            tA = r.v("tensor_copy", c2v[64 * half:64 * half + 64, :, par],
                     cos_t[64 * par:64 * par + 64, 0:4], wait=prev, inc=True)
            tB = r.v("tensor_copy", s2v[64 * half:64 * half + 64, :, par],
                     sin_t[64 * par:64 * par + 64, 0:4], wait=(tA,), inc=True)
            prev = (tB,)
    ar.free("ropei")
    ar.free("halfpi")


def stage_s1_cpe1(ctx):
    r, ar, IN, SC = ctx.r, ctx.ar, ctx.IN, ctx.SC
    ident = ar.bf16("ident", [128, 128])
    ones128 = ar.bf16("ones128", [128, 128])

    ar.alloc("x0tm", NT * DM)
    x0tm = ar.f32("x0tm", [128, NT, DM])
    t_x0 = r.dma(x0tm, IN["xshard"].ap().rearrange("(t p) c -> p t c", p=128))
    ar.alloc("hitm", (NT * DM + 1) // 2)
    ar.alloc("lotm", (NT * DM + 1) // 2)
    hi = ar.bf16("hitm", [128, NT, DM])
    lo = ar.bf16("lotm", [128, NT, DM])
    t_hi = r.v("tensor_copy", hi, x0tm, wait=(t_x0,), inc=True)
    t_lo = r.v("tensor_tensor", lo, x0tm, hi, ALU.subtract, wait=(t_hi,), inc=True)
    ar.free("x0tm")
    r.barrier()

    ar.alloc("x0fm", NC_CH * FW)
    x0fm = ar.f32("x0fm", [128, NC_CH, FW])
    for tile in range(NT):
        bank = ctx.tbank()
        toks = []
        for cc in range(NC_CH):
            w = (bank.free_tok, t_hi) if cc == 0 else ()
            toks.append(r.te("transpose", bank.ap[:, 128 * cc:128 * cc + 128],
                             hi[:, tile, 128 * cc:128 * cc + 128], ident,
                             wait=w, inc=(True if cc == NC_CH - 1 else None)))
        t_tr = toks[-1]
        t_ev = r.v("tensor_copy",
                   x0fm[:, :, 128 * tile:128 * tile + 128],
                   bank.ap.rearrange("p (c t) -> p c t", c=NC_CH),
                   wait=(t_tr,), inc=True)
        bank.free_tok = t_ev
    for tile in range(NT):
        bank = ctx.tbank()
        toks = []
        for cc in range(NC_CH):
            w = (bank.free_tok, t_lo) if cc == 0 else ()
            toks.append(r.te("transpose", bank.ap[:, 128 * cc:128 * cc + 128],
                             lo[:, tile, 128 * cc:128 * cc + 128], ident,
                             wait=w, inc=(True if cc == NC_CH - 1 else None)))
        t_tr = toks[-1]
        t_ev = r.v("tensor_tensor",
                   x0fm[:, :, 128 * tile:128 * tile + 128],
                   x0fm[:, :, 128 * tile:128 * tile + 128],
                   bank.ap.rearrange("p (c t) -> p c t", c=NC_CH),
                   ALU.add, wait=(t_tr,), inc=True)
        bank.free_tok = t_ev
    ar.free("hitm")
    ar.free("lotm")
    t_x0fm = r.tok("vector")
    t_sp = store_fm(ctx, SC["x0fm"], x0fm, wait=(t_x0fm,))
    r.barrier()

    cpe1w, t_w = load_col(ctx, "cpe1w", IN["cpe1_w"])
    cpe1b, t_b = load_col(ctx, "cpe1b", IN["cpe1_b"])
    ar.alloc("cpe1w1p", NC_CH)
    w1p = ar.f32("cpe1w1p", [128, NC_CH])
    t_w1p = r.v("tensor_scalar", w1p, cpe1w[:, :, 1], 1.0, None, ALU.add,
                wait=(t_w,), inc=True)

    ar.alloc("c1", NC_CH * FW)
    c1 = ar.f32("c1", [128, NC_CH, FW])
    conv_toks = conv3_fm(ctx, None, x0fm, cpe1w, cpe1b, w1_plus1_col=w1p,
                         wait=(t_x0fm, t_w, t_b, t_w1p), out_f32=c1)
    t_c1 = r.tok("vector")
    t_c1s = store_fm(ctx, SC["c1"], c1, wait=(t_c1,))
    ar.free("x0fm")
    ar.free("cpe1w"); ar.free("cpe1b"); ar.free("cpe1w1p")
    r.barrier()

    rfull, mufull, t_stats = ln_fm(ctx, c1, ones_bf=ones128, name="n1",
                                   wait=(t_c1,))
    r.barrier()
    n1g, t_g = load_col(ctx, "n1g", IN["norm1_g"])
    n1b, t_bb = load_col(ctx, "n1b", IN["norm1_b"])
    ar.alloc("xn_t", NC_CH * FW)
    xt = ar.f32("xn_t", [128, NC_CH, FW])
    ar.alloc("xn", (NC_CH * FW + 1) // 2, top=True)
    xn = ar.bf16("xn", [128, NC_CH, FW])
    mub = mufull.unsqueeze(1).broadcast_to((128, NC_CH, FW))
    rb = rfull.unsqueeze(1).broadcast_to((128, NC_CH, FW))
    t1 = r.v("tensor_tensor", xt, c1, mub, ALU.subtract,
             wait=(t_stats, t_c1), inc=True)
    t2 = r.v("tensor_tensor", xt, xt, rb, ALU.mult, wait=(t1,), inc=True)
    for c in range(NC_CH):
        r.s("activation", xn[:, c, :], xt[:, c, :], AF.Identity,
            scale=n1g[:, c:c + 1], bias=n1b[:, c:c + 1],
            wait=(t2, t_g, t_bb), inc=True)
    t_xn = r.tok("scalar")
    ctx.carry["xn"] = (xn, t_xn)
    ar.free("xn_t")
    ar.free("c1")
    ar.free("n1_r"); ar.free("n1_mu")
    ar.free("n1g"); ar.free("n1b")



def load_fm_bf(ctx, name, dram, wait=()):
    r, ar = ctx.r, ctx.ar
    ar.alloc(name, (NC_CH * FW + 1) // 2)
    dst = ar.bf16(name, [128, NC_CH, FW])
    tok = r.dma(dst, dram.ap().rearrange("(c p) t -> p c t", p=128), wait=wait)
    return dst, tok


def load_fm_f32_t(ctx, name, dram, wait=()):
    r, ar = ctx.r, ctx.ar
    ar.alloc(name, NC_CH * FW)
    dst = ar.f32(name, [128, NC_CH, FW])
    tok = r.dma(dst, dram.ap().rearrange("(c p) t -> p c t", p=128), wait=wait)
    return dst, tok


def stage_s2_actp(ctx):
    r, ar, IN, SC = ctx.r, ctx.ar, ctx.IN, ctx.SC
    xn, t_xn = ctx.carry["xn"]
    ar.alloc("qk_stg", (NC_CH * FW + 1) // 2, top=True)
    stg = ar.f32("qk_stg", [128, NC_CH, 1024])
    t_ld = r.dma(stg, IN["qk_w"].ap().rearrange("(c p) n -> p c n", p=128)[:, :, 0:1024])
    ctx.carry["qk_stg"] = (stg, t_ld)
    w, t_w = load_wtiles(ctx, "actpw", IN["actp_w"], DM, DM)
    b, t_b = load_col(ctx, "actpb", IN["actp_b"])
    ar.alloc("ares", (NC_CH * FW + 1) // 2)
    ares = ar.bf16("ares", [128, NC_CH, FW])

    def consumer(oc, g0, gw, bank, te_tok):
        return r.s("activation", ares[:, oc, g0:g0 + gw], bank.ap[:, 0:gw],
                   AF.Silu, bias=b[:, oc:oc + 1], wait=(te_tok, t_b), inc=True)

    gemm_fm(ctx, w, xn, NC_CH, consumer, wait_ops=(t_xn, t_w))
    t_done = r.tok("scalar")
    store_fm(ctx, SC["ares"], ares, wait=(t_done,))
    for n in ("actpw", "actpb", "ares"):
        ar.free(n)


def stage_s2b_in(ctx):
    r, ar, IN, SC = ctx.r, ctx.ar, ctx.IN, ctx.SC
    xn, t_xn = ctx.carry.pop("xn")
    w, t_w = load_wtiles(ctx, "inw", IN["in_w"], DM, DM)
    b, t_b = load_col(ctx, "inb", IN["in_b"])
    dwcw, t_dw = load_col(ctx, "dwcw", IN["dwc_w"])
    dwcb, t_db = load_col(ctx, "dwcb", IN["dwc_b"])
    ar.alloc("mrow", (FW + 1) // 2)
    mv = ar.bf16("mrow", [128, FW])
    t_m = r.dma(mv, IN["mask_full_bf"].ap())
    ar.alloc("xipre", (NC_CH * FW + 1) // 2)
    xipre = ar.bf16("xipre", [128, NC_CH, FW])
    evac_toks = []

    def consumer(oc, g0, gw, bank, te_tok):
        t = r.s("activation", xipre[:, oc, g0:g0 + gw], bank.ap[:, 0:gw],
                AF.Identity, bias=b[:, oc:oc + 1], wait=(te_tok, t_b), inc=True)
        evac_toks.append(t)
        return t

    gemm_fm(ctx, w, xn, NC_CH, consumer, wait_ops=(t_xn, t_w))
    t_pre = r.tok("scalar")
    VLO, VHI = HALO + OWN - 32, HALO + OWN
    t_fix = t_pre
    for c in range(NC_CH):
        t_fix = r.v("tensor_tensor", xipre[:, c, VLO:VHI], xipre[:, c, VLO:VHI],
                    mv[:, VLO:VHI], ALU.mult, wait=(t_pre, t_m), inc=True)
    ar.alloc("xi", (NC_CH * FW + 1) // 2, top=True)
    xi = ar.bf16("xi", [128, NC_CH, FW])
    toks = conv3_fm(ctx, xi, xipre, dwcw, dwcb, wait=(t_fix, t_dw, t_db),
                    silu_after=True)
    t_xi = r.tok("scalar")
    ctx.carry["xi"] = (xi, t_xi)
    for n in ("xn", "inw", "inb", "dwcw", "dwcb", "mrow", "xipre"):
        ar.free(n)


def _rope_fm(ctx, out_bf, in_bf, cos_t, sin_t, t_in, name, eng="vector"):
    r, ar = ctx.r, ctx.ar
    ar.alloc(name + "_t", FW)
    tmp = ar.f32(name + "_t", [128, FW])
    toks = []
    prev = None
    for cc in range(NC_CH):
        cos0 = cos_t[0:64, cc:cc + 1]
        cos64 = cos_t[64:128, cc:cc + 1]
        sin0 = sin_t[0:64, cc:cc + 1]
        sin64 = sin_t[64:128, cc:cc + 1]
        e = in_bf[0:64, cc, :]
        o = in_bf[64:128, cc, :]
        if eng == "vector":
            t1 = r.v("tensor_scalar", tmp[0:64, :], o, sin64, None, ALU.mult,
                     wait=(t_in,) + ((prev,) if prev else ()), inc=True)
            t2 = r.v("scalar_tensor_tensor", out_bf[0:64, cc, :], e, cos0,
                     tmp[0:64, :], ALU.mult, ALU.subtract, wait=(t1,), inc=True)
            t3 = r.v("tensor_scalar", tmp[64:128, :], e, sin0, None, ALU.mult,
                     wait=(t2,), inc=True)
            t4 = r.v("scalar_tensor_tensor", out_bf[64:128, cc, :], o, cos64,
                     tmp[64:128, :], ALU.mult, ALU.add, wait=(t3,), inc=True)
        else:
            W = in_bf.shape[-1]
            t1 = r.op(eng, "tensor_tensor", tmp[0:64, :], o,
                      sin64.broadcast_to((64, W)), ALU.mult,
                      wait=(t_in,) + ((prev,) if prev else ()), inc=True)
            ta = r.op(eng, "tensor_tensor", out_bf[0:64, cc, :], e,
                      cos0.broadcast_to((64, W)), ALU.mult, wait=(t1,), inc=True)
            t2 = r.op(eng, "tensor_tensor", out_bf[0:64, cc, :],
                      out_bf[0:64, cc, :], tmp[0:64, :], ALU.subtract,
                      wait=(ta,), inc=True)
            t3 = r.op(eng, "tensor_tensor", tmp[64:128, :], e,
                      sin0.broadcast_to((64, W)), ALU.mult, wait=(t2,), inc=True)
            tb = r.op(eng, "tensor_tensor", out_bf[64:128, cc, :], o,
                      cos64.broadcast_to((64, W)), ALU.mult, wait=(t3,), inc=True)
            t4 = r.op(eng, "tensor_tensor", out_bf[64:128, cc, :],
                      out_bf[64:128, cc, :], tmp[64:128, :], ALU.add,
                      wait=(tb,), inc=True)
        prev = t4
        toks.append(t4)
    ar.free(name + "_t")
    return toks


def _fm_to_tm(ctx, tm_dst_bf, fm_src_bf, ident, t_src, mask_col=None, extra_wait=()):
    r = ctx.r
    toks = []
    for t in range(NT):
        col0 = HALO + 128 * t
        roff = 0
        if col0 + 128 > FW:
            roff = 32
            col0 = HALO + 128 * t - 32
        bank = ctx.tbank()
        tok = None
        for cc in range(NC_CH):
            w = ((bank.free_tok,) if bank.free_tok else ()) + ((t_src,) if cc == 0 else ())
            tok = r.te("transpose", bank.ap[:, 128 * cc:128 * cc + 128],
                       fm_src_bf[:, cc, col0:col0 + 128], ident,
                       wait=w if cc == 0 else (),
                       inc=(True if cc == NC_CH - 1 else None))
        nrows = 128 - roff if t < NT - 1 else OWN - 128 * t
        src = bank.ap.rearrange("p (c t2) -> p c t2", c=NC_CH)[roff:roff + nrows, :, :]
        dst = tm_dst_bf[0:nrows, t, :].rearrange("p (c t2) -> p c t2", c=NC_CH)
        if mask_col is not None:
            ev = r.v("tensor_scalar", dst, src, mask_col[0:nrows, t:t + 1], None,
                     ALU.mult, wait=(tok,) + tuple(extra_wait), inc=True)
        else:
            ev = r.v("tensor_copy", dst, src, wait=(tok,) + tuple(extra_wait), inc=True)
        bank.free_tok = ev
    return r.tok("vector")


def stage_s3_qk(ctx):
    r, ar, IN, SC = ctx.r, ctx.ar, ctx.IN, ctx.SC
    ident = ar.bf16("ident", [128, 128])
    rope2 = ar.f32("rope2", [128, 16])
    cos_t, sin_t = rope2[:, 0:8], rope2[:, 8:16]
    xi, t_xi = ctx.carry["xi"]
    qkb, t_qb = load_col(ctx, "qkb", IN["qk_b"])
    mtm, t_mtm = load_col(ctx, "mtm", IN["mask_tm_f32"])

    stg, t_ld = ctx.carry.pop("qk_stg")
    ar.alloc("qk_stg2", (NC_CH * FW + 1) // 2)
    stg2 = ar.f32("qk_stg2", [128, NC_CH, 1024])
    t_ld2 = r.dma(stg2, IN["qk_w"].ap().rearrange("(c p) n -> p c n", p=128)[:, :, 1024:2048])
    ar.alloc("qkw", NC_CH * 2048 // 2)
    wqk = ar.bf16("qkw", [128, NC_CH, 2048])
    t_w = None
    for oc2 in range(16):
        half, oc2h = (stg, oc2) if oc2 < 8 else (stg2, oc2 - 8)
        psrc = half[:, :, 128 * oc2h:128 * oc2h + 128].rearrange(
            "p c (j par) -> p c par j", par=2)
        pdst = wqk[:, :, 128 * oc2:128 * oc2 + 128].rearrange(
            "p c (par j) -> p c par j", par=2)
        t_w = r.v("tensor_copy", pdst, psrc,
                  wait=(t_ld,) if oc2 < 8 else (t_ld2,), inc=True)
    t_w = r.tok("vector")
    ar.free("qk_stg"); ar.free("qk_stg2")
    r.barrier()

    ar.alloc("tmpA0", 512)
    ar.alloc("tmpA1", 512)
    ar.alloc("tmpB0", 512)
    ar.alloc("tmpB1", 512)
    tmpAs = [ar.f32("tmpA0", [128, 512]), ar.f32("tmpA1", [128, 512])]
    tmpBs = [ar.f32("tmpB0", [128, 512]), ar.f32("tmpB1", [128, 512])]

    prev = {"p4": {0: None, 1: None}, "i": 0}

    def elu_consumer(out_bf, boff):
        def consumer(oc, g0, gw, bank, te_tok):
            par = prev["i"] % 2
            prev["i"] += 1
            tmpA, tmpB = tmpAs[par], tmpBs[par]
            lastp = prev["p4"][par]
            bcol = qkb[:, boff + oc:boff + oc + 1]
            w1 = (te_tok, t_qb) + ((lastp,) if lastp else ())
            p1 = r.v("tensor_scalar", tmpA[:, 0:gw], bank.ap[:, 0:gw], bcol, 0.0,
                     ALU.add, ALU.min, wait=w1, inc=True)
            p2 = r.s("activation", tmpB[:, 0:gw], tmpA[:, 0:gw], AF.Exp,
                     wait=(p1,) + ((lastp,) if lastp else ()), inc=True)
            p3 = r.v("tensor_scalar", tmpA[:, 0:gw], bank.ap[:, 0:gw], bcol, 0.0,
                     ALU.add, ALU.max, wait=(p2,), inc=True)
            p4 = r.v("tensor_tensor", out_bf[:, oc, g0:g0 + gw], tmpB[:, 0:gw],
                     tmpA[:, 0:gw], ALU.add, wait=(p3,), inc=True)
            prev["p4"][par] = p4
            return p3
        return consumer

    ar.alloc("qel", (NC_CH * FW + 1) // 2)
    qel = ar.bf16("qel", [128, NC_CH, FW])
    gemm_fm(ctx, wqk[:, :, 0:1024], xi, NC_CH, elu_consumer(qel, 0),
            wait_ops=(t_xi, t_w))
    t_q = r.tok("vector")
    ar.alloc("qrope", (NC_CH * FW + 1) // 2, top=True)
    qrope = ar.bf16("qrope", [128, NC_CH, FW])
    toks = _rope_fm(ctx, qrope, qel, cos_t, sin_t, t_q, "rq")
    t_qr = r.tok("vector")
    ctx.carry["qrope"] = (qrope, t_qr)
    r.barrier()
    ar.free("qel")
    r.barrier()

    ar.alloc("kel", (NC_CH * FW + 1) // 2)
    kel = ar.bf16("kel", [128, NC_CH, FW])
    gemm_fm(ctx, wqk[:, :, 1024:2048], xi, NC_CH, elu_consumer(kel, 8),
            wait_ops=(t_xi, t_w))
    t_k = r.tok("vector")
    ar.alloc("krope", (NC_CH * FW + 1) // 2)
    krope = ar.bf16("krope", [128, NC_CH, FW])
    toks = _rope_fm(ctx, krope, kel, cos_t, sin_t, t_k, "rk")
    t_kr = r.tok("vector")
    r.barrier()
    ar.free("kel"); ar.free("qkw")
    ar.free("tmpA0"); ar.free("tmpB0")
    ar.free("tmpA1"); ar.free("tmpB1")
    r.barrier()

    ar.alloc("krtm", (NT * DM + 1) // 2, top=True)
    krtm = ar.bf16("krtm", [128, NT, DM])
    tzk = r.v("memset", krtm[:, NT - 1, :], 0.0, inc=True)
    t1 = _fm_to_tm(ctx, krtm, krope, ident, t_kr, extra_wait=(tzk,))
    ctx.carry["krtm"] = (krtm, t1)
    ar.alloc("vtm", (NT * DM + 1) // 2, top=True)
    vtm = ar.bf16("vtm", [128, NT, DM])
    tzv = r.v("memset", vtm[:, NT - 1, :], 0.0, inc=True)
    t2 = _fm_to_tm(ctx, vtm, xi, ident, t_xi, mask_col=mtm, extra_wait=(tzv,))
    ctx.carry["vtm"] = (vtm, t2)
    r.barrier()
    for n in ("qkb", "mtm", "krope"):
        ar.free(n)


def stage_s4_kv(ctx):
    r, ar, IN, SC = ctx.r, ctx.ar, ctx.IN, ctx.SC
    krtm, t_k = ctx.carry.pop("krtm")
    vtm, t_v = ctx.carry.pop("vtm")
    mtmb, t_m = load_col(ctx, "mtmb", IN["mask_tm_bf"])
    ar.alloc("kvsb", 128 * NC_CH + 64)
    kvsb = ar.f32("kvsb", [128, NC_CH, 132])

    for hp in range(NC_CH):
        bank = ctx.fbank()
        tok = None
        for t in range(NT):
            w0 = ((bank.free_tok,) if bank.free_tok else ()) + (t_k, t_v, t_m)
            r.te("matmul", bank.ap[:, 0:128],
                 krtm[:, t, 128 * hp:128 * hp + 128],
                 vtm[:, t, 128 * hp:128 * hp + 128],
                 start=(t == 0), stop=False,
                 wait=w0 if t == 0 else ())
            tok = r.te("matmul", bank.ap[:, 128:129],
                       krtm[:, t, 128 * hp:128 * hp + 128],
                       mtmb[:, t:t + 1],
                       start=False, stop=(t == NT - 1),
                       inc=(True if t == NT - 1 else None))
        ev = r.v("tensor_copy", kvsb[:, hp, 0:129], bank.ap[:, 0:129],
                 wait=(tok,), inc=True)
        bank.free_tok = ev
    t_kv = r.tok("vector")
    t_st = r.dma(SC["cc_in"].ap().rearrange("p (c n) -> p c n", n=132),
                 kvsb, wait=(t_kv,))
    cc = r.sems["cc"]
    t_cc = r.op("gpsimd", "collective_compute", "AllReduce", ALU.add,
                replica_groups=[[0, 1], [2, 3], [4, 5], [6, 7]],
                ins=[SC["cc_in"].ap().opt()], outs=[SC["cc_out"].ap().opt()],
                wait=(t_st,), inc=cc)
    t_rd = r.dma(kvsb, SC["cc_out"].ap().rearrange("p (c n) -> p c n", n=132),
                 wait=(t_cc,))
    ar.alloc("kvbf", (NC_CH * 132 + 1) // 2, top=True)
    kvbf = ar.bf16("kvbf", [128, NC_CH, 132])
    t_cast = r.v("tensor_copy", kvbf, kvsb, wait=(t_rd,), inc=True)
    ctx.carry["kvbf"] = (kvbf, t_cast)
    ar.alloc("sel2", NC_CH * 64, top=True)
    sel2 = ar.bf16("sel2", [128, NC_CH, 128])
    tz = r.v("memset", sel2, 0.0, wait=(t_rd,), inc=True)
    stoks = []
    for cc_i in range(NC_CH):
        for m in range(2):
            srcap = kvsb[64 * m:64 * m + 64, cc_i, 128:129].broadcast_to((64, 64))
            stoks.append(r.v("tensor_copy",
                             sel2[64 * m:64 * m + 64, cc_i, 64 * m:64 * m + 64],
                             srcap, wait=(tz,), inc=True))
    t_sel = r.tok("vector")
    ctx.carry["sel2"] = (sel2, t_sel)
    r.barrier()
    for n in ("krtm", "vtm", "mtmb", "kvsb"):
        ar.free(n)



def stage_s5a_z(ctx):
    r, ar, IN, SC = ctx.r, ctx.ar, ctx.IN, ctx.SC
    qel, t_q = ctx.carry["qrope"]
    sel2, t_s = ctx.carry.pop("sel2")
    ar.alloc("epsrow", 64)
    epsl = ar.bf16("epsrow", [128, 128])
    ar.alloc("onesr", FW // 2 + 1)
    onesr = ar.bf16("onesr", [128, FW])
    t_e1 = r.v("memset", epsl[0:1, :], float(4096 * 1e-6), inc=True)
    t_e2 = r.v("memset", onesr[0:1, :], 1.0, inc=True)
    ar.alloc("zrep", (NC_CH * FW + 1) // 2, top=True)
    zrep = ar.bf16("zrep", [128, NC_CH, FW])
    ar.alloc("tmpZ0", 512)
    ar.alloc("tmpZ1", 512)
    tmpZs = [ar.f32("tmpZ0", [128, 512]), ar.f32("tmpZ1", [128, 512])]
    prevc = {0: None, 1: None}
    it = 0
    for cc in range(NC_CH):
        for (g0, gw) in GRPS:
            par = it % 2
            it += 1
            tmpZ = tmpZs[par]
            bank = ctx.fbank()
            w0 = ((bank.free_tok,) if bank.free_tok else ()) + (t_q, t_s, t_e1, t_e2)
            r.te("matmul", bank.ap[:, 0:gw], sel2[:, cc, :],
                 qel[:, cc, g0:g0 + gw], start=True, stop=False, wait=w0)
            tok = r.te("matmul", bank.ap[:, 0:gw], epsl[0:1, :],
                       onesr[0:1, g0:g0 + gw], start=False, stop=True, inc=True)
            ev = r.v("reciprocal", tmpZ[:, 0:gw], bank.ap[:, 0:gw],
                     wait=(tok,) + ((prevc[par],) if prevc[par] else ()), inc=True)
            prevc[par] = r.v("tensor_copy", zrep[:, cc, g0:g0 + gw], tmpZ[:, 0:gw],
                             wait=(ev,), inc=True)
            bank.free_tok = ev
    t_z = r.tok("vector")
    ctx.carry["zrep"] = (zrep, t_z)
    for n in ("sel2", "epsrow", "onesr", "tmpZ0", "tmpZ1"):
        ar.free(n)
    ar.alloc("ares_l", (NC_CH * FW + 1) // 2, top=True)
    ares = ar.bf16("ares_l", [128, NC_CH, FW])
    t_ar = r.dma(ares, SC["ares"].ap().rearrange("(c p) t -> p c t", p=128))
    ctx.carry["ares_l"] = (ares, t_ar)


def stage_s5b_attn(ctx):
    r, ar, IN, SC = ctx.r, ctx.ar, ctx.IN, ctx.SC
    qr, t_qr = ctx.carry.pop("qrope")
    xi, t_xi = ctx.carry.pop("xi")
    ares, t_ar = ctx.carry.pop("ares_l")
    zrep, t_z = ctx.carry.pop("zrep")
    kvbf, t_kv = ctx.carry.pop("kvbf")
    outw, t_w = load_wtiles(ctx, "outw", IN["out_w"], DM, DM)
    outb, t_b = load_col(ctx, "outb", IN["out_b"])
    lw, t_lw = load_col(ctx, "lepew", IN["lepe_w"])
    lb, t_lb = load_col(ctx, "lepeb", IN["lepe_b"])
    ar.alloc("mrow", 16)
    mv = ar.bf16("mrow", [128, 32])
    t_m = r.dma(mv, IN["mask_full_bf"].ap()[:, HALO + OWN - 32:HALO + OWN])
    r.barrier()
    ar.alloc("tmpT0", 512)
    ar.alloc("tmpT1", 512)
    tmpTs = [ar.f32("tmpT0", [128, 512]), ar.f32("tmpT1", [128, 512])]
    ar.alloc("tmpU0", 512)
    ar.alloc("tmpU1", 512)
    tmpUs = [ar.bf16("tmpU0", [128, 1024]), ar.bf16("tmpU1", [128, 1024])]
    ar.alloc("topg", 2048)
    topg = ar.bf16("topg", [128, NC_CH, 512])
    ar.alloc("c1g", 4096)
    c1g = ar.f32("c1g", [128, NC_CH, 512])
    x1g = c1g
    VLO, VHI = HALO + OWN - 32, HALO + OWN
    t_fix = t_xi
    for c in range(NC_CH):
        t_fix = r.v("tensor_tensor", xi[:, c, VLO:VHI], xi[:, c, VLO:VHI],
                    mv, ALU.mult, wait=(t_xi, t_m), inc=True)
    prev = {"x1s": None}
    lastT = {0: None, 1: None}
    lastU = {0: None, 1: None}

    c1r = SC["c1"].ap().rearrange("(c p) t -> p c t", p=128)
    x1r = SC["x1"].ap().rearrange("(c p) t -> p c t", p=128)
    for gi, (g0, gw) in enumerate(GRPS):
        t_c1 = r.dma(c1g[:, :, 0:gw], c1r[:, :, g0:g0 + gw],
                     wait=(prev["x1s"],) if prev["x1s"] else ())
        lo = max(g0, 1)
        hi = min(g0 + gw, FW - 1)
        mz = []
        if g0 == 0:
            mz.append(r.v("memset", topg[:, :, 0:1], 0.0, inc=True))
        if g0 + gw == FW:
            mz.append(r.v("memset", topg[:, :, gw - 1:gw], 0.0, inc=True))
        for cc in range(NC_CH):
            par = cc % 2
            tmpT = tmpTs[par]
            tmpU = tmpUs[par]
            bank = ctx.fbank()
            w0 = ((bank.free_tok,) if bank.free_tok else ()) + (t_qr, t_kv)
            r.te("matmul", bank.ap[0:64, 0:gw], kvbf[0:64, cc, 0:64],
                 qr[0:64, cc, g0:g0 + gw], start=True, stop=True, wait=w0)
            tok = r.te("matmul", bank.ap[64:128, 0:gw], kvbf[64:128, cc, 64:128],
                       qr[64:128, cc, g0:g0 + gw], start=True, stop=True, inc=True)
            wp1 = (tok, t_z) + ((lastT[par],) if lastT[par] else ())
            p1 = r.v("tensor_tensor", tmpT[:, 0:gw], bank.ap[:, 0:gw],
                     zrep[:, cc, g0:g0 + gw], ALU.mult, wait=wp1, inc=True)
            bank.free_tok = p1
            wu0 = (t_fix, t_lw, t_lb) + ((lastU[par],) if lastU[par] else ())
            u0 = r.s("activation", tmpU[:, lo - g0:hi - g0], xi[:, cc, lo - 1:hi - 1],
                     AF.Identity, scale=lw[:, cc, 0:1], bias=lb[:, cc:cc + 1],
                     wait=wu0, inc=True)
            u1 = r.v("scalar_tensor_tensor", tmpU[:, lo - g0:hi - g0],
                     xi[:, cc, lo:hi], lw[:, cc, 1:2], tmpU[:, lo - g0:hi - g0],
                     ALU.mult, ALU.add, wait=(u0, p1), inc=True)
            u2 = r.v("scalar_tensor_tensor", tmpU[:, lo - g0:hi - g0],
                     xi[:, cc, lo + 1:hi + 1], lw[:, cc, 2:3], tmpU[:, lo - g0:hi - g0],
                     ALU.mult, ALU.add, wait=(u1,), inc=True)
            p2 = r.v("tensor_tensor", tmpT[:, lo - g0:hi - g0], tmpT[:, lo - g0:hi - g0],
                     tmpU[:, lo - g0:hi - g0], ALU.add, wait=(u2,), inc=True)
            p3 = r.v("tensor_tensor", topg[:, cc, lo - g0:hi - g0],
                     tmpT[:, lo - g0:hi - g0], ares[:, cc, lo - g0 + g0:hi],
                     ALU.mult, wait=(p2, t_ar), inc=True)
            lastU[par] = p2
            lastT[par] = p3
        t_top = r.tok("vector")
        evs = []
        for oc in range(NC_CH):
            bank = ctx.fbank()
            tok = None
            for ic in range(NC_CH):
                w0 = ((bank.free_tok,) if bank.free_tok else ()) + (t_top, t_w)
                tok = r.te("matmul", bank.ap[:, 0:gw],
                           outw[:, ic, 128 * oc:128 * oc + 128],
                           topg[:, ic, 0:gw], start=(ic == 0), stop=(ic == NC_CH - 1),
                           wait=w0 if ic == 0 else (),
                           inc=(True if ic == NC_CH - 1 else None))
            ev = r.v("scalar_tensor_tensor", x1g[:, oc, 0:gw], bank.ap[:, 0:gw],
                     outb[:, oc:oc + 1], c1g[:, oc, 0:gw], ALU.add, ALU.add,
                     wait=(tok, t_b, t_c1), inc=True)
            evs.append(ev)
            bank.free_tok = ev
        t_x1 = r.tok("vector")
        prev["x1s"] = r.dma(x1r[:, :, g0:g0 + gw], x1g[:, :, 0:gw], wait=(t_x1,),
                            eng="gpsimd")
    r.barrier()
    for n in ("qrope", "xi", "ares_l", "zrep", "kvbf", "outw", "outb",
              "lepew", "lepeb", "mrow", "tmpT0", "tmpT1", "tmpU0", "tmpU1",
              "topg", "c1g"):
        ar.free(n)


def stage_s6_mlp(ctx):
    r, ar, IN, SC = ctx.r, ctx.ar, ctx.IN, ctx.SC
    ones128 = ar.bf16("ones128", [128, 128])
    x1, t_x1 = load_fm_f32_t(ctx, "x1_l", SC["x1"])
    cw, t_w = load_col(ctx, "cpe2w", IN["cpe2_w"])
    cb, t_b = load_col(ctx, "cpe2b", IN["cpe2_b"])
    ar.alloc("mrow", (FW + 1) // 2)
    mv = ar.bf16("mrow", [128, FW])
    t_m = r.dma(mv, IN["mask_full_bf"].ap())
    VLO, VHI = HALO + OWN - 32, HALO + OWN
    t_fix = t_x1
    for c in range(NC_CH):
        t_fix = r.v("tensor_tensor", x1[:, c, VLO:VHI], x1[:, c, VLO:VHI],
                    mv[:, VLO:VHI], ALU.mult, wait=(t_x1, t_m), inc=True)
    ar.alloc("u2", (NC_CH * FW + 1) // 2)
    u2 = ar.bf16("u2", [128, NC_CH, FW])
    toks = conv3_fm(ctx, u2, x1, cw, cb, wait=(t_fix, t_w, t_b))
    t_x2 = None
    for c in range(NC_CH):
        t_x2 = r.v("tensor_tensor", x1[:, c, :], x1[:, c, :], u2[:, c, :],
                   ALU.add, wait=(toks[c],), inc=True)
    t_x2 = r.tok("vector")
    x2 = x1
    ar.free("u2"); ar.free("cpe2w"); ar.free("cpe2b"); ar.free("mrow")
    r.barrier()
    rfull, mufull, t_stats = ln_fm(ctx, x2, ones_bf=ones128, name="n2",
                                   wait=(t_x2,))
    r.barrier()
    n2g, t_g = load_col(ctx, "n2g", IN["norm2_g"])
    n2b, t_bb = load_col(ctx, "n2b", IN["norm2_b"])
    ar.alloc("xn2", (NC_CH * FW + 1) // 2)
    xn2 = ar.bf16("xn2", [128, NC_CH, FW])
    ar.alloc("xt2", 4 * FW)
    xt = ar.f32("xt2", [128, 4, FW])
    mub = mufull.unsqueeze(1).broadcast_to((128, 4, FW))
    rb = rfull.unsqueeze(1).broadcast_to((128, 4, FW))
    t3 = None
    for hh in range(2):
        c0 = 4 * hh
        war = (t3,) if t3 else ()
        t1 = r.v("tensor_tensor", xt, x2[:, c0:c0 + 4, :], mub, ALU.subtract,
                 wait=(t_stats, t_x2) + war, inc=True)
        t2 = r.v("tensor_tensor", xt, xt, rb, ALU.mult, wait=(t1,), inc=True)
        for c in range(4):
            t3 = r.s("activation", xn2[:, c0 + c, :], xt[:, c, :], AF.Identity,
                     scale=n2g[:, c0 + c:c0 + c + 1],
                     bias=n2b[:, c0 + c:c0 + c + 1],
                     wait=(t2, t_g, t_bb), inc=True)
    t_xn2 = r.tok("scalar")
    ar.free("xt2"); ar.free("n2_r"); ar.free("n2_mu")
    ar.free("n2g"); ar.free("n2b")
    r.barrier()
    fc2b, t_f2b = load_col(ctx, "fc2b", IN["fc2_b"])
    t_pb = None
    for c in range(NC_CH):
        t_pb = r.v("tensor_scalar", x2[:, c, :], x2[:, c, :],
                   fc2b[:, c:c + 1], None, ALU.add,
                   wait=(t_f2b, t_xn2), inc=True)
    t_pb = r.tok("vector")
    x0r = SC["x0fm"].ap().rearrange("(c p) t -> p c t", p=128)
    ar.alloc("x0g0", 4096)
    x0g = ar.f32("x0g0", [128, NC_CH, 512])
    prev_add = None
    for gi, (g0, gw) in enumerate(GRPS):
        tl = r.dma(x0g[:, :, 0:gw], x0r[:, :, g0:g0 + gw],
                   wait=(prev_add,) if prev_add else ())
        prev_add = r.v("tensor_tensor", x2[:, :, g0:g0 + gw],
                       x2[:, :, g0:g0 + gw], x0g[:, :, 0:gw],
                       ALU.add, wait=(tl, t_pb), inc=True)
    t_x0a = r.tok("vector")
    fc1b, t_f1b = load_col(ctx, "fc1b", IN["fc1_b"])
    ar.alloc("hg0", 2048)
    ar.alloc("hg1", 2048)
    hgs = [ar.bf16("hg0", [128, NC_CH, 512]), ar.bf16("hg1", [128, NC_CH, 512])]
    f1r = IN["fc1_w"].ap().rearrange("(c p) n -> p c n", p=128)
    f2r = IN["fc2_w"].ap().rearrange("(c p) n -> p c n", p=128)
    ar.alloc("fstg", NC_CH * 512)
    stg = ar.f32("fstg", [128, NC_CH, 512])
    f1qs = [None, None]
    f2qs = [None, None]
    ar.alloc("f1q0", NC_CH * 512)
    ar.alloc("f1q1", NC_CH * 512)
    f1qs = [ar.bf16("f1q0", [128, NC_CH, 1024]), ar.bf16("f1q1", [128, NC_CH, 1024])]
    ar.alloc("f2q0", NC_CH * 512)
    _f2q = ar.bf16("f2q0", [128, NC_CH, 1024])
    f2qs = [_f2q, _f2q]
    t_cast = None
    t_wq = {}
    load_plan = []
    for q in range(4):
        qp = q % 2
        load_plan.append(("f1", q, qp))
        load_plan.append(("f2", q, qp))

    def emit_wload(which, q, qp):
        nonlocal t_cast
        dst = f1qs[qp] if which == "f1" else f2qs[qp]
        war = t_wq.get(("mmdone", which, q - 2 if which == "f1" else q - 1))
        toks = []
        for hhalf in range(2):
            if which == "f1":
                srcap = f1r[:, :, 1024 * q + 512 * hhalf:1024 * q + 512 * hhalf + 512]
            else:
                srcap = f2r[:, 8 * q:8 * q + 8, 512 * hhalf:512 * hhalf + 512]
            tl = r.dma(stg, srcap, wait=((t_cast,) if t_cast else ()))
            t_cast = r.v("tensor_copy",
                         dst.rearrange("p c (a b) -> p c a b", a=2)[:, :, hhalf, :],
                         stg, wait=(tl,) + ((war,) if (war and hhalf == 0) else ()),
                         inc=True)
            toks.append(t_cast)
        t_wq[(which, q)] = toks[-1]

    emit_wload(*load_plan[0])
    emit_wload(*load_plan[1])
    nload = 2
    for q in range(4):
        qp = q % 2
        f1q, f2q, hg = f1qs[qp], f2qs[qp], hgs[qp]
        tc1 = t_wq[("f1", q)]
        tc2 = t_wq[("f2", q)]
        last_f1mm = None
        last_f2mm = None
        for (g0, gw) in GRPS:
            gtoks = []
            for hc in range(NC_CH):
                bank = ctx.fbank()
                tok = None
                for ic in range(NC_CH):
                    w0 = ()
                    if ic == 0:
                        w0 = ((bank.free_tok,) if bank.free_tok else ()) + \
                            (t_xn2, tc1)
                    tok = r.te("matmul", bank.ap[:, 0:gw],
                               f1q[:, ic, 128 * hc:128 * hc + 128],
                               xn2[:, ic, g0:g0 + gw],
                               start=(ic == 0), stop=(ic == NC_CH - 1),
                               wait=w0, inc=(True if ic == NC_CH - 1 else None))
                last_f1mm = tok
                ge = r.s("activation", hg[:, hc, 0:gw], bank.ap[:, 0:gw], AF.Gelu,
                         bias=fc1b[:, 8 * q + hc:8 * q + hc + 1],
                         wait=(tok, t_f1b) +
                         ((t_wq.get(("hgw", qp)),) if t_wq.get(("hgw", qp)) else ()),
                         inc=True)
                gtoks.append(ge)
                bank.free_tok = ge
            for oc in range(NC_CH):
                bank = ctx.fbank()
                tok = None
                for hc in range(NC_CH):
                    w0 = ()
                    if hc == 0:
                        w0 = ((bank.free_tok,) if bank.free_tok else ()) + \
                            tuple(gtoks) + (tc2,)
                    tok = r.te("matmul", bank.ap[:, 0:gw],
                               f2q[:, hc, 128 * oc:128 * oc + 128],
                               hg[:, hc, 0:gw],
                               start=(hc == 0), stop=(hc == NC_CH - 1),
                               wait=w0, inc=(True if hc == NC_CH - 1 else None))
                last_f2mm = tok
                ev = r.v("scalar_tensor_tensor", x2[:, oc, g0:g0 + gw],
                         bank.ap[:, 0:gw], 1.0, x2[:, oc, g0:g0 + gw],
                         ALU.mult, ALU.add, wait=(tok, t_x0a), inc=True)
                bank.free_tok = ev
            t_wq[("hgw", qp)] = last_f2mm
        t_wq[("mmdone", "f1", q)] = last_f1mm
        t_wq[("mmdone", "f2", q)] = last_f2mm
        while nload < len(load_plan) and nload <= 2 * q + 3:
            emit_wload(*load_plan[nload])
            nload += 1
    t_ps = r.tok("vector")
    ctx.carry["ps1"] = (x2, t_ps)
    r.barrier()
    for n in ("hg0", "hg1", "fc1b", "fc2b", "xn2", "x0g0", "fstg",
              "f1q0", "f1q1", "f2q0"):
        ar.free(n)


def outer_ln(ctx, x_ap3, t_in, ones_bf, name):
    r, ar = ctx.r, ctx.ar
    rfull, mufull, t_stats = ln_fm(ctx, x_ap3, ones_bf=ones_bf, name=name,
                                   wait=(t_in,))
    r.barrier()
    nch = x_ap3.shape[1]
    W = x_ap3.shape[2]
    mub = mufull.unsqueeze(1).broadcast_to((128, nch, W))
    rb = rfull.unsqueeze(1).broadcast_to((128, nch, W))
    t1 = r.v("tensor_tensor", x_ap3, x_ap3, mub, ALU.subtract,
             wait=(t_stats, t_in), inc=True)
    tok = r.v("tensor_tensor", x_ap3, x_ap3, rb, ALU.mult, wait=(t1,),
              inc=True)
    ar.free(name + "_r"); ar.free(name + "_mu")
    return tok


def stage_s7_ffn0(ctx):
    r, ar, IN, SC = ctx.r, ctx.ar, ctx.IN, ctx.SC
    ones128 = ar.bf16("ones128", [128, 128])
    x, t_x = ctx.carry.pop("ps1")
    t_ln = outer_ln(ctx, x, t_x, ones128, "ol1")
    r.barrier()
    asc, t_a = load_col(ctx, "asc", IN["ascale"])
    ar.alloc("scol", 2)
    scol = ar.f32("scol", [128, 2])
    t1 = r.v("tensor_scalar", scol[:, 0:1], asc, 1e-5, None, ALU.max,
             wait=(t_a,), inc=True)
    t2 = r.v("reciprocal", scol[:, 1:2], scol[:, 0:1], wait=(t1,), inc=True)
    ar.alloc("rint", (NC_CH * FW + 1) // 2)
    rint = ar.bf16("rint", [128, NC_CH, FW])
    ar.alloc("qtmp", (NC_CH * FW + 1) // 2)
    qh = ar.f16("qtmp", [128, NC_CH, FW])
    a1 = r.v("tensor_scalar", qh, x, scol[:, 1:2], 1536.0, ALU.mult, ALU.add,
             wait=(t2, t_ln), inc=True)
    a2 = r.v("tensor_scalar", rint, qh, 1536.0, 127.0, ALU.subtract, ALU.min,
             wait=(a1,), inc=True)
    t_q = r.v("tensor_scalar", rint, rint, -127.0, None, ALU.max,
              wait=(a2,), inc=True)
    r.barrier()
    ar.free("qtmp")
    ar.alloc("tstg", NC_CH * DM)
    tstg = ar.i32("tstg", [128, NC_CH, DM])
    t_tl = r.dma(tstg, IN["tern"].ap().rearrange("(c p) n -> p c n", p=128))
    ar.alloc("tnat", (NC_CH * DM + 1) // 2)
    tnat = ar.bf16("tnat", [128, NC_CH, DM])
    t_tc = r.v("tensor_scalar", tnat, tstg, 1.0, None, ALU.subtract,
               wait=(t_tl,), inc=True)
    ar.alloc("ternT", (NC_CH * DM + 1) // 2)
    ternT = ar.bf16("ternT", [128, NC_CH, DM])
    ident = ar.bf16("ident", [128, 128])
    for oc in range(NC_CH):
        bank = ctx.tbank()
        tok = None
        for ic in range(NC_CH):
            w0 = ((bank.free_tok,) if bank.free_tok else ()) + ((t_tc,) if ic == 0 else ())
            tok = r.te("transpose", bank.ap[:, 128 * ic:128 * ic + 128],
                       tnat[:, oc, 128 * ic:128 * ic + 128], ident,
                       wait=w0 if ic == 0 else (),
                       inc=(True if ic == NC_CH - 1 else None))
        ev = r.v("tensor_copy",
                 ternT[:, :, 128 * oc:128 * oc + 128],
                 bank.ap.rearrange("p (c t) -> p c t", c=NC_CH),
                 wait=(tok,), inc=True)
        bank.free_tok = ev
    r.barrier()
    ar.free("tstg"); ar.free("tnat")
    ffnb, t_fb = load_col(ctx, "ffnb", IN["ffn_bias"])
    prev_ev = {"t": None}

    def consumer(oc, g0, gw, bank, te_tok):
        w0 = (te_tok, t_fb, t2) + ((prev_ev["t"],) if prev_ev["t"] else ())
        e1 = r.v("scalar_tensor_tensor", x[:, oc, g0:g0 + gw], bank.ap[:, 0:gw],
                 scol[:, 0:1], x[:, oc, g0:g0 + gw], ALU.mult, ALU.add,
                 wait=w0, inc=True)
        e2 = r.v("tensor_scalar", x[:, oc, g0:g0 + gw], x[:, oc, g0:g0 + gw],
                 ffnb[:, oc:oc + 1], None, ALU.add, wait=(e1,), inc=True)
        prev_ev["t"] = e2
        return e1

    gemm_fm(ctx, ternT, rint, NC_CH, consumer, wait_ops=(t_q,))
    t_ps2 = r.tok("vector")
    r.barrier()
    ar.free("rint"); ar.free("ternT"); ar.free("ffnb")
    ar.free("asc"); ar.free("scol")
    r.barrier()
    t_ln2 = outer_ln(ctx, x, t_ps2, ones128, "ol2")
    store_fm(ctx, SC["x2o"], x, wait=(t_ln2,))
    ar.alloc("mrow", (FW + 1) // 2)
    mv = ar.bf16("mrow", [128, FW])
    t_m = r.dma(mv, IN["mask_full_bf"].ap())
    ar.alloc("x2ob", (NC_CH * FW + 1) // 2, top=True)
    xb = ar.bf16("x2ob", [128, NC_CH, FW])
    tb = None
    for c in range(NC_CH):
        tb = r.v("tensor_tensor", xb[:, c, :], x[:, c, :], mv, ALU.mult,
                 wait=(t_ln2, t_m), inc=True)
    tb = r.tok("vector")
    ctx.carry["x2ob"] = (xb, tb)
    r.barrier()
    for n in ("x1_l", "mrow"):
        ar.free(n)



def transpose_w(ctx, dst_bf3, src_bf3, ident, t_src, scale=None):
    r = ctx.r
    for oc in range(NC_CH):
        bank = ctx.tbank()
        tok = None
        for ic in range(NC_CH):
            w0 = ((bank.free_tok,) if bank.free_tok else ()) + ((t_src,) if ic == 0 else ())
            tok = r.te("transpose", bank.ap[:, 128 * ic:128 * ic + 128],
                       src_bf3[:, oc, 128 * ic:128 * ic + 128], ident,
                       wait=w0 if ic == 0 else (),
                       inc=(True if ic == NC_CH - 1 else None))
        ev = r.v("tensor_copy", dst_bf3[:, :, 128 * oc:128 * oc + 128],
                 bank.ap.rearrange("p (c t) -> p c t", c=NC_CH),
                 wait=(tok,), inc=True)
        bank.free_tok = ev
    return r.tok("vector")


def load_wT(ctx, name, dram, ident, scale=None):
    r, ar = ctx.r, ctx.ar
    ar.alloc(name + "_stg", NC_CH * DM)
    stg = ar.f32(name + "_stg", [128, NC_CH, DM])
    t_ld = r.dma(stg, dram.ap().rearrange("(c p) n -> p c n", p=128))
    ar.alloc(name + "_nat", (NC_CH * DM + 1) // 2)
    nat = ar.bf16(name + "_nat", [128, NC_CH, DM])
    if scale is not None:
        t_c = r.v("tensor_scalar", nat, stg, scale, None, ALU.mult,
                  wait=(t_ld,), inc=True)
    else:
        t_c = r.v("tensor_copy", nat, stg, wait=(t_ld,), inc=True)
    ar.free(name + "_stg")
    ar.alloc(name, (NC_CH * DM + 1) // 2)
    dst = ar.bf16(name, [128, NC_CH, DM])
    tok = transpose_w(ctx, dst, nat, ident, t_c)
    ar.free(name + "_nat")
    return dst, tok


def stage_s8_l1qkv(ctx):
    r, ar, IN, SC = ctx.r, ctx.ar, ctx.IN, ctx.SC
    ident = ar.bf16("ident", [128, 128])
    xb, t_x = ctx.carry.pop("x2ob")
    SCALE = 0.125
    ar.alloc("pstg", (NC_CH * FW + 1) // 2, top=True)
    pstg = ar.f32("pstg", [128, NC_CH, DM])
    wseq = ["Wq", "Wk", "Wv", "Wo"]
    pend = {"t": r.dma(pstg, IN["Wq"].ap().rearrange("(c p) n -> p c n", p=128))}

    def wT_from_stg(idx, wname, scale=None):
        t_ld = pend["t"]
        ar.alloc(wname + "T_nat", (NC_CH * DM + 1) // 2)
        nat = ar.bf16(wname + "T_nat", [128, NC_CH, DM])
        if scale is not None:
            t_c = r.v("tensor_scalar", nat, pstg, scale, None, ALU.mult,
                      wait=(t_ld,), inc=True)
        else:
            t_c = r.v("tensor_copy", nat, pstg, wait=(t_ld,), inc=True)
        if idx + 1 < 4:
            pend["t"] = r.dma(pstg,
                              IN[wseq[idx + 1]].ap().rearrange("(c p) n -> p c n", p=128),
                              wait=(t_c,))
        ar.alloc(wname + "T", (NC_CH * DM + 1) // 2)
        dst = ar.bf16(wname + "T", [128, NC_CH, DM])
        tok = transpose_w(ctx, dst, nat, ident, t_c)
        ar.free(wname + "T_nat")
        return dst, tok

    def proj(idx, wname, bname, out_sc, bscale=None):
        w, t_w = wT_from_stg(idx, wname, scale=(SCALE if bscale else None))
        b, t_b = load_col(ctx, "b_" + bname, IN[bname])
        if bscale:
            t_b = r.v("tensor_scalar", b, b, SCALE, None, ALU.mult,
                      wait=(t_b,), inc=True)
        ar.alloc("o_" + wname, (NC_CH * FW + 1) // 2)
        out = ar.bf16("o_" + wname, [128, NC_CH, FW])

        def consumer(oc, g0, gw, bank, te_tok):
            return r.s("activation", out[:, oc, g0:g0 + gw], bank.ap[:, 0:gw],
                       AF.Identity, bias=b[:, oc:oc + 1], wait=(te_tok, t_b),
                       inc=True)

        gemm_fm(ctx, w, xb, NC_CH, consumer, wait_ops=(t_x, t_w))
        t_d = r.tok("scalar")
        if out_sc is not None:
            store_fm(ctx, SC[out_sc], out, wait=(t_d,))
        r.barrier()
        ar.free(wname + "T"); ar.free("b_" + bname)
        return out, t_d

    qo, _ = proj(0, "Wq", "bq", "qfm", bscale=True)
    ar.free("o_Wq")
    r.barrier()
    ko, _ = proj(1, "Wk", "bk", "kfm")
    ar.free("o_Wk")
    r.barrier()
    vo, t_v = proj(2, "Wv", "bv", None)
    ar.free("x2ob")
    r.barrier()
    ar.alloc("vtml", (NT * DM + 1) // 2, top=True)
    vtml = ar.bf16("vtml", [128, NT, DM])
    tz1 = r.v("memset", vtml[:, NT - 1, :], 0.0, inc=True)
    t1 = _fm_to_tm(ctx, vtml, vo, ident, t_v, extra_wait=(tz1,))
    ctx.carry["vtml"] = (vtml, t1)
    ar.alloc("vtma", (22 * DM + 1) // 2, top=True)
    vtma = ar.bf16("vtma", [128, 22, DM])
    tz2 = r.v("memset", vtma[96:128, :, :], 0.0, inc=True)
    tzb = r.v("memset", vtma[64:96, 21, :], 0.0, wait=(tz2,), inc=True)
    tz3 = r.v("memset", vtma[32:64, 21, :], 0.0, wait=(tzb,), inc=True)
    for a in range(22):
        off = HALO + 96 * a
        nb = 2 if a < 21 else 1
        rows = 48 * nb
        bank = ctx.tbank()
        tok = None
        for cc in range(NC_CH):
            inap = vo[:, cc, off:off + rows]
            w0 = ((bank.free_tok,) if bank.free_tok else ()) + ((t_v,) if cc == 0 else ())
            tok = r.te("transpose", bank.ap[0:rows, 128 * cc:128 * cc + 128],
                       inap, ident, wait=w0 if cc == 0 else (),
                       inc=(True if cc == NC_CH - 1 else None))
        ev = r.v("tensor_copy",
                 vtma[0:rows, a, :].rearrange("p (c t) -> p c t", c=NC_CH),
                 bank.ap.rearrange("p (c t) -> p c t", c=NC_CH)[0:rows, :, :],
                 wait=(tok, tz3), inc=True)
        bank.free_tok = ev
    t2 = r.tok("vector")
    ctx.carry["vtma"] = (vtma, t2)
    r.barrier()
    ar.free("o_Wv")
    r.barrier()
    woT, t_wo = wT_from_stg(3, "Wo")
    store_fm(ctx, SC["woT"], woT, wait=(t_wo,))
    r.barrier()
    ar.free("WoT"); ar.free("pstg")



def _build_blockmask(ctx, name, width, nblk):
    r, ar = ctx.r, ctx.ar
    iota = ar.f32("iota", [128, 128])
    pid = ar.f32("pid", [128, 1])
    ar.alloc(name, 2 * width + 8)
    mm = ar.bf16(name, [128, 4 * width + 16])
    mL = mm[:, 0:width]
    mR = mm[:, width:2 * width]
    crow = mm[:, 2 * width:3 * width]
    ones1 = mm[:, 3 * width:4 * width]
    lo = mm[:, 4 * width:4 * width + 8]
    ar.alloc(name + "_t", 2 * width)
    ta = ar.f32(name + "_t", [128, 2 * width])
    a, b = ta[:, 0:width], ta[:, width:2 * width]
    t0 = r.v("tensor_scalar", lo[:, 0:2].bitcast(F32)[:, 0:1], pid, 16.0, None,
             ALU.mult, inc=True)
    locol = lo[:, 0:2].bitcast(F32)[:, 0:1]
    t1 = r.v("tensor_scalar", a, iota[:, 0:width], locol, None, ALU.is_ge,
             wait=(t0,), inc=True)
    t2 = r.v("tensor_scalar", b, iota[:, 0:width], locol, 16.0, ALU.subtract,
             ALU.is_lt, wait=(t1,), inc=True)
    t3 = r.v("tensor_tensor", a, a, b, ALU.mult, wait=(t2,), inc=True)
    t4 = r.v("tensor_scalar", mL, a, float(2 ** 30), None, ALU.mult,
             wait=(t3,), inc=True)
    t5 = r.v("tensor_copy", mR, a, wait=(t4,), inc=True)
    t6 = r.v("memset", crow[0:1, :], -float(2 ** 30), wait=(t5,), inc=True)
    t7 = r.v("memset", ones1[0:1, :], 1.0, wait=(t6,), inc=True)
    ar.free(name + "_t")
    tok = ctx.r.tok("vector")
    return (mL, mR, crow, ones1), tok


def _load_atrous_mask(ctx):
    r, ar, IN = ctx.r, ctx.ar, ctx.IN
    width = 96
    ar.alloc("mska", 2 * width + 8)
    mm = ar.bf16("mska", [128, 4 * width + 16])
    t = r.dma(mm[:, 0:4 * width], IN["mska"].ap())
    return (mm[:, 0:width], mm[:, width:2 * width],
            mm[:, 2 * width:3 * width], mm[:, 3 * width:4 * width]), t


def _l1_scores(ctx, h, qf, kf, mpanels, t_mask, widths, pslab, zcols,
               ident, t_in, atrous, ps_war):
    r = ctx.r
    m, hc = h % 2, h // 2
    ntile = len(widths)
    nslot = 4
    W = pslab.shape[2]
    mL, mR, crow, ones1 = mpanels
    bank = None
    for ti, w in enumerate(widths):
        slot = ti % nslot
        if slot == 0:
            bank = ctx.fbank()
        off = HALO + (96 if atrous else 128) * ti
        qap = qf[64 * m:64 * m + 64, hc, off:off + w]
        kap = kf[64 * m:64 * m + 64, hc, off:off + w]
        w0 = ((bank.free_tok,) if bank.free_tok else ()) + ((t_in, t_mask) if slot == 0 else ())
        r.te("matmul", bank.ap[0:w, 128 * slot:128 * slot + w],
             mL[0:8, 0:w], mR[0:8, 0:w], start=True, stop=False, wait=w0)
        r.te("matmul", bank.ap[0:w, 128 * slot:128 * slot + w],
             crow[0:1, 0:w], ones1[0:1, 0:w], start=False, stop=False)
        tok = r.te("matmul", bank.ap[0:w, 128 * slot:128 * slot + w], qap, kap,
                   start=False, stop=True,
                   inc=(True if (slot == nslot - 1 or ti == ntile - 1) else None))
        if slot == nslot - 1 or ti == ntile - 1:
            ev = None
            for k in range(slot + 1):
                tk = ti - slot + k
                wk = widths[tk]
                ev = r.s("activation", pslab[0:wk, tk, 0:wk],
                         bank.ap[0:wk, 128 * k:128 * k + wk], AF.Exp,
                         wait=(tok,) + ((ps_war,) if ps_war else ()), inc=True)
            bank.free_tok = ev
    t_exp = r.tok("scalar")
    ntile = len(widths)
    zr = zcols[:, 0:ntile]
    t4 = r.v("tensor_reduce", zr, pslab, AX.X, ALU.add, wait=(t_exp,), inc=True)
    t4b = r.v("tensor_scalar", zr, zr, 1e-30, None, ALU.max, wait=(t4,),
              inc=True)
    t5 = r.v("reciprocal", zr, zr, wait=(t4b,), inc=True)
    zexp = zr.unsqueeze(-1).broadcast_to((128, ntile, W))
    t6 = r.v("scalar_tensor_tensor", pslab, pslab, 1.0, zexp, ALU.mult, ALU.mult,
             wait=(t5,), inc=True)
    return t6


def _l1_av(ctx, h, vt, widths, pslab, ptslab, orow, t_war, ident, t6,
           atrous, row0):
    r = ctx.r
    ntile = len(widths)
    nslot = 4
    W = pslab.shape[2]
    tbank = None
    last_tr = None
    for ti, w in enumerate(widths):
        slot = ti % nslot
        if slot == 0:
            tbank = ctx.tbank()
        w0 = ((tbank.free_tok,) if tbank.free_tok else ()) + ((t6,) if slot == 0 else ())
        tok = r.te("transpose", tbank.ap[0:W, 128 * slot:128 * slot + 128],
                   pslab[:, ti, :], ident, wait=w0,
                   inc=(True if (slot == nslot - 1 or ti == ntile - 1) else None))
        if slot == nslot - 1 or ti == ntile - 1:
            n = slot + 1
            t0 = ti - slot
            qn = ptslab.shape[2]
            src = tbank.ap.rearrange("p (s q) -> p s q", q=128)[0:W, 0:n, 0:qn]
            ev = r.v("tensor_copy", ptslab[0:W, t0:t0 + n, :], src,
                     wait=(tok,), inc=True)
            tbank.free_tok = ev
            last_tr = tok
    t_pt = r.tok("vector")
    abank = None
    for ti, w in enumerate(widths):
        slot = ti % nslot
        if slot == 0:
            abank = ctx.fbank()
        krows = w
        w0 = ((abank.free_tok,) if abank.free_tok else ()) + ((t_pt,) if slot == 0 else ())
        tok = r.te("matmul", abank.ap[0:64, 128 * slot:128 * slot + w],
                   vt[0:krows, ti, 64 * h:64 * h + 64],
                   ptslab[0:krows, ti, 0:w], start=True, stop=True,
                   wait=w0, inc=(True if (slot == nslot - 1 or ti == ntile - 1) else None))
        if slot == nslot - 1 or ti == ntile - 1:
            n = slot + 1
            t0 = ti - slot
            last_ev = None
            for k in range(n):
                tk = t0 + k
                wk = widths[tk]
                off = HALO + (96 if atrous else 128) * tk
                srcap = abank.ap[0:64, 128 * k:128 * k + wk]
                dstap = orow[row0:row0 + 64, off:off + wk]
                if atrous:
                    last_ev = r.v("tensor_tensor", dstap, dstap, srcap,
                                  ALU.add, wait=(tok, t_war), inc=True)
                else:
                    last_ev = r.s("activation", dstap, srcap,
                                  AF.Identity, wait=(tok, t_war), inc=True)
            abank.free_tok = last_ev
    t_done = r.tok("vector" if atrous else "scalar")
    return t_done, last_tr


def stage_s9_l1attn(ctx):
    r, ar, IN, SC = ctx.r, ctx.ar, ctx.IN, ctx.SC
    ident = ar.bf16("ident", [128, 128])
    qf, t_q = load_fm_bf(ctx, "qf_l", SC["qfm"])
    kf, t_k = load_fm_bf(ctx, "kf_l", SC["kfm"])
    vtml, t_vl = ctx.carry.pop("vtml")
    vtma, t_va = ctx.carry.pop("vtma")
    ar.alloc("ofm_sb", (NC_CH * FW + 1) // 2, top=True)
    ofm_sb = ar.bf16("ofm_sb", [128, NC_CH, FW])
    mpl, t_m1 = _build_blockmask(ctx, "mskl", 128, 8)
    mpa, t_m2 = _load_atrous_mask(ctx)
    ar.alloc("pslabL", (NT * 128 + 1) // 2)
    pslabL = ar.bf16("pslabL", [128, NT, 128])
    ar.alloc("pslabA", (22 * 96 + 1) // 2)
    pslabA = ar.bf16("pslabA", [128, 22, 96])
    ar.alloc("ptslabL", (NT * 128 + 1) // 2)
    ptslabL = ar.bf16("ptslabL", [128, NT, 128])
    ar.alloc("ptslabA", (22 * 96 + 1) // 2)
    ptslabA = ar.bf16("ptslabA", [128, 22, 96])
    ar.alloc("zcolsL", 2 * NT)
    zcolsL = ar.f32("zcolsL", [128, 2 * NT])
    ar.alloc("zcolsA", 44)
    zcolsA = ar.f32("zcolsA", [128, 44])
    t_z0 = r.v("memset", ofm_sb, 0.0, inc=True)
    t_z1 = r.v("memset", pslabL, 0.0, wait=(t_z0,), inc=True)
    t_z2 = r.v("memset", pslabA, 0.0, wait=(t_z1,), inc=True)
    t_z3 = r.v("memset", ptslabA[:, :, :], 0.0, wait=(t_z2,), inc=True)
    r.barrier()
    widthsL = [128] * 16 + [16]
    widthsA = [96] * 21 + [48]
    psL_war = None
    psA_war = None
    for h in range(16):
        m = h % 2
        orow = ofm_sb[:, h // 2, :]
        t6L = _l1_scores(ctx, h, qf, kf, mpl, t_m1, widthsL, pslabL,
                         zcolsL, ident, t_q, False, psL_war)
        t6A = _l1_scores(ctx, h, qf, kf, mpa, t_m2, widthsA, pslabA,
                         zcolsA, ident, t_k, True, psA_war)
        t_loc, psL_war = _l1_av(ctx, h, vtml, widthsL, pslabL, ptslabL,
                                orow, t_z3, ident, t6L, False, 64 * m)
        t_atr, psA_war = _l1_av(ctx, h, vtma, widthsA, pslabA, ptslabA,
                                orow, t_loc, ident, t6A, True, 64 * m)
    r.barrier()
    for n in ("qf_l", "kf_l", "vtml", "vtma", "mskl", "mska",
              "pslabL", "pslabA", "ptslabL", "ptslabA", "zcolsL", "zcolsA"):
        ar.free(n)
    t_ofm = ctx.r.tok("vector")
    r.barrier()
    ones128 = ar.bf16("ones128", [128, 128])
    x, t_x = load_fm_f32_t(ctx, "x2o_l", SC["x2o"])
    ofm, t_o = ofm_sb, t_ofm
    ar.alloc("woT", (NC_CH * DM + 1) // 2)
    woT = ar.bf16("woT", [128, NC_CH, DM])
    t_w = r.dma(woT, SC["woT"].ap().rearrange("(c p) n -> p c n", p=128))
    bo, t_b = load_col(ctx, "bo_l", IN["bo"])
    prev_ev = {"t": None}

    def consumer(oc, g0, gw, bank, te_tok):
        w0 = (te_tok, t_b, t_x) + ((prev_ev["t"],) if prev_ev["t"] else ())
        ev = r.v("scalar_tensor_tensor", x[:, oc, g0:g0 + gw], bank.ap[:, 0:gw],
                 bo[:, oc:oc + 1], x[:, oc, g0:g0 + gw], ALU.add, ALU.add,
                 wait=w0, inc=True)
        prev_ev["t"] = ev
        return ev

    gemm_fm(ctx, woT, ofm, NC_CH, consumer, wait_ops=(t_o, t_w))
    t_ps3 = r.tok("vector")
    r.barrier()
    ar.free("ofm_sb"); ar.free("woT"); ar.free("bo_l")
    r.barrier()
    t_ln3 = outer_ln(ctx, x, t_ps3, ones128, "ol3")
    store_fm(ctx, SC["x3o"], x, wait=(t_ln3,))
    r.barrier()
    ar.free("x2o_l")



def _round_f16(ctx, dst, src, f16tmp, wait=(), post=None):
    r = ctx.r
    t1 = r.v("tensor_scalar", f16tmp, src, 1536.0, None, ALU.add,
             wait=wait, inc=True)
    if post is None:
        return r.v("tensor_scalar", dst, f16tmp, 1536.0, None, ALU.subtract,
                   wait=(t1,), inc=True)
    s2, op1 = post
    return r.v("tensor_scalar", dst, f16tmp, 1536.0, s2, ALU.subtract, op1,
               wait=(t1,), inc=True)


def wq_pass1_half(ctx, dram, nrows_h, ncols, acc, t_acc):
    r, ar = ctx.r, ctx.ar
    ones128 = ar.bf16("ones128", [128, 128])
    wr = dram.ap().rearrange("(c p) n -> p c n", p=128)
    ncs = nrows_h // 128
    chunks = []
    for c0 in range(0, ncs, 8):
        cw = min(8, ncs - c0)
        for n0 in range(0, ncols, 1024):
            chunks.append((c0, cw, n0))
    stgs = [ar.f32("wstg0", [128, 8, 1024]), ar.f32("wstg1", [128, 8, 1024])]
    wabs = [ar.bf16("wabs0", [128, 8, 1024]), ar.bf16("wabs1", [128, 8, 1024])]
    t_ld = [None, None]
    pend = ctx.carry.pop("wq1_pend", {})
    t_ab = [pend.get(0), pend.get(1)]
    t_mm = [pend.get("m0"), pend.get("m1")]
    for i, (c0, cw, n0) in enumerate(chunks):
        par = i % 2
        t_ld[par] = r.dma(stgs[par][:, 0:cw, :],
                          wr[:, c0:c0 + cw, n0:n0 + 1024],
                          wait=((t_ab[par],) if t_ab[par] else ()))
        t_ab[par] = r.s("activation", wabs[par][:, 0:cw, :], stgs[par][:, 0:cw, :],
                        AF.Abs,
                        wait=(t_ld[par],) + ((t_mm[par],) if t_mm[par] else ()),
                        inc=True)
        bank = ctx.fbank()
        tok = None
        for ic in range(cw):
            for gg in range(2):
                first = ic == 0 and gg == 0
                w0 = ((((bank.free_tok,) if bank.free_tok else ()) + (t_ab[par],))
                      if first else ())
                tok = r.te("matmul", bank.ap[:, 0:512], ones128,
                           wabs[par][:, ic, 512 * gg:512 * gg + 512],
                           start=first, stop=(ic == cw - 1 and gg == 1),
                           wait=w0,
                           inc=(True if (ic == cw - 1 and gg == 1) else None))
        t_mm[par] = tok
        t_acc = r.v("tensor_tensor", acc, acc, bank.ap[:, 0:512], ALU.add,
                    wait=(tok, t_acc), inc=True)
        bank.free_tok = t_acc
    ctx.carry["wq1_pend"] = {0: t_ab[0], 1: t_ab[1],
                             "m0": t_mm[0], "m1": t_mm[1]}
    return t_acc


def wq_pass2(ctx, dram, rows, cols, out_sc, sinv, t_si):
    r, ar, SC = ctx.r, ctx.ar, ctx.SC
    ident = ar.bf16("ident", [128, 128])
    nrc = rows // 1024
    ncc = cols // 1024
    nit = nrc * ncc
    wr = dram.ap().rearrange("(c p) n -> p c n", p=128)
    out_r = SC[out_sc].ap().rearrange("(c p) n -> p c n", p=128)
    stgs = [ar.f32("wstg0", [128, NC_CH, 1024]), ar.f32("wstg1", [128, NC_CH, 1024])]
    w16 = ar.f16("wq16", [128, NC_CH, 1024])
    wqb = ar.bf16("wqb", [128, NC_CH, 1024])
    wts = [ar.bf16("wabs0", [128, NC_CH, 1024]), ar.bf16("wabs1", [128, NC_CH, 1024])]
    t_ld = [None, None]
    t_q3p = [None, None]
    t_q1 = None
    t_q3 = None
    last_tr = None
    t_ev = [None, None]
    pend = ctx.carry.pop("wq_pend", {})
    t_q3p[0] = pend.get(0)
    t_q3p[1] = pend.get(1)
    t_wdma = [pend.get("d0"), pend.get("d1")]
    for rc in range(nit):
        par = rc % 2
        rr, cc2 = rc // ncc, rc % ncc
        t_ld[par] = r.dma(stgs[par], wr[:, 8 * rr:8 * rr + 8,
                                       1024 * cc2:1024 * cc2 + 1024],
                          wait=((t_q3p[par],) if t_q3p[par] else ()))
        t_q1 = r.v("tensor_scalar", w16, stgs[par], sinv, 1536.0,
                   ALU.mult, ALU.add,
                   wait=(t_ld[par], t_si) + ((t_q3,) if t_q3 else ()), inc=True)
        t_q2 = r.v("tensor_scalar", stgs[par], w16, 1536.0, 1.0, ALU.subtract,
                   ALU.min, wait=(t_q1,), inc=True)
        t_q3 = r.v("tensor_scalar", wqb, stgs[par], -1.0, None, ALU.max,
                   wait=(t_q2,) + ((last_tr,) if last_tr else ()), inc=True)
        t_q3p[par] = t_q3
        wpar = rc % 2
        for ch in range(NC_CH):
            tb = ctx.tbank()
            tok2 = None
            for sub in range(NC_CH):
                w0 = ((tb.free_tok,) if tb.free_tok else ()) + \
                    ((t_q3,) if sub == 0 else ()) + \
                    ((t_wdma[wpar],) if (sub == 0 and ch == 0 and t_wdma[wpar])
                     else ())
                tok2 = r.te("transpose", tb.ap[:, 128 * sub:128 * sub + 128],
                            wqb[:, sub, 128 * ch:128 * ch + 128], ident,
                            wait=w0 if sub == 0 else (),
                            inc=(True if sub == NC_CH - 1 else None))
            t_ev[wpar] = r.v("tensor_copy", wts[wpar][:, ch, :], tb.ap,
                             wait=(tok2,) + ((t_wdma[wpar],) if (ch == 0 and
                                             t_wdma[wpar]) else ()), inc=True)
            tb.free_tok = t_ev[wpar]
            last_tr = tok2
        t_wdma[wpar] = r.dma(
            out_r[:, 8 * cc2:8 * cc2 + 8, 1024 * rr:1024 * rr + 1024],
            wts[wpar], wait=(t_ev[wpar],), eng="gpsimd")
    ctx.carry["wq_pend"] = {0: t_q3p[0], 1: t_q3p[1],
                            "d0": t_wdma[0], "d1": t_wdma[1]}


def stage_s10a_wq(ctx):
    r, ar, IN, SC = ctx.r, ctx.ar, ctx.IN, ctx.SC
    ar.alloc("x3_l", NC_CH * FW, top=True)
    x3 = ar.f32("x3_l", [128, NC_CH, FW])
    t_x3 = r.dma(x3, SC["x3o"].ap().rearrange("(c p) t -> p c t", p=128))
    ctx.carry["x3_l"] = (x3, t_x3)
    ar.alloc("wacc", 512)
    acc3 = ar.f32("wacc", [128, 512])
    ar.alloc("mc3", 16)
    mc = ar.f32("mc3", [128, 16])
    ar.alloc("wstg0", NC_CH * 1024)
    ar.alloc("wstg1", NC_CH * 1024)
    ar.alloc("wabs0", (NC_CH * 1024 + 1) // 2)
    ar.alloc("wabs1", (NC_CH * 1024 + 1) // 2)
    specs = [("wg_h", 2 * DM, DM, "w_gate", 4 * DM, DM, "wgT", 0),
             ("wu_h", 2 * DM, DM, "w_up", 4 * DM, DM, "wuT", 1),
             ("wd_h", DM // 2, 4 * DM, "w_down", DM, 4 * DM, "wdT", 2)]
    t_red = None
    for i, (hname, hr, hc, fname, fr, fc, osc, mi) in enumerate(specs):
        t0 = r.v("memset", acc3, 0.0,
                 wait=((t_red,) if t_red else ()), inc=True)
        t_acc = wq_pass1_half(ctx, IN[hname], hr, hc, acc3, t0)
        t_red = r.v("tensor_reduce", mc[:, mi:mi + 1], acc3, AX.X, ALU.add,
                    wait=(t_acc,), inc=True)
    t_p1 = r.tok("vector")
    t_absd = r.tok("scalar")
    ctx.carry["wq_pend"] = {0: t_absd, 1: t_absd}
    ctx.carry.pop("wq1_pend", None)
    ar.free("wacc")
    t_st = r.dma(SC["cc2_in"].ap(), mc[:, 0:4], wait=(t_p1,))
    cc = r.sems["cc"]
    t_cc = r.op("gpsimd", "collective_compute", "AllReduce", ALU.add,
                replica_groups=[[0, 1], [2, 3], [4, 5], [6, 7]],
                ins=[SC["cc2_in"].ap().opt()], outs=[SC["cc2_out"].ap().opt()],
                wait=(t_st,), inc=cc)
    t_rd = r.dma(mc[:, 0:4], SC["cc2_out"].ap(), wait=(t_cc,))
    t_si = {}
    tprev = t_rd
    for i, (hname, hr, hc, fname, fr, fc, osc, mi) in enumerate(specs):
        t_m = r.v("tensor_scalar", mc[:, 4 + mi:5 + mi], mc[:, mi:mi + 1],
                  1.0 / (fr * fc), 1e-5, ALU.mult, ALU.max, wait=(tprev,),
                  inc=True)
        tprev = r.v("reciprocal", mc[:, 8 + mi:9 + mi], mc[:, 4 + mi:5 + mi],
                    wait=(t_m,), inc=True)
        t_si[mi] = tprev
        r.dma(SC["mw"].ap()[:, mi:mi + 1], mc[:, 4 + mi:5 + mi],
              wait=(t_m,), allow_slow_non_contiguous=True)
    ar.alloc("wq16", NC_CH * 512)
    ar.alloc("wqb", (NC_CH * 1024 + 1) // 2)
    wq_pass2(ctx, IN["w_gate"], 4 * DM, DM, "wgT", mc[:, 8:9], t_si[0])
    wq_pass2(ctx, IN["w_up"], 4 * DM, DM, "wuT", mc[:, 9:10], t_si[1])
    wq_pass2(ctx, IN["w_down"], DM, 4 * DM, "wdT", mc[:, 10:11], t_si[2])
    ctx.carry.pop("wq_pend", None)
    for n in ("wstg0", "wstg1", "mc3", "wq16", "wqb", "wabs0", "wabs1"):
        ar.free(n)
    r.barrier()


def stage_s10b_ffn(ctx):
    r, ar, IN, SC = ctx.r, ctx.ar, ctx.IN, ctx.SC
    ident = ar.bf16("ident", [128, 128])
    OWNW = OWN
    x3, t_x3 = ctx.carry.pop("x3_l")
    ar.alloc("hifm", (NC_CH * FW + 1) // 2)
    hi = ar.bf16("hifm", [128, NC_CH, FW])
    ar.alloc("lofm", (NC_CH * FW + 1) // 2)
    lo = ar.bf16("lofm", [128, NC_CH, FW])
    t_hi = r.v("tensor_copy", hi, x3, wait=(t_x3,), inc=True)
    t_lo = None
    for c in range(NC_CH):
        t_lo = r.v("tensor_tensor", lo[:, c, :], x3[:, c, :], hi[:, c, :],
                   ALU.subtract, wait=(t_hi,), inc=True)
    t_lo = r.tok("vector")
    ar.free("x3_l")
    r.barrier()
    ar.alloc("hitm", (NT * DM + 1) // 2)
    hitm = ar.bf16("hitm", [128, NT, DM])
    tzh = r.v("memset", hitm[:, NT - 1, :], 0.0, inc=True)
    t1 = _fm_to_tm(ctx, hitm, hi, ident, t_hi, extra_wait=(tzh,))
    ar.alloc("lotm", (NT * DM + 1) // 2)
    lotm = ar.bf16("lotm", [128, NT, DM])
    tzl = r.v("memset", lotm[:, NT - 1, :], 0.0, inc=True)
    t2 = _fm_to_tm(ctx, lotm, lo, ident, t_lo, extra_wait=(tzl,))
    ar.free("hifm"); ar.free("lofm")
    r.barrier()
    ar.alloc("xtall", NT * DM)
    xta = ar.f32("xtall", [128, NT, DM])
    ta = r.v("tensor_tensor", xta, hitm, lotm, ALU.add, wait=(t1, t2), inc=True)
    td = r.dma(SC["x3tm"].ap().rearrange("(t p) c -> p t c", p=128), xta,
               wait=(ta,), eng="gpsimd")
    ar.free("hitm"); ar.free("lotm")
    ar.alloc("qcols", 4 * NT)
    qc = ar.f32("qcols", [128, 4 * NT])
    a0, a1 = qc[:, 0:NT], qc[:, NT:2 * NT]
    asc, axv = qc[:, 2 * NT:3 * NT], qc[:, 3 * NT:4 * NT]
    m1 = r.v("tensor_reduce", a0, xta, AX.X, ALU.max,
             apply_absolute_value=True, wait=(ta,), inc=True)
    m2 = r.v("tensor_scalar", a1, a0, 1e-5, None, ALU.max, wait=(m1,), inc=True)
    m3 = r.v("reciprocal", asc, a1, wait=(m2,), inc=True)
    m4 = r.v("tensor_scalar", asc, asc, 127.0, None, ALU.mult, wait=(m3,),
             inc=True)
    m5 = r.v("tensor_scalar", axv, a1, 1.0 / 127.0, None, ALU.mult,
             wait=(m2,), inc=True)
    r.barrier()
    ar.alloc("x16", (NT * DM + 1) // 2)
    x16 = ar.f16("x16", [128, NT, DM])
    ar.alloc("xqtm", (NT * DM + 1) // 2)
    xqa = ar.bf16("xqtm", [128, NT, DM])
    tq = None
    for t in range(NT):
        tq = r.v("tensor_scalar", x16[:, t, :], xta[:, t, :], asc[:, t:t + 1],
                 1536.0, ALU.mult, ALU.add, wait=(m4, td),
                 inc=(True if t == NT - 1 else None))
    t_xq = r.v("tensor_scalar", xqa, x16, 1536.0, None, ALU.subtract,
               wait=(tq,), inc=True)
    ar.free("xtall"); ar.free("x16")
    r.barrier()
    ar.alloc("xqfm", (NC_CH * OWNW + 1) // 2)
    xqf = ar.bf16("xqfm", [128, NC_CH, OWNW])
    ar.alloc("wgT_l", NC_CH * 2048)
    wg = ar.bf16("wgT_l", [128, NC_CH, 4 * DM])
    t_wg = r.dma(wg, SC["wgT"].ap().rearrange("(c p) n -> p c n", p=128))
    ar.alloc("wuT_l", NC_CH * 2048)
    wu = ar.bf16("wuT_l", [128, NC_CH, 4 * DM])
    t_wu = r.dma(wu, SC["wuT"].ap().rearrange("(c p) n -> p c n", p=128))
    ar.alloc("mw_l", 8)
    mw = ar.f32("mw_l", [128, 8])
    t_mw = r.dma(mw[:, 0:3], SC["mw"].ap()[:, 0:3],
                 allow_slow_non_contiguous=True)
    prevc = None
    for t in range(NT):
        nr = 128 if t < NT - 1 else OWN - 128 * t
        bank = ctx.tbank()
        tok = None
        for cc in range(NC_CH):
            w0 = ((bank.free_tok,) if bank.free_tok else ()) + ((t_xq,) if cc == 0 else ())
            tok = r.te("transpose", bank.ap[:, 128 * cc:128 * cc + 128],
                       xqa[:, t, 128 * cc:128 * cc + 128], ident,
                       wait=w0 if cc == 0 else (),
                       inc=(True if cc == NC_CH - 1 else None))
        prevc = r.v("tensor_copy",
                    xqf[:, :, 128 * t:128 * t + nr],
                    bank.ap.rearrange("p (c q) -> p c q", c=NC_CH)[:, :, 0:nr],
                    wait=(tok,), inc=True)
        bank.free_tok = prevc
    t_xqf = r.tok("vector")
    ar.free("xqtm")
    r.barrier()
    ar.alloc("scg", 2 * NT)
    scg = ar.f32("scg", [128, 2 * NT])
    t_s1 = r.v("tensor_scalar", scg[:, 0:NT], axv, mw[:, 0:1], None, ALU.mult,
               wait=(t_mw, m5), inc=True)
    t_s2 = r.v("tensor_scalar", scg[:, NT:2 * NT], axv, mw[:, 1:2], None,
               ALU.mult, wait=(t_s1,), inc=True)
    ar.alloc("ht0", 2048)
    ar.alloc("ht1", 2048)
    hts = [ar.bf16("ht0", [128, 4 * DM]), ar.bf16("ht1", [128, 4 * DM])]
    ar.alloc("sg0", 256)
    ar.alloc("sg1", 256)
    sgs = [ar.bf16("sg0", [128, 512]), ar.bf16("sg1", [128, 512])]
    ar.alloc("h16", 2048)
    h16 = ar.f16("h16", [128, 4 * DM])
    ar.alloc("hqt", 2048)
    hqt = ar.bf16("hqt", [128, 4 * DM])
    ar.alloc("hqc", 8)
    hqc = ar.f32("hqc", [128, 8])
    ar.alloc("ahinv", NT)
    ahv = ar.f32("ahinv", [128, NT])
    hq_r = SC["hq"].ap().rearrange("(t p) n -> p t n", p=128)
    prev_dma = None
    prev_q6 = None
    q5_done = {0: None, 1: None}
    sg_last = {0: None, 1: None}
    qa_last = None
    for t in range(NT):
        par = t % 2
        ht = hts[par]
        ntok = 128 if t < NT - 1 else OWN - 128 * t
        for hs in range(8):
            spar = hs % 2
            sgb = sgs[spar]
            bg = ctx.fbank()
            bu = ctx.fbank()
            tg = tu = None
            for ic in range(NC_CH):
                w0 = (((bg.free_tok,) if bg.free_tok else ()) +
                      ((t_xqf, t_wg) if ic == 0 else ()))
                tg = r.te("matmul", bg.ap[0:ntok, 0:512],
                          xqf[:, ic, 128 * t:128 * t + ntok],
                          wg[:, ic, 512 * hs:512 * hs + 512],
                          start=(ic == 0), stop=(ic == NC_CH - 1),
                          wait=w0, inc=(True if ic == NC_CH - 1 else None))
            for ic in range(NC_CH):
                w0 = (((bu.free_tok,) if bu.free_tok else ()) +
                      ((t_xqf, t_wu) if ic == 0 else ()))
                tu = r.te("matmul", bu.ap[0:ntok, 0:512],
                          xqf[:, ic, 128 * t:128 * t + ntok],
                          wu[:, ic, 512 * hs:512 * hs + 512],
                          start=(ic == 0), stop=(ic == NC_CH - 1),
                          wait=w0, inc=(True if ic == NC_CH - 1 else None))
            w1 = (tg, t_s1) + ((sg_last[spar],) if sg_last[spar] else ())
            e1 = r.s("activation", sgb, bg.ap[0:128, 0:512], AF.Silu,
                     scale=scg[:, t:t + 1], wait=w1, inc=True)
            bg.free_tok = e1
            w2 = (tu, e1, t_s2) + ((q5_done[par],) if q5_done[par] else ())
            e2 = r.v("scalar_tensor_tensor", ht[:, 512 * hs:512 * hs + 512],
                     bu.ap[0:128, 0:512], scg[:, NT + t:NT + t + 1], sgb,
                     ALU.mult, ALU.mult, wait=w2, inc=True)
            bu.free_tok = e2
            sg_last[spar] = e2
        t_h = r.tok("vector")
        cs = 4 * par
        wq1 = (t_h,) + ((qa_last,) if qa_last else ())
        q1 = r.v("tensor_reduce", hqc[:, cs:cs + 1], ht, AX.X, ALU.max,
                 apply_absolute_value=True, wait=wq1, inc=True)
        q2 = r.v("tensor_scalar", hqc[:, cs + 1:cs + 2], hqc[:, cs:cs + 1],
                 1e-5, None, ALU.max, wait=(q1,), inc=True)
        q3 = r.v("reciprocal", hqc[:, cs + 2:cs + 3], hqc[:, cs + 1:cs + 2],
                 wait=(q2,), inc=True)
        q4 = r.v("tensor_scalar", hqc[:, cs + 3:cs + 4], hqc[:, cs + 2:cs + 3],
                 127.0, None, ALU.mult, wait=(q3,), inc=True)
        qa_last = r.v("tensor_scalar", ahv[:, t:t + 1], hqc[:, cs + 1:cs + 2],
                      1.0 / 127.0, None, ALU.mult, wait=(q2,), inc=True)
        w5 = (q4,) + ((prev_q6,) if prev_q6 else ())
        q5 = r.v("tensor_scalar", h16, ht, hqc[:, cs + 3:cs + 4], 1536.0,
                 ALU.mult, ALU.add, wait=w5, inc=True)
        q5_done[par] = q5
        w6 = (q5,) + ((prev_dma,) if prev_dma else ())
        prev_q6 = r.v("tensor_scalar", hqt, h16, 1536.0, None, ALU.subtract,
                      wait=w6, inc=True)
        prev_dma = r.dma(hq_r[:, t, :], hqt, wait=(prev_q6,), eng="gpsimd")
    t_ah = r.tok("vector")
    r.barrier()
    for n in ("wgT_l", "wuT_l", "scg", "ht0", "ht1", "sg0", "sg1", "h16",
              "hqt", "hqc", "xqfm", "qcols"):
        ar.free(n)
    r.barrier()
    ar.alloc("wdT_l", 16 * 1024)
    wd = ar.bf16("wdT_l", [128, 32, DM])
    t_wd = r.dma(wd, SC["wdT"].ap().rearrange("(c p) n -> p c n", p=128))
    ar.alloc("scd", NT)
    scd = ar.f32("scd", [128, NT])
    t_sd = r.v("tensor_scalar", scd, ahv, mw[:, 2:3], None, ALU.mult,
               wait=(t_ah, t_mw), inc=True)
    ar.free("ahinv")
    ar.alloc("hq_t0", 2048)
    ar.alloc("hq_t1", 2048)
    hqLs = [ar.bf16("hq_t0", [128, 4 * DM]), ar.bf16("hq_t1", [128, 4 * DM])]
    ar.alloc("hqT0", 16 * 128)
    ar.alloc("hqT1", 16 * 128)
    hqTs = [ar.bf16("hqT0", [128, 32, 128]), ar.bf16("hqT1", [128, 32, 128])]
    ar.alloc("x3t0", DM)
    ar.alloc("x3t1", DM)
    ar.alloc("x3t2", DM)
    x3ts = [ar.f32("x3t0", [128, DM]), ar.f32("x3t1", [128, DM]),
            ar.f32("x3t2", [128, DM])]
    ar.alloc("pst0", DM)
    ar.alloc("pst1", DM)
    ar.alloc("pst2", DM)
    psts = [ar.f32("pst0", [128, DM]), ar.f32("pst1", [128, DM]),
            ar.f32("pst2", [128, DM])]
    ar.alloc("sqt", DM)
    sqt = ar.f32("sqt", [128, DM])
    ar.alloc("lncol", 24)
    lnc_all = ar.f32("lncol", [128, 24])
    x3tm_r = SC["x3tm"].ap().rearrange("(t p) c -> p t c", p=128)
    y_r = ctx.y.ap().rearrange("(t p) c -> p t c", p=128)
    prev_y = {0: None, 1: None, 2: None}
    hql_free = {0: (t_sd,), 1: (t_sd,)}
    hqT_free = {0: None, 1: None}
    x3t_free = {0: (), 1: (), 2: ()}
    ly_prev = {0: None, 1: None, 2: None}
    prev_l1 = None
    for t in range(NT):
        par = t % 2
        p3 = t % 3
        hqL, hqT, x3t, pst = hqLs[par], hqTs[par], x3ts[p3], psts[p3]
        lnc = lnc_all[:, 8 * p3:8 * p3 + 8]
        ntok = 128 if t < NT - 1 else OWN - 128 * t
        t_h = r.dma(hqL, hq_r[:, t, :], wait=hql_free[par])
        t_x = r.dma(x3t, x3tm_r[:, t, :], wait=x3t_free[p3])
        t_tr = None
        for half in range(4):
            bank = ctx.tbank()
            tok = None
            for sub in range(8):
                hc = 8 * half + sub
                w0 = ((bank.free_tok,) if bank.free_tok else ()) + ((t_h,) if sub == 0 else ())
                tok = r.te("transpose", bank.ap[:, 128 * sub:128 * sub + 128],
                           hqL[:, 128 * hc:128 * hc + 128], ident,
                           wait=w0 if sub == 0 else (),
                           inc=(True if sub == 7 else None))
            ev = r.v("tensor_copy", hqT[:, 8 * half:8 * half + 8, :],
                     bank.ap.rearrange("p (c q) -> p c q", c=NC_CH),
                     wait=(tok,), inc=True)
            bank.free_tok = ev
        t_tr = r.tok("vector")
        hql_free[par] = (t_tr,)
        evs = []
        last_mm = None
        for fs in range(2):
            bank = ctx.fbank()
            tok = None
            for hc in range(32):
                w0 = (((bank.free_tok,) if bank.free_tok else ()) +
                      ((t_tr, t_wd) if hc == 0 else ()))
                tok = r.te("matmul", bank.ap[0:ntok, 0:512],
                           hqT[:, hc, 0:ntok],
                           wd[:, hc, 512 * fs:512 * fs + 512],
                           start=(hc == 0), stop=(hc == 31),
                           wait=w0, inc=(True if hc == 31 else None))
            last_mm = tok
            wev = (tok, t_x, t_sd) + ((prev_y[p3],) if prev_y[p3] else ())
            ev = r.v("scalar_tensor_tensor", pst[:, 512 * fs:512 * fs + 512],
                     bank.ap[0:128, 0:512], scd[:, t:t + 1],
                     x3t[:, 512 * fs:512 * fs + 512], ALU.mult, ALU.add,
                     wait=wev, inc=True)
            evs.append(ev)
            bank.free_tok = ev
        x3t_free[p3] = (evs[-1],)
        hqT_free[par] = (last_mm,)
        wl1 = tuple(evs) + ((prev_l1,) if prev_l1 else ()) + \
            ((ly_prev[p3],) if ly_prev[p3] else ())
        l1 = r.s("activation", sqt, pst, AF.Square, accum_out=lnc[:, 0:1],
                 wait=wl1, inc=True)
        prev_l1 = l1
        wl2 = tuple(evs) + ((ly_prev[p3],) if ly_prev[p3] else ())
        l2 = r.v("tensor_reduce", lnc[:, 1:2], pst, AX.X, ALU.add,
                 wait=wl2, inc=True)
        l3 = r.v("tensor_scalar", lnc[:, 2:3], lnc[:, 1:2], 1.0 / DM, None,
                 ALU.mult, wait=(l2,), inc=True)
        l4 = r.v("tensor_scalar", lnc[:, 3:4], lnc[:, 2:3], lnc[:, 2:3], EPS,
                 ALU.mult, ALU.subtract, wait=(l3,), inc=True)
        l5 = r.v("scalar_tensor_tensor", lnc[:, 4:5], lnc[:, 0:1], 1.0 / DM,
                 lnc[:, 3:4], ALU.mult, ALU.subtract, wait=(l1, l4), inc=True)
        l6a = r.v("reciprocal", lnc[:, 5:6], lnc[:, 4:5], wait=(l5,), inc=True)
        l6 = r.s("activation", lnc[:, 6:7], lnc[:, 5:6], AF.Sqrt,
                 wait=(l6a,), inc=True)
        l7 = r.v("tensor_scalar", lnc[:, 7:8], lnc[:, 2:3], lnc[:, 6:7], -1.0,
                 ALU.mult, ALU.mult, wait=(l6,), inc=True)
        ly = r.s("activation", pst, pst, AF.Identity, scale=lnc[:, 6:7],
                 bias=lnc[:, 7:8], wait=(l7,), inc=True)
        ly_prev[p3] = ly
        prev_y[p3] = r.dma(y_r[:, t, :], pst, wait=(ly,), eng="gpsimd")
    r.barrier()



_PROG_CACHE = {}


def _get_prog(stop_stage, taps):
    key = (stop_stage, taps)
    if key not in _PROG_CACHE:
        _PROG_CACHE[key] = build_program(stop_stage, taps)
    return _PROG_CACHE[key]


_PERM1 = np.zeros(1024, np.int64)
for _c in range(8):
    for _j in range(64):
        _PERM1[128 * _c + _j] = 128 * _c + 2 * _j
        _PERM1[128 * _c + 64 + _j] = 128 * _c + 2 * _j + 1
_PERM2 = np.concatenate([_PERM1, _PERM1 + 1024])


def _colvec(v):
    v = np.asarray(v, np.float32).reshape(-1)
    n = v.shape[0] // 128
    return np.ascontiguousarray(v.reshape(n, 128).T)


def _colmat(v):
    v = np.asarray(v, np.float32)
    v = v.reshape(v.shape[0] // 128, 128, -1)
    return np.ascontiguousarray(v.transpose(1, 0, 2))


def _atr_mask():
    import ml_dtypes
    ind = np.zeros((128, 96), np.float32)
    for j in range(96):
        ind[(j // 48) * 3 + (j % 3), j] = 1.0
    mm = np.zeros((128, 4 * 96), np.float32)
    mm[:, 0:96] = ind * float(2 ** 30)
    mm[:, 96:192] = ind
    mm[0, 192:288] = -float(2 ** 30)
    mm[0, 288:384] = 1.0
    return np.ascontiguousarray(mm.astype(ml_dtypes.bfloat16))


_ATR_MASK = None


def make_inmaps(inputs):
    import ml_dtypes
    global _ATR_MASK
    if _ATR_MASK is None:
        _ATR_MASK = _atr_mask()
    x = np.asarray(inputs["L0_attn_x"])
    in_maps = []
    for c in range(8):
        s, h = c // 2, c % 2
        start = OWN * h
        lo, hi = start - HALO, start + OWN + HALO
        shard = np.zeros((FW, DM), np.float32)
        a, b = max(lo, 0), min(hi, 4096)
        shard[a - lo:a - lo + (b - a)] = x[s, a:b]
        n_real = 4096 - start if h == 1 else OWN
        n_real = min(n_real, OWN)
        mrow = np.zeros((FW,), np.float32)
        mrow[HALO:HALO + n_real] = 1.0
        mask_full = np.broadcast_to(mrow, (128, FW)).astype(ml_dtypes.bfloat16)
        mact = np.zeros((NT * 128,), np.float32)
        mact[:n_real] = 1.0
        mask_tm = mact.reshape(NT, 128).T.copy()
        m = {
            "xshard": shard,
            "mask_full_bf": np.ascontiguousarray(mask_full),
            "mask_tm_bf": mask_tm.astype(ml_dtypes.bfloat16),
            "mask_tm_f32": mask_tm,
            "ascale": np.full((128, 1), np.asarray(inputs["L0_ffn_act_scale"]),
                              np.float32),
            "mska": _ATR_MASK,
            "cpe1_w": _colmat(np.asarray(inputs["L0_attn_cpe1_w"]).reshape(DM, 3)),
            "cpe1_b": _colvec(inputs["L0_attn_cpe1_b"]),
            "norm1_g": _colvec(inputs["L0_attn_norm1_g"]),
            "norm1_b": _colvec(inputs["L0_attn_norm1_b"]),
            "in_w": np.asarray(inputs["L0_attn_in_w"]),
            "in_b": _colvec(inputs["L0_attn_in_b"]),
            "actp_w": np.asarray(inputs["L0_attn_actp_w"]),
            "actp_b": _colvec(inputs["L0_attn_actp_b"]),
            "dwc_w": _colmat(np.asarray(inputs["L0_attn_dwc_w"]).reshape(DM, 3)),
            "dwc_b": _colvec(inputs["L0_attn_dwc_b"]),
            "qk_w": np.asarray(inputs["L0_attn_qk_w"]),
            "qk_b": _colvec(np.asarray(inputs["L0_attn_qk_b"])[_PERM2]),
            "lepe_w": _colmat(np.asarray(inputs["L0_attn_lepe_w"]).reshape(DM, 3)),
            "lepe_b": _colvec(inputs["L0_attn_lepe_b"]),
            "out_w": np.asarray(inputs["L0_attn_out_w"]),
            "out_b": _colvec(inputs["L0_attn_out_b"]),
            "cpe2_w": _colmat(np.asarray(inputs["L0_attn_cpe2_w"]).reshape(DM, 3)),
            "cpe2_b": _colvec(inputs["L0_attn_cpe2_b"]),
            "norm2_g": _colvec(inputs["L0_attn_norm2_g"]),
            "norm2_b": _colvec(inputs["L0_attn_norm2_b"]),
            "fc1_w": np.asarray(inputs["L0_attn_fc1_w"]),
            "fc1_b": _colvec(inputs["L0_attn_fc1_b"]),
            "fc2_w": np.asarray(inputs["L0_attn_fc2_w"]),
            "fc2_b": _colvec(inputs["L0_attn_fc2_b"]),
            "tern": np.asarray(inputs["L0_ffn_ternary_weight"]),
            "ffn_bias": _colvec(inputs["L0_ffn_bias"]),
            "Wq": np.asarray(inputs["L1_attn_Wq"]), "bq": _colvec(inputs["L1_attn_bq"]),
            "Wk": np.asarray(inputs["L1_attn_Wk"]), "bk": _colvec(inputs["L1_attn_bk"]),
            "Wv": np.asarray(inputs["L1_attn_Wv"]), "bv": _colvec(inputs["L1_attn_bv"]),
            "Wo": np.asarray(inputs["L1_attn_Wo"]), "bo": _colvec(inputs["L1_attn_bo"]),
            "w_gate": np.asarray(inputs["L1_ffn_w_gate"]),
            "w_up": np.asarray(inputs["L1_ffn_w_up"]),
            "w_down": np.asarray(inputs["L1_ffn_w_down"]),
            "wg_h": np.ascontiguousarray(
                np.asarray(inputs["L1_ffn_w_gate"])[2048 * h:2048 * (h + 1)]),
            "wu_h": np.ascontiguousarray(
                np.asarray(inputs["L1_ffn_w_up"])[2048 * h:2048 * (h + 1)]),
            "wd_h": np.ascontiguousarray(
                np.asarray(inputs["L1_ffn_w_down"])[512 * h:512 * (h + 1)]),
        }
        in_maps.append(m)
    return in_maps


def kernel(**inputs):
    stop = os.environ.get("K_STOP_STAGE") or None
    taps = tuple(t for t in os.environ.get("K_TAPS", "").split(",") if t)
    nc = _get_prog(stop, taps)
    in_maps = make_inmaps(inputs)
    res = run_bass_kernel_spmd(nc, in_maps, core_ids=list(range(8)))
    out = np.zeros((4, 4096, DM), np.float32)
    for c in range(8):
        s, h = c // 2, c % 2
        start = OWN * h
        n_real = min(4096 - start, OWN)
        y = res.results[c]["y"]
        out[s, start:start + n_real] = y[0:n_real]
    if taps or stop:
        kernel.last_results = res
    return out


# version 24 (final)
# speedup vs baseline: 1.0790x; 1.0122x over previous
import os
import contextlib
import numpy as np
import concourse.bass as bass
import concourse.mybir as mybir
from concourse.bass_utils import run_bass_kernel_spmd

F32 = mybir.dt.float32
F16 = mybir.dt.float16
BF16 = mybir.dt.bfloat16
I32 = mybir.dt.int32
AF = mybir.ActivationFunctionType
ALU = mybir.AluOpType
AX = mybir.AxisListType

OWN = 2064
HALO = 4
WIN = OWN + 2 * HALO
FW = 2176
NT = 17
NC_CH = 8
DM = 1024
GRPS = [(0, 512), (512, 512), (1024, 512), (1536, 512), (2048, 128)]
EPS = 1e-5

ENGS = ("sync", "tensor", "vector", "scalar", "gpsimd")


class Sem:
    __slots__ = ("name", "handle", "count", "last_tok")

    def __init__(self, name, handle):
        self.name = name
        self.handle = handle
        self.count = 0
        self.last_tok = None


class Tok:
    __slots__ = ("sem", "val")

    def __init__(self, sem, val):
        self.sem = sem
        self.val = val


class Arena:

    def __init__(self, handle, n):
        self.h = handle
        self.n = n
        self.blocks = {}
        self.free_list = [(0, n)]
        self.peak = 0

    def alloc(self, name, nf32, top=False):
        nf32 = (nf32 + 1) & ~1
        assert name not in self.blocks
        it = enumerate(self.free_list)
        if top:
            it = reversed(list(it))
        for i, (off, sz) in it:
            if sz >= nf32:
                if top:
                    self.blocks[name] = (off + sz - nf32, nf32)
                    if sz == nf32:
                        self.free_list.pop(i)
                    else:
                        self.free_list[i] = (off, sz - nf32)
                else:
                    self.blocks[name] = (off, nf32)
                    if sz == nf32:
                        self.free_list.pop(i)
                    else:
                        self.free_list[i] = (off + nf32, sz - nf32)
                used = sum(s for (_, s) in self.blocks.values())
                self.peak = max(self.peak, used)
                return self.blocks[name][0]
        raise MemoryError(f"SBUF arena OOM allocating {name} ({nf32} f32); "
                          f"live={ {k: v[1] for k, v in self.blocks.items()} }")

    def free(self, name):
        off, sz = self.blocks.pop(name)
        if not hasattr(self, "pending"):
            self.pending = []
        self.pending.append((off, sz))

    def flush_frees(self):
        for b in getattr(self, "pending", []):
            self.free_list.append(b)
        self.pending = []
        self.free_list.sort()
        out = []
        for b in self.free_list:
            if out and out[-1][0] + out[-1][1] == b[0]:
                out[-1] = (out[-1][0], out[-1][1] + b[1])
            else:
                out.append(list(b))
        self.free_list = [tuple(b) for b in out]

    def f32(self, name, shape):
        off, sz = self.blocks[name]
        n = int(np.prod(shape[1:]))
        assert n <= sz, (name, shape, sz)
        ap = self.h[:, off:off + n]
        if len(shape) == 2:
            return ap
        return ap.rearrange("p (a b) -> p a b", b=shape[-1]) if len(shape) == 3 else \
            ap.rearrange("p (a b c) -> p a b c", b=shape[-2], c=shape[-1])

    def bf16(self, name, shape):
        off, sz = self.blocks[name]
        n = int(np.prod(shape[1:]))
        assert n <= 2 * sz, (name, shape, sz)
        ap = self.h[:, off:off + (n + 1) // 2].bitcast(BF16)[:, 0:n]
        if len(shape) == 2:
            return ap
        return ap.rearrange("p (a b) -> p a b", b=shape[-1]) if len(shape) == 3 else \
            ap.rearrange("p (a b c) -> p a b c", b=shape[-2], c=shape[-1])

    def f16(self, name, shape):
        off, sz = self.blocks[name]
        n = int(np.prod(shape[1:]))
        assert n <= 2 * sz
        ap = self.h[:, off:off + (n + 1) // 2].bitcast(F16)[:, 0:n]
        if len(shape) == 2:
            return ap
        return ap.rearrange("p (a b) -> p a b", b=shape[-1])

    def i32(self, name, shape):
        off, sz = self.blocks[name]
        n = int(np.prod(shape[1:]))
        assert n <= sz
        ap = self.h[:, off:off + n].bitcast(I32)
        if len(shape) == 2:
            return ap
        return ap.rearrange("p (a b) -> p a b", b=shape[-1])


class Bank:
    __slots__ = ("ap", "free_tok", "idx")

    def __init__(self, ap, idx):
        self.ap = ap
        self.free_tok = None
        self.idx = idx


class R:

    def __init__(self, nc, sems):
        self.nc = nc
        self.q = {e: [] for e in ENGS}
        self.water = {e: {} for e in ENGS}
        self.sems = {}
        for name, handle in sems.items():
            self.sems[name] = Sem(name, handle)
        self.prog = {e: self.sems["p_" + e] for e in ENGS}
        allp = [v for k, v in sorted(self.sems.items()) if k.startswith("tok")]
        self.pool = {"tensor": allp[0:10], "vector": allp[10:20],
                     "scalar": allp[20:26], "gpsimd": allp[26:28]}
        self.dpool = {"sync": allp[28:48], "scalar": allp[48:56],
                      "gpsimd": allp[56:64]}
        self.pool_i = {e: 0 for e in self.pool}
        self.dpool_i = {"sync": 0, "gpsimd": 0, "scalar": 0}
        self.opcount = {e: 0 for e in ENGS}
        self.last_barrier = {e: 0 for e in ENGS}
        self.last_incs = {e: None for e in ENGS}
        self.last_tok = {e: (None, -1) for e in ENGS}
        self.dma_toks = []

    def _next_tok_sem(self, eng):
        p = self.pool[eng]
        s = p[self.pool_i[eng] % len(p)]
        self.pool_i[eng] += 1
        return s

    def _next_dma_sem(self, eng="sync"):
        p = self.dpool[eng]
        s = p[self.dpool_i[eng] % len(p)]
        self.dpool_i[eng] += 1
        return s

    def _wait(self, eng, tok):
        if tok is None:
            return
        w = self.water[eng]
        if w.get(tok.sem.name, -1) >= tok.val:
            return
        w[tok.sem.name] = tok.val
        h, v = tok.sem.handle, tok.val

        def fn(e):
            e.wait_ge(h, v)
        self.q[eng].append(fn)

    def op(self, eng, method, *args, wait=(), inc=None, inc_k=1, **kw):
        for t in wait:
            self._wait(eng, t)
        tok = None
        incs = []
        if inc is not None:
            sem = self._next_tok_sem(eng) if inc is True else inc
            sem.count += inc_k
            tok = Tok(sem, sem.count)
            incs.append((sem.handle, inc_k))

        import traceback as _tb
        site = "".join(_tb.format_stack(limit=4)[:-2])

        def fn(e):
            try:
                ins = getattr(e, method)(*args, **kw)
            except Exception:
                print("EMIT FAIL:", eng, method, "recorded at:\n", site)
                raise
            for (h, k) in incs:
                ins.then_inc(h, k)
        self.q[eng].append(fn)
        self.opcount[eng] += 1
        if method != "nop":
            self.last_incs[eng] = incs
            if tok is not None:
                self.last_tok[eng] = (tok, self.opcount[eng])
        return tok

    def tok(self, eng):
        lt, at = self.last_tok[eng]
        if lt is not None and at == self.opcount[eng]:
            return lt
        incs = self.last_incs[eng]
        assert incs is not None, f"no instruction to attach token on {eng}"
        sem = self._next_tok_sem(eng)
        sem.count += 1
        incs.append((sem.handle, 1))
        t = Tok(sem, sem.count)
        self.last_tok[eng] = (t, self.opcount[eng])
        return t

    def dma(self, out, in_, eng="sync", wait=(), **kw):
        sem = self._next_dma_sem(eng)
        for t in wait:
            self._wait(eng, t)
        prev = sem.last_tok
        if prev is not None:
            self._wait(eng, prev)
        sem.count += 16
        tok = Tok(sem, sem.count)
        incs = [(sem.handle, 16)]

        def fn(e):
            ins = e.dma_start(out=out, in_=in_, **kw)
            for (h, k) in incs:
                ins.then_inc(h, k)
        self.q[eng].append(fn)
        self.opcount[eng] += 1
        self.dma_toks.append(tok)
        sem.last_tok = tok
        self.last_incs[eng] = incs
        self.last_tok[eng] = (tok, self.opcount[eng])
        return tok

    def dmat(self, out, in_, eng="sync", wait=()):
        sem = self._next_dma_sem(eng)
        for t in wait:
            self._wait(eng, t)
        prev = sem.last_tok
        if prev is not None:
            self._wait(eng, prev)
        sem.count += 16
        tok = Tok(sem, sem.count)
        incs = [(sem.handle, 16)]

        def fn(e):
            ins = e.dma_start_transpose(out, in_)
            for (h, k) in incs:
                ins.then_inc(h, k)
        self.q[eng].append(fn)
        self.opcount[eng] += 1
        self.dma_toks.append(tok)
        sem.last_tok = tok
        self.last_incs[eng] = incs
        self.last_tok[eng] = (tok, self.opcount[eng])
        return tok

    def barrier(self):
        for hook in getattr(self, "hooks", []):
            hook()
        toks = []
        for e in ENGS:
            if self.opcount[e] > self.last_barrier[e] and self.last_incs[e]:
                toks.append((e, self.tok(e)))
        for e in ENGS:
            for (src, t) in toks:
                self._wait(e, t)
            for t in self.dma_toks:
                self._wait(e, t)
            self.last_barrier[e] = self.opcount[e]
        self.dma_toks = []

    def te(self, m, *a, **k):
        return self.op("tensor", m, *a, **k)

    def v(self, m, *a, **k):
        return self.op("vector", m, *a, **k)

    def s(self, m, *a, **k):
        return self.op("scalar", m, *a, **k)

    def g(self, m, *a, **k):
        return self.op("gpsimd", m, *a, **k)

    def emit_into(self, block):
        for ename in ENGS:
            fns = self.q[ename]
            if not fns:
                continue
            dec = getattr(block, "sync" if ename == "sync" else ename)

            def make(fns):
                def body(e):
                    for f in fns:
                        f(e)
                return body
            dec(make(fns))



def build_program(stop_stage=None, taps=()):
    nc = bass.Bass("TRN2", target_bir_lowering=False)

    IN = {}

    def inp(name, shape, dtype=F32):
        IN[name] = nc.dram_tensor(name, list(shape), dtype, kind="ExternalInput")
        return IN[name]

    inp("xshard", [FW, DM])
    inp("mask_full_bf", [128, FW], BF16)
    inp("mask_tm_bf", [128, NT], BF16)
    inp("mask_tm_f32", [128, NT])
    inp("ascale", [128, 1])
    inp("mska", [128, 4 * 96], BF16)
    inp("cpe1_w", [128, NC_CH, 3]); inp("cpe1_b", [128, NC_CH])
    inp("norm1_g", [128, NC_CH]); inp("norm1_b", [128, NC_CH])
    inp("in_w", [DM, DM]); inp("in_b", [128, NC_CH])
    inp("actp_w", [DM, DM]); inp("actp_b", [128, NC_CH])
    inp("dwc_w", [128, NC_CH, 3]); inp("dwc_b", [128, NC_CH])
    inp("qk_w", [DM, 2 * DM]); inp("qk_b", [128, 16])
    inp("lepe_w", [128, NC_CH, 3]); inp("lepe_b", [128, NC_CH])
    inp("out_w", [DM, DM]); inp("out_b", [128, NC_CH])
    inp("cpe2_w", [128, NC_CH, 3]); inp("cpe2_b", [128, NC_CH])
    inp("norm2_g", [128, NC_CH]); inp("norm2_b", [128, NC_CH])
    inp("fc1_w", [DM, 4 * DM]); inp("fc1_b", [128, 32])
    inp("fc2_w", [4 * DM, DM]); inp("fc2_b", [128, NC_CH])
    inp("tern", [DM, DM], I32); inp("ffn_bias", [128, NC_CH])
    for w in ("Wq", "Wk", "Wv", "Wo"):
        inp(w, [DM, DM])
    for b in ("bq", "bk", "bv", "bo"):
        inp(b, [128, NC_CH])
    inp("w_gate", [4 * DM, DM]); inp("w_up", [4 * DM, DM]); inp("w_down", [DM, 4 * DM])
    inp("wg_h", [2 * DM, DM]); inp("wu_h", [2 * DM, DM]); inp("wd_h", [DM // 2, 4 * DM])

    y_out = nc.dram_tensor("y", [FW, DM], F32, kind="ExternalOutput")

    SC = {}

    def scr(name, shape, dtype=F32):
        SC[name] = nc.dram_tensor("sc_" + name, list(shape), dtype)
        return SC[name]

    scr("x0fm", [DM, FW])
    scr("c1", [DM, FW])
    scr("xn", [DM, FW], BF16)
    scr("ares", [DM, FW], BF16)
    scr("xi", [DM, FW], BF16)
    scr("qel", [DM, FW], BF16)
    scr("krtm", [FW, DM], BF16)
    scr("vtm", [FW, DM], BF16)
    scr("kvred", [128, NC_CH * 132])
    scr("sel2", [NC_CH * 128, 128], BF16)
    scr("ksumsc", [128, NC_CH])
    scr("zrep", [DM, FW], BF16)
    scr("rintsc", [DM, FW], BF16)
    scr("woT", [DM, DM], BF16)
    scr("mw", [128, 8])
    scr("xqfm_sc", [DM, FW], BF16)
    scr("axinv", [128, NT])
    scr("ahinv", [128, NT])
    scr("kvbf", [NC_CH * 128, 132], BF16)
    scr("qrope", [DM, FW], BF16)
    scr("cc_in", [128, NC_CH * 132])
    scr("cc_out", [128, NC_CH * 132])
    scr("cc2_in", [128, 4])
    scr("cc2_out", [128, 4])
    scr("top", [DM, FW], BF16)
    scr("x1", [DM, FW])
    scr("x2", [DM, FW])
    scr("xn2", [DM, FW], BF16)
    scr("ps1", [DM, FW])
    scr("x1o", [DM, FW])
    scr("ps2", [DM, FW])
    scr("x2o", [DM, FW])
    scr("x2ob", [DM, FW], BF16)
    scr("qfm", [DM, FW], BF16)
    scr("kfm", [DM, FW], BF16)
    scr("vfm", [DM, FW], BF16)
    scr("vtml", [FW, DM], BF16)
    scr("vtma", [128 * 22, DM], BF16)
    scr("ofm", [DM, FW], BF16)
    scr("ps3", [DM, FW])
    scr("x3o", [DM, FW])
    scr("x3tm", [FW, DM])
    scr("xqfm", [DM, FW], BF16)
    scr("hq", [FW, 4 * DM], BF16)
    scr("wgT", [DM, 4 * DM], BF16)
    scr("wuT", [DM, 4 * DM], BF16)
    scr("wdT", [4 * DM, DM], BF16)
    scr("ps4tm", [FW, DM])

    taps_out = {}
    for t in taps:
        src = SC[t]
        taps_out[t] = nc.dram_tensor("tap_" + t, list(src.shape),
                                     src.dtype, kind="ExternalOutput")

    stack = contextlib.ExitStack()
    pool_sems = {}
    with (
        nc.sbuf_tensor("arena", [128, 52224], F32) as arena_h,
        nc.psum_tensor("pb0", [128, 512], F32) as pb0,
        nc.psum_tensor("pb1", [128, 512], F32) as pb1,
        nc.psum_tensor("pb2", [128, 512], F32) as pb2,
        nc.psum_tensor("pb3", [128, 512], F32) as pb3,
        nc.psum_tensor("pb4", [128, 512], F32) as pb4,
        nc.psum_tensor("pb5", [128, 512], F32) as pb5,
        nc.psum_tensor("pt0", [128, 1024], BF16) as pt0,
        nc.psum_tensor("pt1", [128, 1024], BF16) as pt1,
        nc.semaphore("s_dma") as h_dma,
        nc.semaphore("s_dmag") as h_dmag,
        nc.semaphore("s_pt") as h_pt,
        nc.semaphore("s_pv") as h_pv,
        nc.semaphore("s_ps") as h_ps,
        nc.semaphore("s_pg") as h_pg,
        nc.semaphore("s_pq") as h_pq,
        nc.semaphore("s_cc") as h_cc,
        nc.Block() as block,
    ):
        for i in range(64):
            pool_sems["tok%02d" % i] = stack.enter_context(nc.semaphore("tk%02d" % i))
        sems = {
            "dma_sync": h_dma, "dma_gpsimd": h_dmag,
            "p_tensor": h_pt, "p_vector": h_pv, "p_scalar": h_ps,
            "p_gpsimd": h_pg, "p_sync": h_pq, "cc": h_cc,
        }
        sems.update(pool_sems)
        r = R(nc, sems)
        ar = Arena(arena_h, 52224)
        r.hooks = [ar.flush_frees]
        fbanks = [Bank(b[:], i) for i, b in enumerate([pb0, pb1, pb2, pb3, pb4, pb5])]
        tbanks = [Bank(b[:], i) for i, b in enumerate([pt0, pt1])]

        ctx = Ctx(nc, r, ar, fbanks, tbanks, IN, SC, y_out)
        emit_stages(ctx, stop_stage)

        r.barrier()
        toks = []
        for t, dst in taps_out.items():
            toks.append(r.dma(dst.ap(), SC[t].ap()))
        for t in toks:
            r._wait("sync", t)

        r.emit_into(block)
        stack.close()
    return nc


class Ctx:
    def __init__(self, nc, r, ar, fbanks, tbanks, IN, SC, y_out):
        self.carry = {}
        self.nc = nc
        self.r = r
        self.ar = ar
        self.fbanks = fbanks
        self.tbanks = tbanks
        self.IN = IN
        self.SC = SC
        self.y = y_out
        self.fb_i = 0
        self.tb_i = 0

    def fbank(self):
        b = self.fbanks[self.fb_i % len(self.fbanks)]
        self.fb_i += 1
        return b

    def tbank(self):
        b = self.tbanks[self.tb_i % len(self.tbanks)]
        self.tb_i += 1
        return b



def load_fm_f32(ctx, dst_ap3, dram, ncols=FW):
    src = dram.ap().rearrange("(c p) t -> p c t", p=128)
    return ctx.r.dma(dst_ap3, src)


def store_fm(ctx, dram, src_ap3, wait=()):
    dst = dram.ap().rearrange("(c p) t -> p c t", p=128)
    return ctx.r.dma(dst, src_ap3, wait=wait, eng="gpsimd")


def load_wtiles(ctx, name, dram, rows, cols, dtype=BF16, scale_col=None,
                col_ap=None, sub1=False):
    r, ar = ctx.r, ctx.ar
    nch = rows // 128
    ar.alloc(name + "_stg", nch * cols)
    stg = ar.f32(name + "_stg", [128, nch, cols]) if dram.dtype == F32 else \
        ar.i32(name + "_stg", [128, nch, cols])
    src = dram.ap().rearrange("(c p) n -> p c n", p=128)
    if col_ap is not None:
        src = col_ap
    t_ld = r.dma(stg, src)
    ar.alloc(name, (nch * cols + 1) // 2)
    wb = ar.bf16(name, [128, nch, cols])
    if sub1:
        tok = r.v("tensor_scalar", wb, stg, 1.0, None, ALU.subtract,
                  wait=(t_ld,), inc=True)
    elif scale_col is not None:
        tok = r.v("tensor_scalar", wb, stg, scale_col, None, ALU.mult,
                  wait=(t_ld,), inc=True)
    else:
        tok = r.v("tensor_copy", wb, stg, wait=(t_ld,), inc=True)
    ar.free(name + "_stg")
    return wb, tok


def load_col(ctx, name, dram):
    r, ar = ctx.r, ctx.ar
    shape = list(dram.shape)
    n = int(np.prod(shape[1:]))
    if dram.dtype == BF16:
        ar.alloc(name, (n + 1) // 2)
        dst = ar.bf16(name, [128] + shape[1:])
    else:
        ar.alloc(name, n)
        dst = ar.f32(name, [128] + shape[1:])
    tok = r.dma(dst, dram.ap())
    return dst, tok


def gemm_fm(ctx, w_ap3, x_ap3_bf, nin, psum_consumer, wait_ops=(),
            groups=GRPS, nout=8):
    r = ctx.r
    for oc in range(nout):
        for (g0, gw) in groups:
            bank = ctx.fbank()
            tok = None
            for ic in range(nin):
                first = ic == 0
                tok = r.te("matmul", bank.ap[:, 0:gw],
                           w_ap3[:, ic, 128 * oc:128 * oc + 128],
                           x_ap3_bf[:, ic, g0:g0 + gw],
                           start=first, stop=(ic == nin - 1),
                           wait=(((bank.free_tok,) if bank.free_tok else ())
                                 + tuple(wait_ops)) if first else (),
                           inc=(True if ic == nin - 1 else None))
            free_tok = psum_consumer(oc, g0, gw, bank, tok)
            bank.free_tok = free_tok


def conv3_fm(ctx, out_bf3, in_ap3, w_cols, b_col, w1_plus1_col=None,
             in_is_bf=True, wait=(), silu_after=False, out_f32=None):
    r, ar = ctx.r, ctx.ar
    ar.alloc("conv_u", FW + 2)
    ub = ar.bf16("conv_u", [128, 2, FW])
    dst0 = out_bf3 if out_f32 is None else out_f32
    r.v("memset", dst0[:, :, 0:1], 0.0, inc=True)
    r.v("memset", dst0[:, :, FW - 1:FW], 0.0, inc=True)
    toks = []
    for c in range(NC_CH):
        par = c % 2
        u = ub[:, par:par + 1, :]
        w0 = w_cols[:, c, 0:1]
        w1 = w1_plus1_col[:, c:c + 1] if w1_plus1_col is not None else w_cols[:, c, 1:2]
        w2 = w_cols[:, c, 2:3]
        war = (toks[c - 2],) if c >= 2 else ()
        t0 = r.s("activation", u[:, 0, 1:FW], in_ap3[:, c, 0:FW - 1], AF.Identity,
                 bias=b_col[:, c:c + 1], scale=w0, wait=wait + war, inc=True)
        t1 = r.v("scalar_tensor_tensor", u[:, 0, 1:FW], in_ap3[:, c, 1:FW], w1,
                 u[:, 0, 1:FW], ALU.mult, ALU.add, wait=(t0,), inc=True)
        dst = out_bf3 if out_f32 is None else out_f32
        t2 = r.v("scalar_tensor_tensor", dst[:, c, 1:FW - 1], in_ap3[:, c, 2:FW], w2,
                 u[:, 0, 1:FW - 1], ALU.mult, ALU.add, wait=(t1,), inc=True)
        toks.append(t2)
        if silu_after:
            toks[-1] = r.s("activation", dst[:, c, 1:FW - 1], dst[:, c, 1:FW - 1],
                           AF.Silu, wait=(t2,), inc=True)
    ar.free("conv_u")
    return toks


def ln_fm(ctx, x_ap3_f32, sq_src_bf=None, ones_bf=None, name="ln",
          x_bf_for_sums=None, wait=()):
    r, ar = ctx.r, ctx.ar
    tmp_names = []
    if x_bf_for_sums is None:
        ar.alloc(name + "_xb", (NC_CH * FW + 1) // 2)
        x_bf_for_sums = ar.bf16(name + "_xb", [128, NC_CH, FW])
        t_cast = r.v("tensor_copy", x_bf_for_sums, x_ap3_f32, wait=wait, inc=True)
        tmp_names.append(name + "_xb")
    else:
        t_cast = None
    ar.alloc(name + "_sq", (NC_CH * FW + 1) // 2)
    sq = ar.bf16(name + "_sq", [128, NC_CH, FW])
    t_sq = r.s("activation", sq, x_ap3_f32, AF.Square, wait=wait, inc=True)
    tmp_names.append(name + "_sq")

    ar.alloc(name + "_r", FW)
    ar.alloc(name + "_mu", FW)
    r_full = ctx.ar.f32(name + "_r", [128, FW])
    mu_full = ctx.ar.f32(name + "_mu", [128, FW])
    ar.alloc(name + "_t", 512)
    ttile = ctx.ar.f32(name + "_t", [128, 512])

    last = None
    for (g0, gw) in GRPS:
        b_s = ctx.fbank()
        b_q = ctx.fbank()
        for ic in range(NC_CH):
            w = (b_s.free_tok, t_cast) if ic == 0 else ()
            r.te("matmul", b_s.ap[:, 0:gw], ones_bf, x_bf_for_sums[:, ic, g0:g0 + gw],
                 start=(ic == 0), stop=(ic == NC_CH - 1), wait=w)
        t_s = r.tok("tensor")
        for ic in range(NC_CH):
            w = (b_q.free_tok, t_sq) if ic == 0 else ()
            r.te("matmul", b_q.ap[:, 0:gw], ones_bf, sq[:, ic, g0:g0 + gw],
                 start=(ic == 0), stop=(ic == NC_CH - 1), wait=w)
        t_q = r.tok("tensor")
        t_mu = r.s("activation", mu_full[:, g0:g0 + gw], b_s.ap[:, 0:gw],
                   AF.Identity, scale=1.0 / DM, wait=(t_s,), inc=True)
        t_m2 = r.s("activation", ttile[:, 0:gw], b_s.ap[:, 0:gw],
                   AF.Square, scale=1.0 / DM,
                   wait=(last,) if last is not None else (), inc=True)
        t_var = r.v("scalar_tensor_tensor", ttile[:, 0:gw], b_q.ap[:, 0:gw],
                    1.0 / DM, ttile[:, 0:gw], ALU.mult, ALU.subtract,
                    wait=(t_q, t_m2), inc=True)
        b_s.free_tok = t_var
        b_q.free_tok = t_var
        t_e = r.v("tensor_scalar", ttile[:, 0:gw], ttile[:, 0:gw], EPS, None,
                  ALU.add, wait=(t_var,), inc=True)
        t_rc = r.v("reciprocal", ttile[:, 0:gw], ttile[:, 0:gw], wait=(t_e,), inc=True)
        last = r.s("activation", r_full[:, g0:g0 + gw], ttile[:, 0:gw], AF.Sqrt,
                   wait=(t_rc,), inc=True)
    ar.free(name + "_t")
    for t in tmp_names:
        ar.free(t)
    return r_full, mu_full, last



def emit_stages(ctx, stop_stage):
    order = ["prep", "s1_cpe1", "s2_actp", "s2b_in", "s3_qk", "s4_kv",
             "s5a_z", "s5b_attn", "s6_mlp", "s7_ffn0", "s8_l1qkv", "s9_l1attn",
             "s10a_wq", "s10b_ffn"]
    for st in order:
        globals()["stage_" + st](ctx)
        ctx.r.barrier()
        if stop_stage == st:
            return


def stage_prep(ctx):
    r, ar, IN = ctx.r, ctx.ar, ctx.IN
    r.v("memset", ar.h[:, 0:26112], 0.0, inc=True)
    r.g("memset", ar.h[:, 26112:52224], 0.0, inc=True)
    r.barrier()
    ar.alloc("ident", 64)
    ar.alloc("ones128", 64)
    ar.alloc("iota", 128)
    ar.alloc("pid", 1)
    ident = ar.bf16("ident", [128, 128])
    ones128 = ar.bf16("ones128", [128, 128])
    iota = ar.f32("iota", [128, 128])
    pid = ar.f32("pid", [128, 1])
    r.g("iota", iota, pattern=[[1, 128]], base=0, channel_multiplier=0,
        allow_small_or_imprecise_dtypes=True, inc=True)
    r.g("iota", pid, pattern=[[0, 1]], base=0, channel_multiplier=1,
        allow_small_or_imprecise_dtypes=True, inc=True)
    ar.alloc("rope", 4 * 4)
    rope = ar.f32("rope", [128, 16])
    cos_t, sin_t = rope[:, 0:4], rope[:, 4:8]
    ar.alloc("ropei", 4)
    ri = ar.f32("ropei", [128, 4])
    r.g("iota", ri, pattern=[[128, 4]], base=0, channel_multiplier=1,
        allow_small_or_imprecise_dtypes=True, inc=True)
    r.barrier()
    r.v("tensor_scalar", ident, iota, pid, None, ALU.is_equal, inc=True)
    r.v("memset", ones128, 1.0, inc=True)
    t_th = r.s("activation", rope[:, 8:12], ri, AF.Exp,
               scale=-float(np.log(10000.0) / 512.0), inc=True)
    ar.alloc("halfpi", 1)
    hp = ar.f32("halfpi", [128, 1])
    t_hp = r.v("memset", hp, float(np.pi / 2), inc=True)
    t_sn = r.s("activation", sin_t, rope[:, 8:12], AF.Sin, wait=(t_th,), inc=True)
    t_cs = r.s("activation", cos_t, rope[:, 8:12], AF.Sin, scale=-1.0, bias=hp,
               wait=(t_th, t_hp), inc=True)
    ar.alloc("rope2", 16)
    rope2 = ar.f32("rope2", [128, 16])
    c2v = rope2[:, 0:8].rearrange("p (a b) -> p a b", b=2)
    s2v = rope2[:, 8:16].rearrange("p (a b) -> p a b", b=2)
    prev = (t_sn, t_cs)
    for par in range(2):
        for half in range(2):
            tA = r.v("tensor_copy", c2v[64 * half:64 * half + 64, :, par],
                     cos_t[64 * par:64 * par + 64, 0:4], wait=prev, inc=True)
            tB = r.v("tensor_copy", s2v[64 * half:64 * half + 64, :, par],
                     sin_t[64 * par:64 * par + 64, 0:4], wait=(tA,), inc=True)
            prev = (tB,)
    ar.free("ropei")
    ar.free("halfpi")


def stage_s1_cpe1(ctx):
    r, ar, IN, SC = ctx.r, ctx.ar, ctx.IN, ctx.SC
    ident = ar.bf16("ident", [128, 128])
    ones128 = ar.bf16("ones128", [128, 128])

    ar.alloc("x0tm", NT * DM)
    x0tm = ar.f32("x0tm", [128, NT, DM])
    t_x0 = r.dma(x0tm, IN["xshard"].ap().rearrange("(t p) c -> p t c", p=128))
    ar.alloc("hitm", (NT * DM + 1) // 2)
    ar.alloc("lotm", (NT * DM + 1) // 2)
    hi = ar.bf16("hitm", [128, NT, DM])
    lo = ar.bf16("lotm", [128, NT, DM])
    t_hi = r.v("tensor_copy", hi, x0tm, wait=(t_x0,), inc=True)
    t_lo = r.v("tensor_tensor", lo, x0tm, hi, ALU.subtract, wait=(t_hi,), inc=True)
    ar.free("x0tm")
    r.barrier()

    ar.alloc("x0fm", NC_CH * FW)
    x0fm = ar.f32("x0fm", [128, NC_CH, FW])
    for tile in range(NT):
        bank = ctx.tbank()
        toks = []
        for cc in range(NC_CH):
            w = (bank.free_tok, t_hi) if cc == 0 else ()
            toks.append(r.te("transpose", bank.ap[:, 128 * cc:128 * cc + 128],
                             hi[:, tile, 128 * cc:128 * cc + 128], ident,
                             wait=w, inc=(True if cc == NC_CH - 1 else None)))
        t_tr = toks[-1]
        t_ev = r.v("tensor_copy",
                   x0fm[:, :, 128 * tile:128 * tile + 128],
                   bank.ap.rearrange("p (c t) -> p c t", c=NC_CH),
                   wait=(t_tr,), inc=True)
        bank.free_tok = t_ev
    for tile in range(NT):
        bank = ctx.tbank()
        toks = []
        for cc in range(NC_CH):
            w = (bank.free_tok, t_lo) if cc == 0 else ()
            toks.append(r.te("transpose", bank.ap[:, 128 * cc:128 * cc + 128],
                             lo[:, tile, 128 * cc:128 * cc + 128], ident,
                             wait=w, inc=(True if cc == NC_CH - 1 else None)))
        t_tr = toks[-1]
        t_ev = r.v("tensor_tensor",
                   x0fm[:, :, 128 * tile:128 * tile + 128],
                   x0fm[:, :, 128 * tile:128 * tile + 128],
                   bank.ap.rearrange("p (c t) -> p c t", c=NC_CH),
                   ALU.add, wait=(t_tr,), inc=True)
        bank.free_tok = t_ev
    ar.free("hitm")
    ar.free("lotm")
    t_x0fm = r.tok("vector")
    t_sp = store_fm(ctx, SC["x0fm"], x0fm, wait=(t_x0fm,))
    r.barrier()

    cpe1w, t_w = load_col(ctx, "cpe1w", IN["cpe1_w"])
    cpe1b, t_b = load_col(ctx, "cpe1b", IN["cpe1_b"])
    ar.alloc("cpe1w1p", NC_CH)
    w1p = ar.f32("cpe1w1p", [128, NC_CH])
    t_w1p = r.v("tensor_scalar", w1p, cpe1w[:, :, 1], 1.0, None, ALU.add,
                wait=(t_w,), inc=True)

    ar.alloc("c1", NC_CH * FW)
    c1 = ar.f32("c1", [128, NC_CH, FW])
    conv_toks = conv3_fm(ctx, None, x0fm, cpe1w, cpe1b, w1_plus1_col=w1p,
                         wait=(t_x0fm, t_w, t_b, t_w1p), out_f32=c1)
    t_c1 = r.tok("vector")
    t_c1s = store_fm(ctx, SC["c1"], c1, wait=(t_c1,))
    ar.free("x0fm")
    ar.free("cpe1w"); ar.free("cpe1b"); ar.free("cpe1w1p")
    r.barrier()

    rfull, mufull, t_stats = ln_fm(ctx, c1, ones_bf=ones128, name="n1",
                                   wait=(t_c1,))
    r.barrier()
    n1g, t_g = load_col(ctx, "n1g", IN["norm1_g"])
    n1b, t_bb = load_col(ctx, "n1b", IN["norm1_b"])
    ar.alloc("xn_t", NC_CH * FW)
    xt = ar.f32("xn_t", [128, NC_CH, FW])
    ar.alloc("xn", (NC_CH * FW + 1) // 2, top=True)
    xn = ar.bf16("xn", [128, NC_CH, FW])
    mub = mufull.unsqueeze(1).broadcast_to((128, NC_CH, FW))
    rb = rfull.unsqueeze(1).broadcast_to((128, NC_CH, FW))
    t1 = r.v("tensor_tensor", xt, c1, mub, ALU.subtract,
             wait=(t_stats, t_c1), inc=True)
    t2 = r.v("tensor_tensor", xt, xt, rb, ALU.mult, wait=(t1,), inc=True)
    for c in range(NC_CH):
        r.s("activation", xn[:, c, :], xt[:, c, :], AF.Identity,
            scale=n1g[:, c:c + 1], bias=n1b[:, c:c + 1],
            wait=(t2, t_g, t_bb), inc=True)
    t_xn = r.tok("scalar")
    ctx.carry["xn"] = (xn, t_xn)
    ar.free("xn_t")
    ar.free("c1")
    ar.free("n1_r"); ar.free("n1_mu")
    ar.free("n1g"); ar.free("n1b")



def load_fm_bf(ctx, name, dram, wait=()):
    r, ar = ctx.r, ctx.ar
    ar.alloc(name, (NC_CH * FW + 1) // 2)
    dst = ar.bf16(name, [128, NC_CH, FW])
    tok = r.dma(dst, dram.ap().rearrange("(c p) t -> p c t", p=128), wait=wait)
    return dst, tok


def load_fm_f32_t(ctx, name, dram, wait=()):
    r, ar = ctx.r, ctx.ar
    ar.alloc(name, NC_CH * FW)
    dst = ar.f32(name, [128, NC_CH, FW])
    tok = r.dma(dst, dram.ap().rearrange("(c p) t -> p c t", p=128), wait=wait)
    return dst, tok


def stage_s2_actp(ctx):
    r, ar, IN, SC = ctx.r, ctx.ar, ctx.IN, ctx.SC
    xn, t_xn = ctx.carry["xn"]
    ar.alloc("qk_stg", (NC_CH * FW + 1) // 2, top=True)
    stg = ar.f32("qk_stg", [128, NC_CH, 1024])
    t_ld = r.dma(stg, IN["qk_w"].ap().rearrange("(c p) n -> p c n", p=128)[:, :, 0:1024])
    ctx.carry["qk_stg"] = (stg, t_ld)
    w, t_w = load_wtiles(ctx, "actpw", IN["actp_w"], DM, DM)
    b, t_b = load_col(ctx, "actpb", IN["actp_b"])
    ar.alloc("ares", (NC_CH * FW + 1) // 2)
    ares = ar.bf16("ares", [128, NC_CH, FW])

    def consumer(oc, g0, gw, bank, te_tok):
        return r.s("activation", ares[:, oc, g0:g0 + gw], bank.ap[:, 0:gw],
                   AF.Silu, bias=b[:, oc:oc + 1], wait=(te_tok, t_b), inc=True)

    gemm_fm(ctx, w, xn, NC_CH, consumer, wait_ops=(t_xn, t_w))
    t_done = r.tok("scalar")
    store_fm(ctx, SC["ares"], ares, wait=(t_done,))
    for n in ("actpw", "actpb", "ares"):
        ar.free(n)


def stage_s2b_in(ctx):
    r, ar, IN, SC = ctx.r, ctx.ar, ctx.IN, ctx.SC
    xn, t_xn = ctx.carry.pop("xn")
    w, t_w = load_wtiles(ctx, "inw", IN["in_w"], DM, DM)
    b, t_b = load_col(ctx, "inb", IN["in_b"])
    dwcw, t_dw = load_col(ctx, "dwcw", IN["dwc_w"])
    dwcb, t_db = load_col(ctx, "dwcb", IN["dwc_b"])
    ar.alloc("mrow", (FW + 1) // 2)
    mv = ar.bf16("mrow", [128, FW])
    t_m = r.dma(mv, IN["mask_full_bf"].ap())
    ar.alloc("xipre", (NC_CH * FW + 1) // 2)
    xipre = ar.bf16("xipre", [128, NC_CH, FW])
    evac_toks = []

    def consumer(oc, g0, gw, bank, te_tok):
        t = r.s("activation", xipre[:, oc, g0:g0 + gw], bank.ap[:, 0:gw],
                AF.Identity, bias=b[:, oc:oc + 1], wait=(te_tok, t_b), inc=True)
        evac_toks.append(t)
        return t

    gemm_fm(ctx, w, xn, NC_CH, consumer, wait_ops=(t_xn, t_w))
    t_pre = r.tok("scalar")
    VLO, VHI = HALO + OWN - 32, HALO + OWN
    t_fix = t_pre
    for c in range(NC_CH):
        t_fix = r.v("tensor_tensor", xipre[:, c, VLO:VHI], xipre[:, c, VLO:VHI],
                    mv[:, VLO:VHI], ALU.mult, wait=(t_pre, t_m), inc=True)
    ar.alloc("xi", (NC_CH * FW + 1) // 2, top=True)
    xi = ar.bf16("xi", [128, NC_CH, FW])
    toks = conv3_fm(ctx, xi, xipre, dwcw, dwcb, wait=(t_fix, t_dw, t_db),
                    silu_after=True)
    t_xi = r.tok("scalar")
    ctx.carry["xi"] = (xi, t_xi)
    for n in ("xn", "inw", "inb", "dwcw", "dwcb", "mrow", "xipre"):
        ar.free(n)


def _rope_fm(ctx, out_bf, in_bf, cos_t, sin_t, t_in, name, eng="vector"):
    r, ar = ctx.r, ctx.ar
    ar.alloc(name + "_t", FW)
    tmp = ar.f32(name + "_t", [128, FW])
    toks = []
    prev = None
    for cc in range(NC_CH):
        cos0 = cos_t[0:64, cc:cc + 1]
        cos64 = cos_t[64:128, cc:cc + 1]
        sin0 = sin_t[0:64, cc:cc + 1]
        sin64 = sin_t[64:128, cc:cc + 1]
        e = in_bf[0:64, cc, :]
        o = in_bf[64:128, cc, :]
        if eng == "vector":
            t1 = r.v("tensor_scalar", tmp[0:64, :], o, sin64, None, ALU.mult,
                     wait=(t_in,) + ((prev,) if prev else ()), inc=True)
            t2 = r.v("scalar_tensor_tensor", out_bf[0:64, cc, :], e, cos0,
                     tmp[0:64, :], ALU.mult, ALU.subtract, wait=(t1,), inc=True)
            t3 = r.v("tensor_scalar", tmp[64:128, :], e, sin0, None, ALU.mult,
                     wait=(t2,), inc=True)
            t4 = r.v("scalar_tensor_tensor", out_bf[64:128, cc, :], o, cos64,
                     tmp[64:128, :], ALU.mult, ALU.add, wait=(t3,), inc=True)
        else:
            W = in_bf.shape[-1]
            t1 = r.op(eng, "tensor_tensor", tmp[0:64, :], o,
                      sin64.broadcast_to((64, W)), ALU.mult,
                      wait=(t_in,) + ((prev,) if prev else ()), inc=True)
            ta = r.op(eng, "tensor_tensor", out_bf[0:64, cc, :], e,
                      cos0.broadcast_to((64, W)), ALU.mult, wait=(t1,), inc=True)
            t2 = r.op(eng, "tensor_tensor", out_bf[0:64, cc, :],
                      out_bf[0:64, cc, :], tmp[0:64, :], ALU.subtract,
                      wait=(ta,), inc=True)
            t3 = r.op(eng, "tensor_tensor", tmp[64:128, :], e,
                      sin0.broadcast_to((64, W)), ALU.mult, wait=(t2,), inc=True)
            tb = r.op(eng, "tensor_tensor", out_bf[64:128, cc, :], o,
                      cos64.broadcast_to((64, W)), ALU.mult, wait=(t3,), inc=True)
            t4 = r.op(eng, "tensor_tensor", out_bf[64:128, cc, :],
                      out_bf[64:128, cc, :], tmp[64:128, :], ALU.add,
                      wait=(tb,), inc=True)
        prev = t4
        toks.append(t4)
    ar.free(name + "_t")
    return toks


def _fm_to_tm(ctx, tm_dst_bf, fm_src_bf, ident, t_src, mask_col=None, extra_wait=()):
    r = ctx.r
    toks = []
    for t in range(NT):
        col0 = HALO + 128 * t
        roff = 0
        if col0 + 128 > FW:
            roff = 32
            col0 = HALO + 128 * t - 32
        bank = ctx.tbank()
        tok = None
        for cc in range(NC_CH):
            w = ((bank.free_tok,) if bank.free_tok else ()) + ((t_src,) if cc == 0 else ())
            tok = r.te("transpose", bank.ap[:, 128 * cc:128 * cc + 128],
                       fm_src_bf[:, cc, col0:col0 + 128], ident,
                       wait=w if cc == 0 else (),
                       inc=(True if cc == NC_CH - 1 else None))
        nrows = 128 - roff if t < NT - 1 else OWN - 128 * t
        src = bank.ap.rearrange("p (c t2) -> p c t2", c=NC_CH)[roff:roff + nrows, :, :]
        dst = tm_dst_bf[0:nrows, t, :].rearrange("p (c t2) -> p c t2", c=NC_CH)
        if mask_col is not None:
            ev = r.v("tensor_scalar", dst, src, mask_col[0:nrows, t:t + 1], None,
                     ALU.mult, wait=(tok,) + tuple(extra_wait), inc=True)
        else:
            ev = r.v("tensor_copy", dst, src, wait=(tok,) + tuple(extra_wait), inc=True)
        bank.free_tok = ev
    return r.tok("vector")


def stage_s3_qk(ctx):
    r, ar, IN, SC = ctx.r, ctx.ar, ctx.IN, ctx.SC
    ident = ar.bf16("ident", [128, 128])
    rope2 = ar.f32("rope2", [128, 16])
    cos_t, sin_t = rope2[:, 0:8], rope2[:, 8:16]
    xi, t_xi = ctx.carry["xi"]
    qkb, t_qb = load_col(ctx, "qkb", IN["qk_b"])
    mtm, t_mtm = load_col(ctx, "mtm", IN["mask_tm_f32"])

    stg, t_ld = ctx.carry.pop("qk_stg")
    ar.alloc("qk_stg2", (NC_CH * FW + 1) // 2)
    stg2 = ar.f32("qk_stg2", [128, NC_CH, 1024])
    t_ld2 = r.dma(stg2, IN["qk_w"].ap().rearrange("(c p) n -> p c n", p=128)[:, :, 1024:2048])
    ar.alloc("qkw", NC_CH * 2048 // 2)
    wqk = ar.bf16("qkw", [128, NC_CH, 2048])
    t_w = None
    for oc2 in range(16):
        half, oc2h = (stg, oc2) if oc2 < 8 else (stg2, oc2 - 8)
        psrc = half[:, :, 128 * oc2h:128 * oc2h + 128].rearrange(
            "p c (j par) -> p c par j", par=2)
        pdst = wqk[:, :, 128 * oc2:128 * oc2 + 128].rearrange(
            "p c (par j) -> p c par j", par=2)
        t_w = r.v("tensor_copy", pdst, psrc,
                  wait=(t_ld,) if oc2 < 8 else (t_ld2,), inc=True)
    t_w = r.tok("vector")
    ar.free("qk_stg"); ar.free("qk_stg2")
    r.barrier()

    ar.alloc("tmpA0", 512)
    ar.alloc("tmpA1", 512)
    ar.alloc("tmpB0", 512)
    ar.alloc("tmpB1", 512)
    tmpAs = [ar.f32("tmpA0", [128, 512]), ar.f32("tmpA1", [128, 512])]
    tmpBs = [ar.f32("tmpB0", [128, 512]), ar.f32("tmpB1", [128, 512])]

    prev = {"p4": {0: None, 1: None}, "i": 0}

    def elu_consumer(out_bf, boff):
        def consumer(oc, g0, gw, bank, te_tok):
            par = prev["i"] % 2
            prev["i"] += 1
            tmpA, tmpB = tmpAs[par], tmpBs[par]
            lastp = prev["p4"][par]
            bcol = qkb[:, boff + oc:boff + oc + 1]
            w1 = (te_tok, t_qb) + ((lastp,) if lastp else ())
            p1 = r.v("tensor_scalar", tmpA[:, 0:gw], bank.ap[:, 0:gw], bcol, 0.0,
                     ALU.add, ALU.min, wait=w1, inc=True)
            p2 = r.s("activation", tmpB[:, 0:gw], tmpA[:, 0:gw], AF.Exp,
                     wait=(p1,) + ((lastp,) if lastp else ()), inc=True)
            p3 = r.v("tensor_scalar", tmpA[:, 0:gw], bank.ap[:, 0:gw], bcol, 0.0,
                     ALU.add, ALU.max, wait=(p2,), inc=True)
            p4 = r.v("tensor_tensor", out_bf[:, oc, g0:g0 + gw], tmpB[:, 0:gw],
                     tmpA[:, 0:gw], ALU.add, wait=(p3,), inc=True)
            prev["p4"][par] = p4
            return p3
        return consumer

    ar.alloc("qel", (NC_CH * FW + 1) // 2)
    qel = ar.bf16("qel", [128, NC_CH, FW])
    gemm_fm(ctx, wqk[:, :, 0:1024], xi, NC_CH, elu_consumer(qel, 0),
            wait_ops=(t_xi, t_w))
    t_q = r.tok("vector")
    ar.alloc("qrope", (NC_CH * FW + 1) // 2, top=True)
    qrope = ar.bf16("qrope", [128, NC_CH, FW])
    toks = _rope_fm(ctx, qrope, qel, cos_t, sin_t, t_q, "rq")
    t_qr = r.tok("vector")
    ctx.carry["qrope"] = (qrope, t_qr)
    r.barrier()
    ar.free("qel")
    r.barrier()

    ar.alloc("kel", (NC_CH * FW + 1) // 2)
    kel = ar.bf16("kel", [128, NC_CH, FW])
    gemm_fm(ctx, wqk[:, :, 1024:2048], xi, NC_CH, elu_consumer(kel, 8),
            wait_ops=(t_xi, t_w))
    t_k = r.tok("vector")
    ar.alloc("krope", (NC_CH * FW + 1) // 2)
    krope = ar.bf16("krope", [128, NC_CH, FW])
    toks = _rope_fm(ctx, krope, kel, cos_t, sin_t, t_k, "rk")
    t_kr = r.tok("vector")
    r.barrier()
    ar.free("kel"); ar.free("qkw")
    ar.free("tmpA0"); ar.free("tmpB0")
    ar.free("tmpA1"); ar.free("tmpB1")
    r.barrier()

    ar.alloc("krtm", (NT * DM + 1) // 2, top=True)
    krtm = ar.bf16("krtm", [128, NT, DM])
    tzk = r.v("memset", krtm[:, NT - 1, :], 0.0, inc=True)
    t1 = _fm_to_tm(ctx, krtm, krope, ident, t_kr, extra_wait=(tzk,))
    ctx.carry["krtm"] = (krtm, t1)
    ar.alloc("vtm", (NT * DM + 1) // 2, top=True)
    vtm = ar.bf16("vtm", [128, NT, DM])
    tzv = r.v("memset", vtm[:, NT - 1, :], 0.0, inc=True)
    t2 = _fm_to_tm(ctx, vtm, xi, ident, t_xi, mask_col=mtm, extra_wait=(tzv,))
    ctx.carry["vtm"] = (vtm, t2)
    r.barrier()
    for n in ("qkb", "mtm", "krope"):
        ar.free(n)


def stage_s4_kv(ctx):
    r, ar, IN, SC = ctx.r, ctx.ar, ctx.IN, ctx.SC
    krtm, t_k = ctx.carry.pop("krtm")
    vtm, t_v = ctx.carry.pop("vtm")
    mtmb, t_m = load_col(ctx, "mtmb", IN["mask_tm_bf"])
    ar.alloc("kvsb", 128 * NC_CH + 64)
    kvsb = ar.f32("kvsb", [128, NC_CH, 132])

    for hp in range(NC_CH):
        bank = ctx.fbank()
        tok = None
        for t in range(NT):
            w0 = ((bank.free_tok,) if bank.free_tok else ()) + (t_k, t_v, t_m)
            r.te("matmul", bank.ap[:, 0:128],
                 krtm[:, t, 128 * hp:128 * hp + 128],
                 vtm[:, t, 128 * hp:128 * hp + 128],
                 start=(t == 0), stop=False,
                 wait=w0 if t == 0 else ())
            tok = r.te("matmul", bank.ap[:, 128:129],
                       krtm[:, t, 128 * hp:128 * hp + 128],
                       mtmb[:, t:t + 1],
                       start=False, stop=(t == NT - 1),
                       inc=(True if t == NT - 1 else None))
        ev = r.v("tensor_copy", kvsb[:, hp, 0:129], bank.ap[:, 0:129],
                 wait=(tok,), inc=True)
        bank.free_tok = ev
    t_kv = r.tok("vector")
    t_st = r.dma(SC["cc_in"].ap().rearrange("p (c n) -> p c n", n=132),
                 kvsb, wait=(t_kv,))
    cc = r.sems["cc"]
    t_cc = r.op("gpsimd", "collective_compute", "AllReduce", ALU.add,
                replica_groups=[[0, 1], [2, 3], [4, 5], [6, 7]],
                ins=[SC["cc_in"].ap().opt()], outs=[SC["cc_out"].ap().opt()],
                wait=(t_st,), inc=cc)
    t_rd = r.dma(kvsb, SC["cc_out"].ap().rearrange("p (c n) -> p c n", n=132),
                 wait=(t_cc,))
    ar.alloc("kvbf", (NC_CH * 132 + 1) // 2, top=True)
    kvbf = ar.bf16("kvbf", [128, NC_CH, 132])
    t_cast = r.v("tensor_copy", kvbf, kvsb, wait=(t_rd,), inc=True)
    ctx.carry["kvbf"] = (kvbf, t_cast)
    ar.alloc("sel2", NC_CH * 64, top=True)
    sel2 = ar.bf16("sel2", [128, NC_CH, 128])
    tz = r.v("memset", sel2, 0.0, wait=(t_rd,), inc=True)
    stoks = []
    for cc_i in range(NC_CH):
        for m in range(2):
            srcap = kvsb[64 * m:64 * m + 64, cc_i, 128:129].broadcast_to((64, 64))
            stoks.append(r.v("tensor_copy",
                             sel2[64 * m:64 * m + 64, cc_i, 64 * m:64 * m + 64],
                             srcap, wait=(tz,), inc=True))
    t_sel = r.tok("vector")
    ctx.carry["sel2"] = (sel2, t_sel)
    r.barrier()
    for n in ("krtm", "vtm", "mtmb", "kvsb"):
        ar.free(n)



def stage_s5a_z(ctx):
    r, ar, IN, SC = ctx.r, ctx.ar, ctx.IN, ctx.SC
    qel, t_q = ctx.carry["qrope"]
    sel2, t_s = ctx.carry.pop("sel2")
    ar.alloc("epsrow", 64)
    epsl = ar.bf16("epsrow", [128, 128])
    ar.alloc("onesr", FW // 2 + 1)
    onesr = ar.bf16("onesr", [128, FW])
    t_e1 = r.v("memset", epsl[0:1, :], float(4096 * 1e-6), inc=True)
    t_e2 = r.v("memset", onesr[0:1, :], 1.0, inc=True)
    ar.alloc("zrep", (NC_CH * FW + 1) // 2, top=True)
    zrep = ar.bf16("zrep", [128, NC_CH, FW])
    ar.alloc("tmpZ0", 512)
    ar.alloc("tmpZ1", 512)
    tmpZs = [ar.f32("tmpZ0", [128, 512]), ar.f32("tmpZ1", [128, 512])]
    prevc = {0: None, 1: None}
    it = 0
    for cc in range(NC_CH):
        for (g0, gw) in GRPS:
            par = it % 2
            it += 1
            tmpZ = tmpZs[par]
            bank = ctx.fbank()
            w0 = ((bank.free_tok,) if bank.free_tok else ()) + (t_q, t_s, t_e1, t_e2)
            r.te("matmul", bank.ap[:, 0:gw], sel2[:, cc, :],
                 qel[:, cc, g0:g0 + gw], start=True, stop=False, wait=w0)
            tok = r.te("matmul", bank.ap[:, 0:gw], epsl[0:1, :],
                       onesr[0:1, g0:g0 + gw], start=False, stop=True, inc=True)
            ev = r.v("reciprocal", tmpZ[:, 0:gw], bank.ap[:, 0:gw],
                     wait=(tok,) + ((prevc[par],) if prevc[par] else ()), inc=True)
            prevc[par] = r.v("tensor_copy", zrep[:, cc, g0:g0 + gw], tmpZ[:, 0:gw],
                             wait=(ev,), inc=True)
            bank.free_tok = ev
    t_z = r.tok("vector")
    ctx.carry["zrep"] = (zrep, t_z)
    for n in ("sel2", "epsrow", "onesr", "tmpZ0", "tmpZ1"):
        ar.free(n)
    ar.alloc("ares_l", (NC_CH * FW + 1) // 2, top=True)
    ares = ar.bf16("ares_l", [128, NC_CH, FW])
    t_ar = r.dma(ares, SC["ares"].ap().rearrange("(c p) t -> p c t", p=128))
    ctx.carry["ares_l"] = (ares, t_ar)


def stage_s5b_attn(ctx):
    r, ar, IN, SC = ctx.r, ctx.ar, ctx.IN, ctx.SC
    qr, t_qr = ctx.carry.pop("qrope")
    xi, t_xi = ctx.carry.pop("xi")
    ares, t_ar = ctx.carry.pop("ares_l")
    zrep, t_z = ctx.carry.pop("zrep")
    kvbf, t_kv = ctx.carry.pop("kvbf")
    outw, t_w = load_wtiles(ctx, "outw", IN["out_w"], DM, DM)
    outb, t_b = load_col(ctx, "outb", IN["out_b"])
    lw, t_lw = load_col(ctx, "lepew", IN["lepe_w"])
    lb, t_lb = load_col(ctx, "lepeb", IN["lepe_b"])
    ar.alloc("mrow", 16)
    mv = ar.bf16("mrow", [128, 32])
    t_m = r.dma(mv, IN["mask_full_bf"].ap()[:, HALO + OWN - 32:HALO + OWN])
    r.barrier()
    ar.alloc("tmpT0", 512)
    ar.alloc("tmpT1", 512)
    tmpTs = [ar.f32("tmpT0", [128, 512]), ar.f32("tmpT1", [128, 512])]
    ar.alloc("tmpU0", 512)
    ar.alloc("tmpU1", 512)
    tmpUs = [ar.bf16("tmpU0", [128, 1024]), ar.bf16("tmpU1", [128, 1024])]
    ar.alloc("topg", 2048)
    topg = ar.bf16("topg", [128, NC_CH, 512])
    ar.alloc("c1g", 4096)
    c1g = ar.f32("c1g", [128, NC_CH, 512])
    x1g = c1g
    VLO, VHI = HALO + OWN - 32, HALO + OWN
    t_fix = t_xi
    for c in range(NC_CH):
        t_fix = r.v("tensor_tensor", xi[:, c, VLO:VHI], xi[:, c, VLO:VHI],
                    mv, ALU.mult, wait=(t_xi, t_m), inc=True)
    prev = {"x1s": None}
    lastT = {0: None, 1: None}
    lastU = {0: None, 1: None}

    c1r = SC["c1"].ap().rearrange("(c p) t -> p c t", p=128)
    x1r = SC["x1"].ap().rearrange("(c p) t -> p c t", p=128)
    for gi, (g0, gw) in enumerate(GRPS):
        t_c1 = r.dma(c1g[:, :, 0:gw], c1r[:, :, g0:g0 + gw],
                     wait=(prev["x1s"],) if prev["x1s"] else ())
        lo = max(g0, 1)
        hi = min(g0 + gw, FW - 1)
        mz = []
        if g0 == 0:
            mz.append(r.v("memset", topg[:, :, 0:1], 0.0, inc=True))
        if g0 + gw == FW:
            mz.append(r.v("memset", topg[:, :, gw - 1:gw], 0.0, inc=True))
        for cc in range(NC_CH):
            par = cc % 2
            tmpT = tmpTs[par]
            tmpU = tmpUs[par]
            bank = ctx.fbank()
            w0 = ((bank.free_tok,) if bank.free_tok else ()) + (t_qr, t_kv)
            r.te("matmul", bank.ap[0:64, 0:gw], kvbf[0:64, cc, 0:64],
                 qr[0:64, cc, g0:g0 + gw], start=True, stop=True, wait=w0)
            tok = r.te("matmul", bank.ap[64:128, 0:gw], kvbf[64:128, cc, 64:128],
                       qr[64:128, cc, g0:g0 + gw], start=True, stop=True, inc=True)
            wp1 = (tok, t_z) + ((lastT[par],) if lastT[par] else ())
            p1 = r.v("tensor_tensor", tmpT[:, 0:gw], bank.ap[:, 0:gw],
                     zrep[:, cc, g0:g0 + gw], ALU.mult, wait=wp1, inc=True)
            bank.free_tok = p1
            wu0 = (t_fix, t_lw, t_lb) + ((lastU[par],) if lastU[par] else ())
            u0 = r.s("activation", tmpU[:, lo - g0:hi - g0], xi[:, cc, lo - 1:hi - 1],
                     AF.Identity, scale=lw[:, cc, 0:1], bias=lb[:, cc:cc + 1],
                     wait=wu0, inc=True)
            u1 = r.v("scalar_tensor_tensor", tmpU[:, lo - g0:hi - g0],
                     xi[:, cc, lo:hi], lw[:, cc, 1:2], tmpU[:, lo - g0:hi - g0],
                     ALU.mult, ALU.add, wait=(u0, p1), inc=True)
            u2 = r.v("scalar_tensor_tensor", tmpU[:, lo - g0:hi - g0],
                     xi[:, cc, lo + 1:hi + 1], lw[:, cc, 2:3], tmpU[:, lo - g0:hi - g0],
                     ALU.mult, ALU.add, wait=(u1,), inc=True)
            p2 = r.v("tensor_tensor", tmpT[:, lo - g0:hi - g0], tmpT[:, lo - g0:hi - g0],
                     tmpU[:, lo - g0:hi - g0], ALU.add, wait=(u2,), inc=True)
            p3 = r.v("tensor_tensor", topg[:, cc, lo - g0:hi - g0],
                     tmpT[:, lo - g0:hi - g0], ares[:, cc, lo - g0 + g0:hi],
                     ALU.mult, wait=(p2, t_ar), inc=True)
            lastU[par] = p2
            lastT[par] = p3
        t_top = r.tok("vector")
        evs = []
        for oc in range(NC_CH):
            bank = ctx.fbank()
            tok = None
            for ic in range(NC_CH):
                w0 = ((bank.free_tok,) if bank.free_tok else ()) + (t_top, t_w)
                tok = r.te("matmul", bank.ap[:, 0:gw],
                           outw[:, ic, 128 * oc:128 * oc + 128],
                           topg[:, ic, 0:gw], start=(ic == 0), stop=(ic == NC_CH - 1),
                           wait=w0 if ic == 0 else (),
                           inc=(True if ic == NC_CH - 1 else None))
            ev = r.v("scalar_tensor_tensor", x1g[:, oc, 0:gw], bank.ap[:, 0:gw],
                     outb[:, oc:oc + 1], c1g[:, oc, 0:gw], ALU.add, ALU.add,
                     wait=(tok, t_b, t_c1), inc=True)
            evs.append(ev)
            bank.free_tok = ev
        t_x1 = r.tok("vector")
        prev["x1s"] = r.dma(x1r[:, :, g0:g0 + gw], x1g[:, :, 0:gw], wait=(t_x1,),
                            eng="gpsimd")
    r.barrier()
    for n in ("qrope", "xi", "ares_l", "zrep", "kvbf", "outw", "outb",
              "lepew", "lepeb", "mrow", "tmpT0", "tmpT1", "tmpU0", "tmpU1",
              "topg", "c1g"):
        ar.free(n)


def stage_s6_mlp(ctx):
    r, ar, IN, SC = ctx.r, ctx.ar, ctx.IN, ctx.SC
    ones128 = ar.bf16("ones128", [128, 128])
    x1, t_x1 = load_fm_f32_t(ctx, "x1_l", SC["x1"])
    cw, t_w = load_col(ctx, "cpe2w", IN["cpe2_w"])
    cb, t_b = load_col(ctx, "cpe2b", IN["cpe2_b"])
    ar.alloc("mrow", (FW + 1) // 2)
    mv = ar.bf16("mrow", [128, FW])
    t_m = r.dma(mv, IN["mask_full_bf"].ap())
    VLO, VHI = HALO + OWN - 32, HALO + OWN
    t_fix = t_x1
    for c in range(NC_CH):
        t_fix = r.v("tensor_tensor", x1[:, c, VLO:VHI], x1[:, c, VLO:VHI],
                    mv[:, VLO:VHI], ALU.mult, wait=(t_x1, t_m), inc=True)
    ar.alloc("u2", (NC_CH * FW + 1) // 2)
    u2 = ar.bf16("u2", [128, NC_CH, FW])
    toks = conv3_fm(ctx, u2, x1, cw, cb, wait=(t_fix, t_w, t_b))
    t_x2 = None
    for c in range(NC_CH):
        t_x2 = r.v("tensor_tensor", x1[:, c, :], x1[:, c, :], u2[:, c, :],
                   ALU.add, wait=(toks[c],), inc=True)
    t_x2 = r.tok("vector")
    x2 = x1
    ar.free("u2"); ar.free("cpe2w"); ar.free("cpe2b"); ar.free("mrow")
    r.barrier()
    rfull, mufull, t_stats = ln_fm(ctx, x2, ones_bf=ones128, name="n2",
                                   wait=(t_x2,))
    r.barrier()
    n2g, t_g = load_col(ctx, "n2g", IN["norm2_g"])
    n2b, t_bb = load_col(ctx, "n2b", IN["norm2_b"])
    ar.alloc("xn2", (NC_CH * FW + 1) // 2)
    xn2 = ar.bf16("xn2", [128, NC_CH, FW])
    ar.alloc("xt2", 4 * FW)
    xt = ar.f32("xt2", [128, 4, FW])
    mub = mufull.unsqueeze(1).broadcast_to((128, 4, FW))
    rb = rfull.unsqueeze(1).broadcast_to((128, 4, FW))
    t3 = None
    for hh in range(2):
        c0 = 4 * hh
        war = (t3,) if t3 else ()
        t1 = r.v("tensor_tensor", xt, x2[:, c0:c0 + 4, :], mub, ALU.subtract,
                 wait=(t_stats, t_x2) + war, inc=True)
        t2 = r.v("tensor_tensor", xt, xt, rb, ALU.mult, wait=(t1,), inc=True)
        for c in range(4):
            t3 = r.s("activation", xn2[:, c0 + c, :], xt[:, c, :], AF.Identity,
                     scale=n2g[:, c0 + c:c0 + c + 1],
                     bias=n2b[:, c0 + c:c0 + c + 1],
                     wait=(t2, t_g, t_bb), inc=True)
    t_xn2 = r.tok("scalar")
    ar.free("xt2"); ar.free("n2_r"); ar.free("n2_mu")
    ar.free("n2g"); ar.free("n2b")
    r.barrier()
    fc2b, t_f2b = load_col(ctx, "fc2b", IN["fc2_b"])
    t_pb = None
    for c in range(NC_CH):
        t_pb = r.v("tensor_scalar", x2[:, c, :], x2[:, c, :],
                   fc2b[:, c:c + 1], None, ALU.add,
                   wait=(t_f2b, t_xn2), inc=True)
    t_pb = r.tok("vector")
    x0r = SC["x0fm"].ap().rearrange("(c p) t -> p c t", p=128)
    ar.alloc("x0g0", 4096)
    x0g = ar.f32("x0g0", [128, NC_CH, 512])
    prev_add = None
    for gi, (g0, gw) in enumerate(GRPS):
        tl = r.dma(x0g[:, :, 0:gw], x0r[:, :, g0:g0 + gw],
                   wait=(prev_add,) if prev_add else ())
        prev_add = r.v("tensor_tensor", x2[:, :, g0:g0 + gw],
                       x2[:, :, g0:g0 + gw], x0g[:, :, 0:gw],
                       ALU.add, wait=(tl, t_pb), inc=True)
    t_x0a = r.tok("vector")
    fc1b, t_f1b = load_col(ctx, "fc1b", IN["fc1_b"])
    ar.alloc("hg0", 2048)
    ar.alloc("hg1", 2048)
    hgs = [ar.bf16("hg0", [128, NC_CH, 512]), ar.bf16("hg1", [128, NC_CH, 512])]
    f1r = IN["fc1_w"].ap().rearrange("(c p) n -> p c n", p=128)
    f2r = IN["fc2_w"].ap().rearrange("(c p) n -> p c n", p=128)
    ar.alloc("fstg", NC_CH * 512)
    stg = ar.f32("fstg", [128, NC_CH, 512])
    f1qs = [None, None]
    f2qs = [None, None]
    ar.alloc("f1q0", NC_CH * 512)
    ar.alloc("f1q1", NC_CH * 512)
    f1qs = [ar.bf16("f1q0", [128, NC_CH, 1024]), ar.bf16("f1q1", [128, NC_CH, 1024])]
    ar.alloc("f2q0", NC_CH * 512)
    _f2q = ar.bf16("f2q0", [128, NC_CH, 1024])
    f2qs = [_f2q, _f2q]
    t_cast = None
    t_wq = {}
    load_plan = []
    for q in range(4):
        qp = q % 2
        load_plan.append(("f1", q, qp))
        load_plan.append(("f2", q, qp))

    def emit_wload(which, q, qp):
        nonlocal t_cast
        dst = f1qs[qp] if which == "f1" else f2qs[qp]
        war = t_wq.get(("mmdone", which, q - 2 if which == "f1" else q - 1))
        toks = []
        for hhalf in range(2):
            if which == "f1":
                srcap = f1r[:, :, 1024 * q + 512 * hhalf:1024 * q + 512 * hhalf + 512]
            else:
                srcap = f2r[:, 8 * q:8 * q + 8, 512 * hhalf:512 * hhalf + 512]
            tl = r.dma(stg, srcap, wait=((t_cast,) if t_cast else ()))
            t_cast = r.v("tensor_copy",
                         dst.rearrange("p c (a b) -> p c a b", a=2)[:, :, hhalf, :],
                         stg, wait=(tl,) + ((war,) if (war and hhalf == 0) else ()),
                         inc=True)
            toks.append(t_cast)
        t_wq[(which, q)] = toks[-1]

    emit_wload(*load_plan[0])
    emit_wload(*load_plan[1])
    nload = 2
    for q in range(4):
        qp = q % 2
        f1q, f2q, hg = f1qs[qp], f2qs[qp], hgs[qp]
        tc1 = t_wq[("f1", q)]
        tc2 = t_wq[("f2", q)]
        last_f1mm = None
        last_f2mm = None
        for (g0, gw) in GRPS:
            gtoks = []
            for hc in range(NC_CH):
                bank = ctx.fbank()
                tok = None
                for ic in range(NC_CH):
                    w0 = ()
                    if ic == 0:
                        w0 = ((bank.free_tok,) if bank.free_tok else ()) + \
                            (t_xn2, tc1)
                    tok = r.te("matmul", bank.ap[:, 0:gw],
                               f1q[:, ic, 128 * hc:128 * hc + 128],
                               xn2[:, ic, g0:g0 + gw],
                               start=(ic == 0), stop=(ic == NC_CH - 1),
                               wait=w0, inc=(True if ic == NC_CH - 1 else None))
                last_f1mm = tok
                ge = r.s("activation", hg[:, hc, 0:gw], bank.ap[:, 0:gw], AF.Gelu,
                         bias=fc1b[:, 8 * q + hc:8 * q + hc + 1],
                         wait=(tok, t_f1b) +
                         ((t_wq.get(("hgw", qp)),) if t_wq.get(("hgw", qp)) else ()),
                         inc=True)
                gtoks.append(ge)
                bank.free_tok = ge
            for oc in range(NC_CH):
                bank = ctx.fbank()
                tok = None
                for hc in range(NC_CH):
                    w0 = ()
                    if hc == 0:
                        w0 = ((bank.free_tok,) if bank.free_tok else ()) + \
                            tuple(gtoks) + (tc2,)
                    tok = r.te("matmul", bank.ap[:, 0:gw],
                               f2q[:, hc, 128 * oc:128 * oc + 128],
                               hg[:, hc, 0:gw],
                               start=(hc == 0), stop=(hc == NC_CH - 1),
                               wait=w0, inc=(True if hc == NC_CH - 1 else None))
                last_f2mm = tok
                ev = r.v("scalar_tensor_tensor", x2[:, oc, g0:g0 + gw],
                         bank.ap[:, 0:gw], 1.0, x2[:, oc, g0:g0 + gw],
                         ALU.mult, ALU.add, wait=(tok, t_x0a), inc=True)
                bank.free_tok = ev
            t_wq[("hgw", qp)] = last_f2mm
        t_wq[("mmdone", "f1", q)] = last_f1mm
        t_wq[("mmdone", "f2", q)] = last_f2mm
        while nload < len(load_plan) and nload <= 2 * q + 3:
            emit_wload(*load_plan[nload])
            nload += 1
    t_ps = r.tok("vector")
    ctx.carry["ps1"] = (x2, t_ps)
    r.barrier()
    for n in ("hg0", "hg1", "fc1b", "fc2b", "xn2", "x0g0", "fstg",
              "f1q0", "f1q1", "f2q0"):
        ar.free(n)


def outer_ln(ctx, x_ap3, t_in, ones_bf, name):
    r, ar = ctx.r, ctx.ar
    rfull, mufull, t_stats = ln_fm(ctx, x_ap3, ones_bf=ones_bf, name=name,
                                   wait=(t_in,))
    r.barrier()
    nch = x_ap3.shape[1]
    W = x_ap3.shape[2]
    mub = mufull.unsqueeze(1).broadcast_to((128, nch, W))
    rb = rfull.unsqueeze(1).broadcast_to((128, nch, W))
    t1 = r.v("tensor_tensor", x_ap3, x_ap3, mub, ALU.subtract,
             wait=(t_stats, t_in), inc=True)
    tok = r.v("tensor_tensor", x_ap3, x_ap3, rb, ALU.mult, wait=(t1,),
              inc=True)
    ar.free(name + "_r"); ar.free(name + "_mu")
    return tok


def stage_s7_ffn0(ctx):
    r, ar, IN, SC = ctx.r, ctx.ar, ctx.IN, ctx.SC
    ones128 = ar.bf16("ones128", [128, 128])
    x, t_x = ctx.carry.pop("ps1")
    t_ln = outer_ln(ctx, x, t_x, ones128, "ol1")
    r.barrier()
    asc, t_a = load_col(ctx, "asc", IN["ascale"])
    ar.alloc("scol", 2)
    scol = ar.f32("scol", [128, 2])
    t1 = r.v("tensor_scalar", scol[:, 0:1], asc, 1e-5, None, ALU.max,
             wait=(t_a,), inc=True)
    t2 = r.v("reciprocal", scol[:, 1:2], scol[:, 0:1], wait=(t1,), inc=True)
    ar.alloc("rint", (NC_CH * FW + 1) // 2)
    rint = ar.bf16("rint", [128, NC_CH, FW])
    ar.alloc("qtmp", (NC_CH * FW + 1) // 2)
    qh = ar.f16("qtmp", [128, NC_CH, FW])
    a1 = r.v("tensor_scalar", qh, x, scol[:, 1:2], 1536.0, ALU.mult, ALU.add,
             wait=(t2, t_ln), inc=True)
    a2 = r.v("tensor_scalar", rint, qh, 1536.0, 127.0, ALU.subtract, ALU.min,
             wait=(a1,), inc=True)
    t_q = r.v("tensor_scalar", rint, rint, -127.0, None, ALU.max,
              wait=(a2,), inc=True)
    r.barrier()
    ar.free("qtmp")
    ar.alloc("tstg", NC_CH * DM)
    tstg = ar.i32("tstg", [128, NC_CH, DM])
    t_tl = r.dma(tstg, IN["tern"].ap().rearrange("(c p) n -> p c n", p=128))
    ar.alloc("tnat", (NC_CH * DM + 1) // 2)
    tnat = ar.bf16("tnat", [128, NC_CH, DM])
    t_tc = r.v("tensor_scalar", tnat, tstg, 1.0, None, ALU.subtract,
               wait=(t_tl,), inc=True)
    ar.alloc("ternT", (NC_CH * DM + 1) // 2)
    ternT = ar.bf16("ternT", [128, NC_CH, DM])
    ident = ar.bf16("ident", [128, 128])
    for oc in range(NC_CH):
        bank = ctx.tbank()
        tok = None
        for ic in range(NC_CH):
            w0 = ((bank.free_tok,) if bank.free_tok else ()) + ((t_tc,) if ic == 0 else ())
            tok = r.te("transpose", bank.ap[:, 128 * ic:128 * ic + 128],
                       tnat[:, oc, 128 * ic:128 * ic + 128], ident,
                       wait=w0 if ic == 0 else (),
                       inc=(True if ic == NC_CH - 1 else None))
        ev = r.v("tensor_copy",
                 ternT[:, :, 128 * oc:128 * oc + 128],
                 bank.ap.rearrange("p (c t) -> p c t", c=NC_CH),
                 wait=(tok,), inc=True)
        bank.free_tok = ev
    r.barrier()
    ar.free("tstg"); ar.free("tnat")
    ffnb, t_fb = load_col(ctx, "ffnb", IN["ffn_bias"])
    prev_ev = {"t": None}

    def consumer(oc, g0, gw, bank, te_tok):
        w0 = (te_tok, t_fb, t2) + ((prev_ev["t"],) if prev_ev["t"] else ())
        e1 = r.v("scalar_tensor_tensor", x[:, oc, g0:g0 + gw], bank.ap[:, 0:gw],
                 scol[:, 0:1], x[:, oc, g0:g0 + gw], ALU.mult, ALU.add,
                 wait=w0, inc=True)
        e2 = r.v("tensor_scalar", x[:, oc, g0:g0 + gw], x[:, oc, g0:g0 + gw],
                 ffnb[:, oc:oc + 1], None, ALU.add, wait=(e1,), inc=True)
        prev_ev["t"] = e2
        return e1

    gemm_fm(ctx, ternT, rint, NC_CH, consumer, wait_ops=(t_q,))
    t_ps2 = r.tok("vector")
    r.barrier()
    ar.free("rint"); ar.free("ternT"); ar.free("ffnb")
    ar.free("asc"); ar.free("scol")
    r.barrier()
    t_ln2 = outer_ln(ctx, x, t_ps2, ones128, "ol2")
    store_fm(ctx, SC["x2o"], x, wait=(t_ln2,))
    ar.alloc("mrow", (FW + 1) // 2)
    mv = ar.bf16("mrow", [128, FW])
    t_m = r.dma(mv, IN["mask_full_bf"].ap())
    ar.alloc("x2ob", (NC_CH * FW + 1) // 2, top=True)
    xb = ar.bf16("x2ob", [128, NC_CH, FW])
    tb = None
    for c in range(NC_CH):
        tb = r.v("tensor_tensor", xb[:, c, :], x[:, c, :], mv, ALU.mult,
                 wait=(t_ln2, t_m), inc=True)
    tb = r.tok("vector")
    ctx.carry["x2ob"] = (xb, tb)
    r.barrier()
    for n in ("x1_l", "mrow"):
        ar.free(n)



def transpose_w(ctx, dst_bf3, src_bf3, ident, t_src, scale=None):
    r = ctx.r
    for oc in range(NC_CH):
        bank = ctx.tbank()
        tok = None
        for ic in range(NC_CH):
            w0 = ((bank.free_tok,) if bank.free_tok else ()) + ((t_src,) if ic == 0 else ())
            tok = r.te("transpose", bank.ap[:, 128 * ic:128 * ic + 128],
                       src_bf3[:, oc, 128 * ic:128 * ic + 128], ident,
                       wait=w0 if ic == 0 else (),
                       inc=(True if ic == NC_CH - 1 else None))
        ev = r.v("tensor_copy", dst_bf3[:, :, 128 * oc:128 * oc + 128],
                 bank.ap.rearrange("p (c t) -> p c t", c=NC_CH),
                 wait=(tok,), inc=True)
        bank.free_tok = ev
    return r.tok("vector")


def load_wT(ctx, name, dram, ident, scale=None):
    r, ar = ctx.r, ctx.ar
    ar.alloc(name + "_stg", NC_CH * DM)
    stg = ar.f32(name + "_stg", [128, NC_CH, DM])
    t_ld = r.dma(stg, dram.ap().rearrange("(c p) n -> p c n", p=128))
    ar.alloc(name + "_nat", (NC_CH * DM + 1) // 2)
    nat = ar.bf16(name + "_nat", [128, NC_CH, DM])
    if scale is not None:
        t_c = r.v("tensor_scalar", nat, stg, scale, None, ALU.mult,
                  wait=(t_ld,), inc=True)
    else:
        t_c = r.v("tensor_copy", nat, stg, wait=(t_ld,), inc=True)
    ar.free(name + "_stg")
    ar.alloc(name, (NC_CH * DM + 1) // 2)
    dst = ar.bf16(name, [128, NC_CH, DM])
    tok = transpose_w(ctx, dst, nat, ident, t_c)
    ar.free(name + "_nat")
    return dst, tok


def stage_s8_l1qkv(ctx):
    r, ar, IN, SC = ctx.r, ctx.ar, ctx.IN, ctx.SC
    ident = ar.bf16("ident", [128, 128])
    xb, t_x = ctx.carry.pop("x2ob")
    SCALE = 0.125
    ar.alloc("pstg", (NC_CH * FW + 1) // 2, top=True)
    pstg = ar.f32("pstg", [128, NC_CH, DM])
    wseq = ["Wq", "Wk", "Wv", "Wo"]
    pend = {"t": r.dma(pstg, IN["Wq"].ap().rearrange("(c p) n -> p c n", p=128))}

    def wT_from_stg(idx, wname, scale=None):
        t_ld = pend["t"]
        ar.alloc(wname + "T_nat", (NC_CH * DM + 1) // 2)
        nat = ar.bf16(wname + "T_nat", [128, NC_CH, DM])
        if scale is not None:
            t_c = r.v("tensor_scalar", nat, pstg, scale, None, ALU.mult,
                      wait=(t_ld,), inc=True)
        else:
            t_c = r.v("tensor_copy", nat, pstg, wait=(t_ld,), inc=True)
        if idx + 1 < 4:
            pend["t"] = r.dma(pstg,
                              IN[wseq[idx + 1]].ap().rearrange("(c p) n -> p c n", p=128),
                              wait=(t_c,))
        ar.alloc(wname + "T", (NC_CH * DM + 1) // 2)
        dst = ar.bf16(wname + "T", [128, NC_CH, DM])
        tok = transpose_w(ctx, dst, nat, ident, t_c)
        ar.free(wname + "T_nat")
        return dst, tok

    def proj(idx, wname, bname, out_sc, bscale=None):
        w, t_w = wT_from_stg(idx, wname, scale=(SCALE if bscale else None))
        b, t_b = load_col(ctx, "b_" + bname, IN[bname])
        if bscale:
            t_b = r.v("tensor_scalar", b, b, SCALE, None, ALU.mult,
                      wait=(t_b,), inc=True)
        ar.alloc("o_" + wname, (NC_CH * FW + 1) // 2)
        out = ar.bf16("o_" + wname, [128, NC_CH, FW])

        def consumer(oc, g0, gw, bank, te_tok):
            return r.s("activation", out[:, oc, g0:g0 + gw], bank.ap[:, 0:gw],
                       AF.Identity, bias=b[:, oc:oc + 1], wait=(te_tok, t_b),
                       inc=True)

        gemm_fm(ctx, w, xb, NC_CH, consumer, wait_ops=(t_x, t_w))
        t_d = r.tok("scalar")
        if out_sc is not None:
            store_fm(ctx, SC[out_sc], out, wait=(t_d,))
        r.barrier()
        ar.free(wname + "T"); ar.free("b_" + bname)
        return out, t_d

    qo, _ = proj(0, "Wq", "bq", "qfm", bscale=True)
    ar.free("o_Wq")
    r.barrier()
    ko, _ = proj(1, "Wk", "bk", "kfm")
    ar.free("o_Wk")
    r.barrier()
    vo, t_v = proj(2, "Wv", "bv", None)
    ar.free("x2ob")
    r.barrier()
    ar.alloc("vtml", (NT * DM + 1) // 2, top=True)
    vtml = ar.bf16("vtml", [128, NT, DM])
    tz1 = r.v("memset", vtml[:, NT - 1, :], 0.0, inc=True)
    t1 = _fm_to_tm(ctx, vtml, vo, ident, t_v, extra_wait=(tz1,))
    ctx.carry["vtml"] = (vtml, t1)
    ar.alloc("vtma", (22 * DM + 1) // 2, top=True)
    vtma = ar.bf16("vtma", [128, 22, DM])
    tz2 = r.v("memset", vtma[96:128, :, :], 0.0, inc=True)
    tzb = r.v("memset", vtma[64:96, 21, :], 0.0, wait=(tz2,), inc=True)
    tz3 = r.v("memset", vtma[32:64, 21, :], 0.0, wait=(tzb,), inc=True)
    for a in range(22):
        off = HALO + 96 * a
        nb = 2 if a < 21 else 1
        rows = 48 * nb
        bank = ctx.tbank()
        tok = None
        for cc in range(NC_CH):
            inap = vo[:, cc, off:off + rows]
            w0 = ((bank.free_tok,) if bank.free_tok else ()) + ((t_v,) if cc == 0 else ())
            tok = r.te("transpose", bank.ap[0:rows, 128 * cc:128 * cc + 128],
                       inap, ident, wait=w0 if cc == 0 else (),
                       inc=(True if cc == NC_CH - 1 else None))
        ev = r.v("tensor_copy",
                 vtma[0:rows, a, :].rearrange("p (c t) -> p c t", c=NC_CH),
                 bank.ap.rearrange("p (c t) -> p c t", c=NC_CH)[0:rows, :, :],
                 wait=(tok, tz3), inc=True)
        bank.free_tok = ev
    t2 = r.tok("vector")
    ctx.carry["vtma"] = (vtma, t2)
    r.barrier()
    ar.free("o_Wv")
    r.barrier()
    woT, t_wo = wT_from_stg(3, "Wo")
    store_fm(ctx, SC["woT"], woT, wait=(t_wo,))
    r.barrier()
    ar.free("WoT"); ar.free("pstg")



def _build_blockmask(ctx, name, width, nblk):
    r, ar = ctx.r, ctx.ar
    iota = ar.f32("iota", [128, 128])
    pid = ar.f32("pid", [128, 1])
    ar.alloc(name, 2 * width + 8)
    mm = ar.bf16(name, [128, 4 * width + 16])
    mL = mm[:, 0:width]
    mR = mm[:, width:2 * width]
    crow = mm[:, 2 * width:3 * width]
    ones1 = mm[:, 3 * width:4 * width]
    lo = mm[:, 4 * width:4 * width + 8]
    ar.alloc(name + "_t", 2 * width)
    ta = ar.f32(name + "_t", [128, 2 * width])
    a, b = ta[:, 0:width], ta[:, width:2 * width]
    t0 = r.v("tensor_scalar", lo[:, 0:2].bitcast(F32)[:, 0:1], pid, 16.0, None,
             ALU.mult, inc=True)
    locol = lo[:, 0:2].bitcast(F32)[:, 0:1]
    t1 = r.v("tensor_scalar", a, iota[:, 0:width], locol, None, ALU.is_ge,
             wait=(t0,), inc=True)
    t2 = r.v("tensor_scalar", b, iota[:, 0:width], locol, 16.0, ALU.subtract,
             ALU.is_lt, wait=(t1,), inc=True)
    t3 = r.v("tensor_tensor", a, a, b, ALU.mult, wait=(t2,), inc=True)
    t4 = r.v("tensor_scalar", mL, a, float(2 ** 30), None, ALU.mult,
             wait=(t3,), inc=True)
    t5 = r.v("tensor_copy", mR, a, wait=(t4,), inc=True)
    t6 = r.v("memset", crow[0:1, :], -float(2 ** 30), wait=(t5,), inc=True)
    t7 = r.v("memset", ones1[0:1, :], 1.0, wait=(t6,), inc=True)
    ar.free(name + "_t")
    tok = ctx.r.tok("vector")
    return (mL, mR, crow, ones1), tok


def _load_atrous_mask(ctx):
    r, ar, IN = ctx.r, ctx.ar, ctx.IN
    width = 96
    ar.alloc("mska", 2 * width + 8)
    mm = ar.bf16("mska", [128, 4 * width + 16])
    t = r.dma(mm[:, 0:4 * width], IN["mska"].ap())
    return (mm[:, 0:width], mm[:, width:2 * width],
            mm[:, 2 * width:3 * width], mm[:, 3 * width:4 * width]), t


def _l1_scores(ctx, h, qf, kf, mpanels, t_mask, widths, pslab, zcols,
               ident, t_in, atrous, ps_war):
    r = ctx.r
    m, hc = h % 2, h // 2
    ntile = len(widths)
    nslot = 4
    W = pslab.shape[2]
    mL, mR, crow, ones1 = mpanels
    bank = None
    for ti, w in enumerate(widths):
        slot = ti % nslot
        if slot == 0:
            bank = ctx.fbank()
        off = HALO + (96 if atrous else 128) * ti
        qap = qf[64 * m:64 * m + 64, hc, off:off + w]
        kap = kf[64 * m:64 * m + 64, hc, off:off + w]
        w0 = ((bank.free_tok,) if bank.free_tok else ()) + ((t_in, t_mask) if slot == 0 else ())
        r.te("matmul", bank.ap[0:w, 128 * slot:128 * slot + w],
             mL[0:8, 0:w], mR[0:8, 0:w], start=True, stop=False, wait=w0)
        r.te("matmul", bank.ap[0:w, 128 * slot:128 * slot + w],
             crow[0:1, 0:w], ones1[0:1, 0:w], start=False, stop=False)
        tok = r.te("matmul", bank.ap[0:w, 128 * slot:128 * slot + w], qap, kap,
                   start=False, stop=True,
                   inc=(True if (slot == nslot - 1 or ti == ntile - 1) else None))
        if slot == nslot - 1 or ti == ntile - 1:
            ev = None
            for k in range(slot + 1):
                tk = ti - slot + k
                wk = widths[tk]
                ev = r.s("activation", pslab[0:wk, tk, 0:wk],
                         bank.ap[0:wk, 128 * k:128 * k + wk], AF.Exp,
                         wait=(tok,) + ((ps_war,) if ps_war else ()), inc=True)
            bank.free_tok = ev
    t_exp = r.tok("scalar")
    ntile = len(widths)
    zr = zcols[:, 0:ntile]
    t4 = r.v("tensor_reduce", zr, pslab, AX.X, ALU.add, wait=(t_exp,), inc=True)
    t4b = r.v("tensor_scalar", zr, zr, 1e-30, None, ALU.max, wait=(t4,),
              inc=True)
    t5 = r.v("reciprocal", zr, zr, wait=(t4b,), inc=True)
    zexp = zr.unsqueeze(-1).broadcast_to((128, ntile, W))
    t6 = r.v("scalar_tensor_tensor", pslab, pslab, 1.0, zexp, ALU.mult, ALU.mult,
             wait=(t5,), inc=True)
    return t6


def _l1_av(ctx, h, vt, widths, pslab, ptslab, orow, t_war, ident, t6,
           atrous, row0):
    r = ctx.r
    ntile = len(widths)
    nslot = 4
    W = pslab.shape[2]
    tbank = None
    last_tr = None
    for ti, w in enumerate(widths):
        slot = ti % nslot
        if slot == 0:
            tbank = ctx.tbank()
        w0 = ((tbank.free_tok,) if tbank.free_tok else ()) + ((t6,) if slot == 0 else ())
        tok = r.te("transpose", tbank.ap[0:W, 128 * slot:128 * slot + 128],
                   pslab[:, ti, :], ident, wait=w0,
                   inc=(True if (slot == nslot - 1 or ti == ntile - 1) else None))
        if slot == nslot - 1 or ti == ntile - 1:
            n = slot + 1
            t0 = ti - slot
            qn = ptslab.shape[2]
            src = tbank.ap.rearrange("p (s q) -> p s q", q=128)[0:W, 0:n, 0:qn]
            ev = r.v("tensor_copy", ptslab[0:W, t0:t0 + n, :], src,
                     wait=(tok,), inc=True)
            tbank.free_tok = ev
            last_tr = tok
    t_pt = r.tok("vector")
    abank = None
    for ti, w in enumerate(widths):
        slot = ti % nslot
        if slot == 0:
            abank = ctx.fbank()
        krows = w
        w0 = ((abank.free_tok,) if abank.free_tok else ()) + ((t_pt,) if slot == 0 else ())
        tok = r.te("matmul", abank.ap[0:64, 128 * slot:128 * slot + w],
                   vt[0:krows, ti, 64 * h:64 * h + 64],
                   ptslab[0:krows, ti, 0:w], start=True, stop=True,
                   wait=w0, inc=(True if (slot == nslot - 1 or ti == ntile - 1) else None))
        if slot == nslot - 1 or ti == ntile - 1:
            n = slot + 1
            t0 = ti - slot
            last_ev = None
            for k in range(n):
                tk = t0 + k
                wk = widths[tk]
                off = HALO + (96 if atrous else 128) * tk
                srcap = abank.ap[0:64, 128 * k:128 * k + wk]
                dstap = orow[row0:row0 + 64, off:off + wk]
                if atrous:
                    last_ev = r.v("tensor_tensor", dstap, dstap, srcap,
                                  ALU.add, wait=(tok, t_war), inc=True)
                else:
                    last_ev = r.s("activation", dstap, srcap,
                                  AF.Identity, wait=(tok, t_war), inc=True)
            abank.free_tok = last_ev
    t_done = r.tok("vector" if atrous else "scalar")
    return t_done, last_tr


def stage_s9_l1attn(ctx):
    r, ar, IN, SC = ctx.r, ctx.ar, ctx.IN, ctx.SC
    ident = ar.bf16("ident", [128, 128])
    qf, t_q = load_fm_bf(ctx, "qf_l", SC["qfm"])
    kf, t_k = load_fm_bf(ctx, "kf_l", SC["kfm"])
    vtml, t_vl = ctx.carry.pop("vtml")
    vtma, t_va = ctx.carry.pop("vtma")
    ar.alloc("ofm_sb", (NC_CH * FW + 1) // 2, top=True)
    ofm_sb = ar.bf16("ofm_sb", [128, NC_CH, FW])
    mpl, t_m1 = _build_blockmask(ctx, "mskl", 128, 8)
    mpa, t_m2 = _load_atrous_mask(ctx)
    ar.alloc("pslabL", (NT * 128 + 1) // 2)
    pslabL = ar.bf16("pslabL", [128, NT, 128])
    ar.alloc("pslabA", (22 * 96 + 1) // 2)
    pslabA = ar.bf16("pslabA", [128, 22, 96])
    ar.alloc("ptslabL", (NT * 128 + 1) // 2)
    ptslabL = ar.bf16("ptslabL", [128, NT, 128])
    ar.alloc("ptslabA", (22 * 96 + 1) // 2)
    ptslabA = ar.bf16("ptslabA", [128, 22, 96])
    ar.alloc("zcolsL", 2 * NT)
    zcolsL = ar.f32("zcolsL", [128, 2 * NT])
    ar.alloc("zcolsA", 44)
    zcolsA = ar.f32("zcolsA", [128, 44])
    t_z0 = r.v("memset", ofm_sb, 0.0, inc=True)
    t_z1 = r.v("memset", pslabL, 0.0, wait=(t_z0,), inc=True)
    t_z2 = r.v("memset", pslabA, 0.0, wait=(t_z1,), inc=True)
    t_z3 = r.v("memset", ptslabA[:, :, :], 0.0, wait=(t_z2,), inc=True)
    r.barrier()
    widthsL = [128] * 16 + [16]
    widthsA = [96] * 21 + [48]
    psL_war = None
    psA_war = None
    for h in range(16):
        m = h % 2
        orow = ofm_sb[:, h // 2, :]
        t6L = _l1_scores(ctx, h, qf, kf, mpl, t_m1, widthsL, pslabL,
                         zcolsL, ident, t_q, False, psL_war)
        t6A = _l1_scores(ctx, h, qf, kf, mpa, t_m2, widthsA, pslabA,
                         zcolsA, ident, t_k, True, psA_war)
        t_loc, psL_war = _l1_av(ctx, h, vtml, widthsL, pslabL, ptslabL,
                                orow, t_z3, ident, t6L, False, 64 * m)
        t_atr, psA_war = _l1_av(ctx, h, vtma, widthsA, pslabA, ptslabA,
                                orow, t_loc, ident, t6A, True, 64 * m)
    r.barrier()
    for n in ("qf_l", "kf_l", "vtml", "vtma", "mskl", "mska",
              "pslabL", "pslabA", "ptslabL", "ptslabA", "zcolsL", "zcolsA"):
        ar.free(n)
    t_ofm = ctx.r.tok("vector")
    r.barrier()
    ones128 = ar.bf16("ones128", [128, 128])
    x, t_x = load_fm_f32_t(ctx, "x2o_l", SC["x2o"])
    ofm, t_o = ofm_sb, t_ofm
    ar.alloc("woT", (NC_CH * DM + 1) // 2)
    woT = ar.bf16("woT", [128, NC_CH, DM])
    t_w = r.dma(woT, SC["woT"].ap().rearrange("(c p) n -> p c n", p=128))
    bo, t_b = load_col(ctx, "bo_l", IN["bo"])
    prev_ev = {"t": None}

    def consumer(oc, g0, gw, bank, te_tok):
        w0 = (te_tok, t_b, t_x) + ((prev_ev["t"],) if prev_ev["t"] else ())
        ev = r.v("scalar_tensor_tensor", x[:, oc, g0:g0 + gw], bank.ap[:, 0:gw],
                 bo[:, oc:oc + 1], x[:, oc, g0:g0 + gw], ALU.add, ALU.add,
                 wait=w0, inc=True)
        prev_ev["t"] = ev
        return ev

    gemm_fm(ctx, woT, ofm, NC_CH, consumer, wait_ops=(t_o, t_w))
    t_ps3 = r.tok("vector")
    r.barrier()
    ar.free("ofm_sb"); ar.free("woT"); ar.free("bo_l")
    r.barrier()
    t_ln3 = outer_ln(ctx, x, t_ps3, ones128, "ol3")
    store_fm(ctx, SC["x3o"], x, wait=(t_ln3,))
    r.barrier()
    ar.free("x2o_l")



def _round_f16(ctx, dst, src, f16tmp, wait=(), post=None):
    r = ctx.r
    t1 = r.v("tensor_scalar", f16tmp, src, 1536.0, None, ALU.add,
             wait=wait, inc=True)
    if post is None:
        return r.v("tensor_scalar", dst, f16tmp, 1536.0, None, ALU.subtract,
                   wait=(t1,), inc=True)
    s2, op1 = post
    return r.v("tensor_scalar", dst, f16tmp, 1536.0, s2, ALU.subtract, op1,
               wait=(t1,), inc=True)


def wq_pass1_half(ctx, dram, nrows_h, ncols, acc, t_acc):
    r, ar = ctx.r, ctx.ar
    ones128 = ar.bf16("ones128", [128, 128])
    wr = dram.ap().rearrange("(c p) n -> p c n", p=128)
    ncs = nrows_h // 128
    chunks = []
    for c0 in range(0, ncs, 8):
        cw = min(8, ncs - c0)
        for n0 in range(0, ncols, 1024):
            chunks.append((c0, cw, n0))
    stgs = [ar.f32("wstg0", [128, 8, 1024]), ar.f32("wstg1", [128, 8, 1024])]
    wabs = [ar.bf16("wabs0", [128, 8, 1024]), ar.bf16("wabs1", [128, 8, 1024])]
    t_ld = [None, None]
    pend = ctx.carry.pop("wq1_pend", {})
    t_ab = [pend.get(0), pend.get(1)]
    t_mm = [pend.get("m0"), pend.get("m1")]
    for i, (c0, cw, n0) in enumerate(chunks):
        par = i % 2
        t_ld[par] = r.dma(stgs[par][:, 0:cw, :],
                          wr[:, c0:c0 + cw, n0:n0 + 1024],
                          wait=((t_ab[par],) if t_ab[par] else ()),
                          eng="scalar")
        t_ab[par] = r.s("activation", wabs[par][:, 0:cw, :], stgs[par][:, 0:cw, :],
                        AF.Abs,
                        wait=(t_ld[par],) + ((t_mm[par],) if t_mm[par] else ()),
                        inc=True)
        bank = ctx.fbank()
        tok = None
        for ic in range(cw):
            for gg in range(2):
                first = ic == 0 and gg == 0
                w0 = ((((bank.free_tok,) if bank.free_tok else ()) + (t_ab[par],))
                      if first else ())
                tok = r.te("matmul", bank.ap[:, 0:512], ones128,
                           wabs[par][:, ic, 512 * gg:512 * gg + 512],
                           start=first, stop=(ic == cw - 1 and gg == 1),
                           wait=w0,
                           inc=(True if (ic == cw - 1 and gg == 1) else None))
        t_mm[par] = tok
        t_acc = r.v("tensor_tensor", acc, acc, bank.ap[:, 0:512], ALU.add,
                    wait=(tok, t_acc), inc=True)
        bank.free_tok = t_acc
    ctx.carry["wq1_pend"] = {0: t_ab[0], 1: t_ab[1],
                             "m0": t_mm[0], "m1": t_mm[1]}
    return t_acc


def wq_pass2(ctx, dram, rows, cols, out_sc, sinv, t_si):
    r, ar, SC = ctx.r, ctx.ar, ctx.SC
    ident = ar.bf16("ident", [128, 128])
    nrc = rows // 1024
    ncc = cols // 1024
    nit = nrc * ncc
    wr = dram.ap().rearrange("(c p) n -> p c n", p=128)
    out_r = SC[out_sc].ap().rearrange("(c p) n -> p c n", p=128)
    stgs = [ar.f32("wstg0", [128, NC_CH, 1024]), ar.f32("wstg1", [128, NC_CH, 1024])]
    w16 = ar.f16("wq16", [128, NC_CH, 1024])
    wqb = ar.bf16("wqb", [128, NC_CH, 1024])
    wts = [ar.bf16("wabs0", [128, NC_CH, 1024]), ar.bf16("wabs1", [128, NC_CH, 1024])]
    t_ld = [None, None]
    t_q3p = [None, None]
    t_q1 = None
    t_q3 = None
    last_tr = None
    t_ev = [None, None]
    pend = ctx.carry.pop("wq_pend", {})
    t_q3p[0] = pend.get(0)
    t_q3p[1] = pend.get(1)
    t_wdma = [pend.get("d0"), pend.get("d1")]
    for rc in range(nit):
        par = rc % 2
        rr, cc2 = rc // ncc, rc % ncc
        t_ld[par] = r.dma(stgs[par], wr[:, 8 * rr:8 * rr + 8,
                                       1024 * cc2:1024 * cc2 + 1024],
                          wait=((t_q3p[par],) if t_q3p[par] else ()),
                          eng="scalar")
        t_q1 = r.v("tensor_scalar", w16, stgs[par], sinv, 1536.0,
                   ALU.mult, ALU.add,
                   wait=(t_ld[par], t_si) + ((t_q3,) if t_q3 else ()), inc=True)
        t_q2 = r.v("tensor_scalar", stgs[par], w16, 1536.0, 1.0, ALU.subtract,
                   ALU.min, wait=(t_q1,), inc=True)
        t_q3 = r.v("tensor_scalar", wqb, stgs[par], -1.0, None, ALU.max,
                   wait=(t_q2,) + ((last_tr,) if last_tr else ()), inc=True)
        t_q3p[par] = t_q3
        wpar = rc % 2
        for ch in range(NC_CH):
            tb = ctx.tbank()
            tok2 = None
            for sub in range(NC_CH):
                w0 = ((tb.free_tok,) if tb.free_tok else ()) + \
                    ((t_q3,) if sub == 0 else ()) + \
                    ((t_wdma[wpar],) if (sub == 0 and ch == 0 and t_wdma[wpar])
                     else ())
                tok2 = r.te("transpose", tb.ap[:, 128 * sub:128 * sub + 128],
                            wqb[:, sub, 128 * ch:128 * ch + 128], ident,
                            wait=w0 if sub == 0 else (),
                            inc=(True if sub == NC_CH - 1 else None))
            t_ev[wpar] = r.v("tensor_copy", wts[wpar][:, ch, :], tb.ap,
                             wait=(tok2,) + ((t_wdma[wpar],) if (ch == 0 and
                                             t_wdma[wpar]) else ()), inc=True)
            tb.free_tok = t_ev[wpar]
            last_tr = tok2
        t_wdma[wpar] = r.dma(
            out_r[:, 8 * cc2:8 * cc2 + 8, 1024 * rr:1024 * rr + 1024],
            wts[wpar], wait=(t_ev[wpar],), eng="gpsimd")
    ctx.carry["wq_pend"] = {0: t_q3p[0], 1: t_q3p[1],
                            "d0": t_wdma[0], "d1": t_wdma[1]}


def stage_s10a_wq(ctx):
    r, ar, IN, SC = ctx.r, ctx.ar, ctx.IN, ctx.SC
    ar.alloc("x3_l", NC_CH * FW, top=True)
    x3 = ar.f32("x3_l", [128, NC_CH, FW])
    t_x3 = r.dma(x3, SC["x3o"].ap().rearrange("(c p) t -> p c t", p=128))
    ctx.carry["x3_l"] = (x3, t_x3)
    ar.alloc("wacc", 512)
    acc3 = ar.f32("wacc", [128, 512])
    ar.alloc("mc3", 16)
    mc = ar.f32("mc3", [128, 16])
    ar.alloc("wstg0", NC_CH * 1024)
    ar.alloc("wstg1", NC_CH * 1024)
    ar.alloc("wabs0", (NC_CH * 1024 + 1) // 2)
    ar.alloc("wabs1", (NC_CH * 1024 + 1) // 2)
    specs = [("wg_h", 2 * DM, DM, "w_gate", 4 * DM, DM, "wgT", 0),
             ("wu_h", 2 * DM, DM, "w_up", 4 * DM, DM, "wuT", 1),
             ("wd_h", DM // 2, 4 * DM, "w_down", DM, 4 * DM, "wdT", 2)]
    t_red = None
    for i, (hname, hr, hc, fname, fr, fc, osc, mi) in enumerate(specs):
        t0 = r.v("memset", acc3, 0.0,
                 wait=((t_red,) if t_red else ()), inc=True)
        t_acc = wq_pass1_half(ctx, IN[hname], hr, hc, acc3, t0)
        t_red = r.v("tensor_reduce", mc[:, mi:mi + 1], acc3, AX.X, ALU.add,
                    wait=(t_acc,), inc=True)
    t_p1 = r.tok("vector")
    t_absd = r.tok("scalar")
    ctx.carry["wq_pend"] = {0: t_absd, 1: t_absd}
    ctx.carry.pop("wq1_pend", None)
    ar.free("wacc")
    t_st = r.dma(SC["cc2_in"].ap(), mc[:, 0:4], wait=(t_p1,))
    cc = r.sems["cc"]
    t_cc = r.op("gpsimd", "collective_compute", "AllReduce", ALU.add,
                replica_groups=[[0, 1], [2, 3], [4, 5], [6, 7]],
                ins=[SC["cc2_in"].ap().opt()], outs=[SC["cc2_out"].ap().opt()],
                wait=(t_st,), inc=cc)
    t_rd = r.dma(mc[:, 0:4], SC["cc2_out"].ap(), wait=(t_cc,))
    t_si = {}
    tprev = t_rd
    for i, (hname, hr, hc, fname, fr, fc, osc, mi) in enumerate(specs):
        t_m = r.v("tensor_scalar", mc[:, 4 + mi:5 + mi], mc[:, mi:mi + 1],
                  1.0 / (fr * fc), 1e-5, ALU.mult, ALU.max, wait=(tprev,),
                  inc=True)
        tprev = r.v("reciprocal", mc[:, 8 + mi:9 + mi], mc[:, 4 + mi:5 + mi],
                    wait=(t_m,), inc=True)
        t_si[mi] = tprev
        r.dma(SC["mw"].ap()[:, mi:mi + 1], mc[:, 4 + mi:5 + mi],
              wait=(t_m,), allow_slow_non_contiguous=True)
    ar.alloc("wq16", NC_CH * 512)
    ar.alloc("wqb", (NC_CH * 1024 + 1) // 2)
    wq_pass2(ctx, IN["w_gate"], 4 * DM, DM, "wgT", mc[:, 8:9], t_si[0])
    wq_pass2(ctx, IN["w_up"], 4 * DM, DM, "wuT", mc[:, 9:10], t_si[1])
    wq_pass2(ctx, IN["w_down"], DM, 4 * DM, "wdT", mc[:, 10:11], t_si[2])
    ctx.carry.pop("wq_pend", None)
    for n in ("wstg0", "wstg1", "mc3", "wq16", "wqb", "wabs0", "wabs1"):
        ar.free(n)
    r.barrier()


def stage_s10b_ffn(ctx):
    r, ar, IN, SC = ctx.r, ctx.ar, ctx.IN, ctx.SC
    ident = ar.bf16("ident", [128, 128])
    OWNW = OWN
    x3, t_x3 = ctx.carry.pop("x3_l")
    ar.alloc("hifm", (NC_CH * FW + 1) // 2)
    hi = ar.bf16("hifm", [128, NC_CH, FW])
    ar.alloc("lofm", (NC_CH * FW + 1) // 2)
    lo = ar.bf16("lofm", [128, NC_CH, FW])
    t_hi = r.v("tensor_copy", hi, x3, wait=(t_x3,), inc=True)
    t_lo = None
    for c in range(NC_CH):
        t_lo = r.v("tensor_tensor", lo[:, c, :], x3[:, c, :], hi[:, c, :],
                   ALU.subtract, wait=(t_hi,), inc=True)
    t_lo = r.tok("vector")
    ar.free("x3_l")
    r.barrier()
    ar.alloc("hitm", (NT * DM + 1) // 2)
    hitm = ar.bf16("hitm", [128, NT, DM])
    tzh = r.v("memset", hitm[:, NT - 1, :], 0.0, inc=True)
    t1 = _fm_to_tm(ctx, hitm, hi, ident, t_hi, extra_wait=(tzh,))
    ar.alloc("lotm", (NT * DM + 1) // 2)
    lotm = ar.bf16("lotm", [128, NT, DM])
    tzl = r.v("memset", lotm[:, NT - 1, :], 0.0, inc=True)
    t2 = _fm_to_tm(ctx, lotm, lo, ident, t_lo, extra_wait=(tzl,))
    ar.free("hifm"); ar.free("lofm")
    r.barrier()
    ar.alloc("xtall", NT * DM)
    xta = ar.f32("xtall", [128, NT, DM])
    ta = r.v("tensor_tensor", xta, hitm, lotm, ALU.add, wait=(t1, t2), inc=True)
    td = r.dma(SC["x3tm"].ap().rearrange("(t p) c -> p t c", p=128), xta,
               wait=(ta,), eng="gpsimd")
    ar.free("hitm"); ar.free("lotm")
    ar.alloc("qcols", 4 * NT)
    qc = ar.f32("qcols", [128, 4 * NT])
    a0, a1 = qc[:, 0:NT], qc[:, NT:2 * NT]
    asc, axv = qc[:, 2 * NT:3 * NT], qc[:, 3 * NT:4 * NT]
    m1 = r.v("tensor_reduce", a0, xta, AX.X, ALU.max,
             apply_absolute_value=True, wait=(ta,), inc=True)
    m2 = r.v("tensor_scalar", a1, a0, 1e-5, None, ALU.max, wait=(m1,), inc=True)
    m3 = r.v("reciprocal", asc, a1, wait=(m2,), inc=True)
    m4 = r.v("tensor_scalar", asc, asc, 127.0, None, ALU.mult, wait=(m3,),
             inc=True)
    m5 = r.v("tensor_scalar", axv, a1, 1.0 / 127.0, None, ALU.mult,
             wait=(m2,), inc=True)
    r.barrier()
    ar.alloc("x16", (NT * DM + 1) // 2)
    x16 = ar.f16("x16", [128, NT, DM])
    ar.alloc("xqtm", (NT * DM + 1) // 2)
    xqa = ar.bf16("xqtm", [128, NT, DM])
    tq = None
    for t in range(NT):
        tq = r.v("tensor_scalar", x16[:, t, :], xta[:, t, :], asc[:, t:t + 1],
                 1536.0, ALU.mult, ALU.add, wait=(m4, td),
                 inc=(True if t == NT - 1 else None))
    t_xq = r.v("tensor_scalar", xqa, x16, 1536.0, None, ALU.subtract,
               wait=(tq,), inc=True)
    ar.free("xtall"); ar.free("x16")
    r.barrier()
    ar.alloc("xqfm", (NC_CH * OWNW + 1) // 2)
    xqf = ar.bf16("xqfm", [128, NC_CH, OWNW])
    ar.alloc("wgT_l", NC_CH * 2048)
    wg = ar.bf16("wgT_l", [128, NC_CH, 4 * DM])
    t_wg = r.dma(wg, SC["wgT"].ap().rearrange("(c p) n -> p c n", p=128),
                 eng="scalar")
    ar.alloc("wuT_l", NC_CH * 2048)
    wu = ar.bf16("wuT_l", [128, NC_CH, 4 * DM])
    t_wu = r.dma(wu, SC["wuT"].ap().rearrange("(c p) n -> p c n", p=128),
                 eng="scalar")
    ar.alloc("mw_l", 8)
    mw = ar.f32("mw_l", [128, 8])
    t_mw = r.dma(mw[:, 0:3], SC["mw"].ap()[:, 0:3],
                 allow_slow_non_contiguous=True)
    prevc = None
    for t in range(NT):
        nr = 128 if t < NT - 1 else OWN - 128 * t
        bank = ctx.tbank()
        tok = None
        for cc in range(NC_CH):
            w0 = ((bank.free_tok,) if bank.free_tok else ()) + ((t_xq,) if cc == 0 else ())
            tok = r.te("transpose", bank.ap[:, 128 * cc:128 * cc + 128],
                       xqa[:, t, 128 * cc:128 * cc + 128], ident,
                       wait=w0 if cc == 0 else (),
                       inc=(True if cc == NC_CH - 1 else None))
        prevc = r.v("tensor_copy",
                    xqf[:, :, 128 * t:128 * t + nr],
                    bank.ap.rearrange("p (c q) -> p c q", c=NC_CH)[:, :, 0:nr],
                    wait=(tok,), inc=True)
        bank.free_tok = prevc
    t_xqf = r.tok("vector")
    ar.free("xqtm")
    r.barrier()
    ar.alloc("scg", 2 * NT)
    scg = ar.f32("scg", [128, 2 * NT])
    t_s1 = r.v("tensor_scalar", scg[:, 0:NT], axv, mw[:, 0:1], None, ALU.mult,
               wait=(t_mw, m5), inc=True)
    t_s2 = r.v("tensor_scalar", scg[:, NT:2 * NT], axv, mw[:, 1:2], None,
               ALU.mult, wait=(t_s1,), inc=True)
    ar.alloc("ht0", 2048)
    ar.alloc("ht1", 2048)
    hts = [ar.bf16("ht0", [128, 4 * DM]), ar.bf16("ht1", [128, 4 * DM])]
    ar.alloc("sg0", 256)
    ar.alloc("sg1", 256)
    sgs = [ar.bf16("sg0", [128, 512]), ar.bf16("sg1", [128, 512])]
    ar.alloc("h16", 2048)
    h16 = ar.f16("h16", [128, 4 * DM])
    ar.alloc("hqt", 2048)
    hqt = ar.bf16("hqt", [128, 4 * DM])
    ar.alloc("hqc", 8)
    hqc = ar.f32("hqc", [128, 8])
    ar.alloc("ahinv", NT)
    ahv = ar.f32("ahinv", [128, NT])
    hq_r = SC["hq"].ap().rearrange("(t p) n -> p t n", p=128)
    prev_dma = None
    prev_q6 = None
    q5_done = {0: None, 1: None}
    sg_last = {0: None, 1: None}
    qa_last = None
    for t in range(NT):
        par = t % 2
        ht = hts[par]
        ntok = 128 if t < NT - 1 else OWN - 128 * t
        for hs in range(8):
            spar = hs % 2
            sgb = sgs[spar]
            bg = ctx.fbank()
            bu = ctx.fbank()
            tg = tu = None
            for ic in range(NC_CH):
                w0 = (((bg.free_tok,) if bg.free_tok else ()) +
                      ((t_xqf, t_wg) if ic == 0 else ()))
                tg = r.te("matmul", bg.ap[0:ntok, 0:512],
                          xqf[:, ic, 128 * t:128 * t + ntok],
                          wg[:, ic, 512 * hs:512 * hs + 512],
                          start=(ic == 0), stop=(ic == NC_CH - 1),
                          wait=w0, inc=(True if ic == NC_CH - 1 else None))
            for ic in range(NC_CH):
                w0 = (((bu.free_tok,) if bu.free_tok else ()) +
                      ((t_xqf, t_wu) if ic == 0 else ()))
                tu = r.te("matmul", bu.ap[0:ntok, 0:512],
                          xqf[:, ic, 128 * t:128 * t + ntok],
                          wu[:, ic, 512 * hs:512 * hs + 512],
                          start=(ic == 0), stop=(ic == NC_CH - 1),
                          wait=w0, inc=(True if ic == NC_CH - 1 else None))
            w1 = (tg, t_s1) + ((sg_last[spar],) if sg_last[spar] else ())
            e1 = r.s("activation", sgb, bg.ap[0:128, 0:512], AF.Silu,
                     scale=scg[:, t:t + 1], wait=w1, inc=True)
            bg.free_tok = e1
            w2 = (tu, e1, t_s2) + ((q5_done[par],) if q5_done[par] else ())
            e2 = r.v("scalar_tensor_tensor", ht[:, 512 * hs:512 * hs + 512],
                     bu.ap[0:128, 0:512], scg[:, NT + t:NT + t + 1], sgb,
                     ALU.mult, ALU.mult, wait=w2, inc=True)
            bu.free_tok = e2
            sg_last[spar] = e2
        t_h = r.tok("vector")
        cs = 4 * par
        wq1 = (t_h,) + ((qa_last,) if qa_last else ())
        q1 = r.v("tensor_reduce", hqc[:, cs:cs + 1], ht, AX.X, ALU.max,
                 apply_absolute_value=True, wait=wq1, inc=True)
        q2 = r.v("tensor_scalar", hqc[:, cs + 1:cs + 2], hqc[:, cs:cs + 1],
                 1e-5, None, ALU.max, wait=(q1,), inc=True)
        q3 = r.v("reciprocal", hqc[:, cs + 2:cs + 3], hqc[:, cs + 1:cs + 2],
                 wait=(q2,), inc=True)
        q4 = r.v("tensor_scalar", hqc[:, cs + 3:cs + 4], hqc[:, cs + 2:cs + 3],
                 127.0, None, ALU.mult, wait=(q3,), inc=True)
        qa_last = r.v("tensor_scalar", ahv[:, t:t + 1], hqc[:, cs + 1:cs + 2],
                      1.0 / 127.0, None, ALU.mult, wait=(q2,), inc=True)
        w5 = (q4,) + ((prev_q6,) if prev_q6 else ())
        q5 = r.v("tensor_scalar", h16, ht, hqc[:, cs + 3:cs + 4], 1536.0,
                 ALU.mult, ALU.add, wait=w5, inc=True)
        q5_done[par] = q5
        w6 = (q5,) + ((prev_dma,) if prev_dma else ())
        prev_q6 = r.v("tensor_scalar", hqt, h16, 1536.0, None, ALU.subtract,
                      wait=w6, inc=True)
        prev_dma = r.dma(hq_r[:, t, :], hqt, wait=(prev_q6,), eng="gpsimd")
    t_ah = r.tok("vector")
    r.barrier()
    for n in ("wgT_l", "wuT_l", "scg", "ht0", "ht1", "sg0", "sg1", "h16",
              "hqt", "hqc", "xqfm", "qcols"):
        ar.free(n)
    r.barrier()
    ar.alloc("wdT_l", 16 * 1024)
    wd = ar.bf16("wdT_l", [128, 32, DM])
    t_wd = r.dma(wd, SC["wdT"].ap().rearrange("(c p) n -> p c n", p=128),
                 eng="scalar")
    ar.alloc("scd", NT)
    scd = ar.f32("scd", [128, NT])
    t_sd = r.v("tensor_scalar", scd, ahv, mw[:, 2:3], None, ALU.mult,
               wait=(t_ah, t_mw), inc=True)
    ar.free("ahinv")
    ar.alloc("hq_t0", 2048)
    ar.alloc("hq_t1", 2048)
    hqLs = [ar.bf16("hq_t0", [128, 4 * DM]), ar.bf16("hq_t1", [128, 4 * DM])]
    ar.alloc("hqT0", 16 * 128)
    ar.alloc("hqT1", 16 * 128)
    hqTs = [ar.bf16("hqT0", [128, 32, 128]), ar.bf16("hqT1", [128, 32, 128])]
    ar.alloc("x3t0", DM)
    ar.alloc("x3t1", DM)
    ar.alloc("x3t2", DM)
    x3ts = [ar.f32("x3t0", [128, DM]), ar.f32("x3t1", [128, DM]),
            ar.f32("x3t2", [128, DM])]
    ar.alloc("pst0", DM)
    ar.alloc("pst1", DM)
    ar.alloc("pst2", DM)
    psts = [ar.f32("pst0", [128, DM]), ar.f32("pst1", [128, DM]),
            ar.f32("pst2", [128, DM])]
    ar.alloc("sqt", DM)
    sqt = ar.f32("sqt", [128, DM])
    ar.alloc("lncol", 24)
    lnc_all = ar.f32("lncol", [128, 24])
    x3tm_r = SC["x3tm"].ap().rearrange("(t p) c -> p t c", p=128)
    y_r = ctx.y.ap().rearrange("(t p) c -> p t c", p=128)
    prev_y = {0: None, 1: None, 2: None}
    hql_free = {0: (t_sd,), 1: (t_sd,)}
    hqT_free = {0: None, 1: None}
    x3t_free = {0: (), 1: (), 2: ()}
    ly_prev = {0: None, 1: None, 2: None}
    prev_l1 = None
    for t in range(NT):
        par = t % 2
        p3 = t % 3
        hqL, hqT, x3t, pst = hqLs[par], hqTs[par], x3ts[p3], psts[p3]
        lnc = lnc_all[:, 8 * p3:8 * p3 + 8]
        ntok = 128 if t < NT - 1 else OWN - 128 * t
        t_h = r.dma(hqL, hq_r[:, t, :], wait=hql_free[par])
        t_x = r.dma(x3t, x3tm_r[:, t, :], wait=x3t_free[p3])
        t_tr = None
        for half in range(4):
            bank = ctx.tbank()
            tok = None
            for sub in range(8):
                hc = 8 * half + sub
                w0 = ((bank.free_tok,) if bank.free_tok else ()) + ((t_h,) if sub == 0 else ())
                tok = r.te("transpose", bank.ap[:, 128 * sub:128 * sub + 128],
                           hqL[:, 128 * hc:128 * hc + 128], ident,
                           wait=w0 if sub == 0 else (),
                           inc=(True if sub == 7 else None))
            ev = r.v("tensor_copy", hqT[:, 8 * half:8 * half + 8, :],
                     bank.ap.rearrange("p (c q) -> p c q", c=NC_CH),
                     wait=(tok,), inc=True)
            bank.free_tok = ev
        t_tr = r.tok("vector")
        hql_free[par] = (t_tr,)
        evs = []
        last_mm = None
        for fs in range(2):
            bank = ctx.fbank()
            tok = None
            for hc in range(32):
                w0 = (((bank.free_tok,) if bank.free_tok else ()) +
                      ((t_tr, t_wd) if hc == 0 else ()))
                tok = r.te("matmul", bank.ap[0:ntok, 0:512],
                           hqT[:, hc, 0:ntok],
                           wd[:, hc, 512 * fs:512 * fs + 512],
                           start=(hc == 0), stop=(hc == 31),
                           wait=w0, inc=(True if hc == 31 else None))
            last_mm = tok
            wev = (tok, t_x, t_sd) + ((prev_y[p3],) if prev_y[p3] else ())
            ev = r.v("scalar_tensor_tensor", pst[:, 512 * fs:512 * fs + 512],
                     bank.ap[0:128, 0:512], scd[:, t:t + 1],
                     x3t[:, 512 * fs:512 * fs + 512], ALU.mult, ALU.add,
                     wait=wev, inc=True)
            evs.append(ev)
            bank.free_tok = ev
        x3t_free[p3] = (evs[-1],)
        hqT_free[par] = (last_mm,)
        wl1 = tuple(evs) + ((prev_l1,) if prev_l1 else ()) + \
            ((ly_prev[p3],) if ly_prev[p3] else ())
        l1 = r.s("activation", sqt, pst, AF.Square, accum_out=lnc[:, 0:1],
                 wait=wl1, inc=True)
        prev_l1 = l1
        wl2 = tuple(evs) + ((ly_prev[p3],) if ly_prev[p3] else ())
        l2 = r.v("tensor_reduce", lnc[:, 1:2], pst, AX.X, ALU.add,
                 wait=wl2, inc=True)
        l3 = r.v("tensor_scalar", lnc[:, 2:3], lnc[:, 1:2], 1.0 / DM, None,
                 ALU.mult, wait=(l2,), inc=True)
        l4 = r.v("tensor_scalar", lnc[:, 3:4], lnc[:, 2:3], lnc[:, 2:3], EPS,
                 ALU.mult, ALU.subtract, wait=(l3,), inc=True)
        l5 = r.v("scalar_tensor_tensor", lnc[:, 4:5], lnc[:, 0:1], 1.0 / DM,
                 lnc[:, 3:4], ALU.mult, ALU.subtract, wait=(l1, l4), inc=True)
        l6a = r.v("reciprocal", lnc[:, 5:6], lnc[:, 4:5], wait=(l5,), inc=True)
        l6 = r.s("activation", lnc[:, 6:7], lnc[:, 5:6], AF.Sqrt,
                 wait=(l6a,), inc=True)
        l7 = r.v("tensor_scalar", lnc[:, 7:8], lnc[:, 2:3], lnc[:, 6:7], -1.0,
                 ALU.mult, ALU.mult, wait=(l6,), inc=True)
        ly = r.s("activation", pst, pst, AF.Identity, scale=lnc[:, 6:7],
                 bias=lnc[:, 7:8], wait=(l7,), inc=True)
        ly_prev[p3] = ly
        prev_y[p3] = r.dma(y_r[:, t, :], pst, wait=(ly,), eng="gpsimd")
    r.barrier()



_PROG_CACHE = {}


def _get_prog(stop_stage, taps):
    key = (stop_stage, taps)
    if key not in _PROG_CACHE:
        _PROG_CACHE[key] = build_program(stop_stage, taps)
    return _PROG_CACHE[key]


_PERM1 = np.zeros(1024, np.int64)
for _c in range(8):
    for _j in range(64):
        _PERM1[128 * _c + _j] = 128 * _c + 2 * _j
        _PERM1[128 * _c + 64 + _j] = 128 * _c + 2 * _j + 1
_PERM2 = np.concatenate([_PERM1, _PERM1 + 1024])


def _colvec(v):
    v = np.asarray(v, np.float32).reshape(-1)
    n = v.shape[0] // 128
    return np.ascontiguousarray(v.reshape(n, 128).T)


def _colmat(v):
    v = np.asarray(v, np.float32)
    v = v.reshape(v.shape[0] // 128, 128, -1)
    return np.ascontiguousarray(v.transpose(1, 0, 2))


def _atr_mask():
    import ml_dtypes
    ind = np.zeros((128, 96), np.float32)
    for j in range(96):
        ind[(j // 48) * 3 + (j % 3), j] = 1.0
    mm = np.zeros((128, 4 * 96), np.float32)
    mm[:, 0:96] = ind * float(2 ** 30)
    mm[:, 96:192] = ind
    mm[0, 192:288] = -float(2 ** 30)
    mm[0, 288:384] = 1.0
    return np.ascontiguousarray(mm.astype(ml_dtypes.bfloat16))


_ATR_MASK = None


def make_inmaps(inputs):
    import ml_dtypes
    global _ATR_MASK
    if _ATR_MASK is None:
        _ATR_MASK = _atr_mask()
    x = np.asarray(inputs["L0_attn_x"])
    in_maps = []
    for c in range(8):
        s, h = c // 2, c % 2
        start = OWN * h
        lo, hi = start - HALO, start + OWN + HALO
        shard = np.zeros((FW, DM), np.float32)
        a, b = max(lo, 0), min(hi, 4096)
        shard[a - lo:a - lo + (b - a)] = x[s, a:b]
        n_real = 4096 - start if h == 1 else OWN
        n_real = min(n_real, OWN)
        mrow = np.zeros((FW,), np.float32)
        mrow[HALO:HALO + n_real] = 1.0
        mask_full = np.broadcast_to(mrow, (128, FW)).astype(ml_dtypes.bfloat16)
        mact = np.zeros((NT * 128,), np.float32)
        mact[:n_real] = 1.0
        mask_tm = mact.reshape(NT, 128).T.copy()
        m = {
            "xshard": shard,
            "mask_full_bf": np.ascontiguousarray(mask_full),
            "mask_tm_bf": mask_tm.astype(ml_dtypes.bfloat16),
            "mask_tm_f32": mask_tm,
            "ascale": np.full((128, 1), np.asarray(inputs["L0_ffn_act_scale"]),
                              np.float32),
            "mska": _ATR_MASK,
            "cpe1_w": _colmat(np.asarray(inputs["L0_attn_cpe1_w"]).reshape(DM, 3)),
            "cpe1_b": _colvec(inputs["L0_attn_cpe1_b"]),
            "norm1_g": _colvec(inputs["L0_attn_norm1_g"]),
            "norm1_b": _colvec(inputs["L0_attn_norm1_b"]),
            "in_w": np.asarray(inputs["L0_attn_in_w"]),
            "in_b": _colvec(inputs["L0_attn_in_b"]),
            "actp_w": np.asarray(inputs["L0_attn_actp_w"]),
            "actp_b": _colvec(inputs["L0_attn_actp_b"]),
            "dwc_w": _colmat(np.asarray(inputs["L0_attn_dwc_w"]).reshape(DM, 3)),
            "dwc_b": _colvec(inputs["L0_attn_dwc_b"]),
            "qk_w": np.asarray(inputs["L0_attn_qk_w"]),
            "qk_b": _colvec(np.asarray(inputs["L0_attn_qk_b"])[_PERM2]),
            "lepe_w": _colmat(np.asarray(inputs["L0_attn_lepe_w"]).reshape(DM, 3)),
            "lepe_b": _colvec(inputs["L0_attn_lepe_b"]),
            "out_w": np.asarray(inputs["L0_attn_out_w"]),
            "out_b": _colvec(inputs["L0_attn_out_b"]),
            "cpe2_w": _colmat(np.asarray(inputs["L0_attn_cpe2_w"]).reshape(DM, 3)),
            "cpe2_b": _colvec(inputs["L0_attn_cpe2_b"]),
            "norm2_g": _colvec(inputs["L0_attn_norm2_g"]),
            "norm2_b": _colvec(inputs["L0_attn_norm2_b"]),
            "fc1_w": np.asarray(inputs["L0_attn_fc1_w"]),
            "fc1_b": _colvec(inputs["L0_attn_fc1_b"]),
            "fc2_w": np.asarray(inputs["L0_attn_fc2_w"]),
            "fc2_b": _colvec(inputs["L0_attn_fc2_b"]),
            "tern": np.asarray(inputs["L0_ffn_ternary_weight"]),
            "ffn_bias": _colvec(inputs["L0_ffn_bias"]),
            "Wq": np.asarray(inputs["L1_attn_Wq"]), "bq": _colvec(inputs["L1_attn_bq"]),
            "Wk": np.asarray(inputs["L1_attn_Wk"]), "bk": _colvec(inputs["L1_attn_bk"]),
            "Wv": np.asarray(inputs["L1_attn_Wv"]), "bv": _colvec(inputs["L1_attn_bv"]),
            "Wo": np.asarray(inputs["L1_attn_Wo"]), "bo": _colvec(inputs["L1_attn_bo"]),
            "w_gate": np.asarray(inputs["L1_ffn_w_gate"]),
            "w_up": np.asarray(inputs["L1_ffn_w_up"]),
            "w_down": np.asarray(inputs["L1_ffn_w_down"]),
            "wg_h": np.ascontiguousarray(
                np.asarray(inputs["L1_ffn_w_gate"])[2048 * h:2048 * (h + 1)]),
            "wu_h": np.ascontiguousarray(
                np.asarray(inputs["L1_ffn_w_up"])[2048 * h:2048 * (h + 1)]),
            "wd_h": np.ascontiguousarray(
                np.asarray(inputs["L1_ffn_w_down"])[512 * h:512 * (h + 1)]),
        }
        in_maps.append(m)
    return in_maps


def kernel(**inputs):
    stop = os.environ.get("K_STOP_STAGE") or None
    taps = tuple(t for t in os.environ.get("K_TAPS", "").split(",") if t)
    nc = _get_prog(stop, taps)
    in_maps = make_inmaps(inputs)
    res = run_bass_kernel_spmd(nc, in_maps, core_ids=list(range(8)))
    out = np.zeros((4, 4096, DM), np.float32)
    for c in range(8):
        s, h = c // 2, c % 2
        start = OWN * h
        n_real = min(4096 - start, OWN)
        y = res.results[c]["y"]
        out[s, start:start + n_real] = y[0:n_real]
    if taps or stop:
        kernel.last_results = res
    return out
